# version 35
# baseline (speedup 1.0000x reference)
import numpy as np
import ml_dtypes
import concourse.bass as bass
import concourse.mybir as mybir
from concourse.bass_utils import run_bass_kernel_spmd
from contextlib import ExitStack

F32 = mybir.dt.float32
F32R = mybir.dt.float32r
BF16 = mybir.dt.bfloat16
AF = mybir.ActivationFunctionType
OP = mybir.AluOpType
AX = mybir.AxisListType

T = 2048
DM = 1024
NC8 = 8
NT = 16
PARTS = 4
DBG_WANT = ()
DBG_RAW = False
B_PHASE = 3

ENGINES = ("pe", "act", "ve", "pool", "sync")


def later(*miles):
    best = None
    for m in miles:
        if m is None:
            continue
        if best is None or m[1] > best[1]:
            assert best is None or m[0] == best[0]
            best = m
    return best


class KB:
    def __init__(self, nc):
        self.nc = nc
        self.q = {e: [] for e in ENGINES}
        self.cnt = {e: 0 for e in ENGINES}
        self.cnt["sd"] = 0
        self.sems = {}
        self.lastw = {}

    N_DSEM = 8

    def make_sems(self, stack):
        for s in ("pe", "act", "ve", "pool", "sync", "sd"):
            self.sems[s] = stack.enter_context(self.nc.semaphore("sem_" + s))
        for i in range(self.N_DSEM):
            self.sems[f"d{i}"] = stack.enter_context(self.nc.semaphore(f"sem_d{i}"))
        self.dma_i = 0
        self.dcnt = [0] * self.N_DSEM

    def wait(self, e, *miles):
        for m in miles:
            if m is None:
                continue
            s, v = m
            if v <= 0 or s == e:
                continue
            key = (e, s)
            if self.lastw.get(key, 0) >= v:
                continue
            self.lastw[key] = v
            sem = self.sems[s]
            self.q[e].append(lambda eng, sem=sem, v=v: eng.wait_ge(sem, v))

    def run(self, e, f, inc=True):
        if inc:
            sem = self.sems[e]
            self.cnt[e] += 1

            def thunk(eng, f=f, sem=sem):
                last = f(eng)
                assert last is not None
                last.then_inc(sem, 1)

            self.q[e].append(thunk)
            return (e, self.cnt[e])
        self.q[e].append(lambda eng, f=f: f(eng))
        return None

    def dma(self, out, in_, **kw):
        si = self.dma_i % self.N_DSEM
        self.dma_i += 1
        sem = self.sems[f"d{si}"]
        prev = self.dcnt[si]
        if prev > 0:
            self.q["sync"].append(lambda eng, sem=sem, v=prev: eng.wait_ge(sem, v))
        self.dcnt[si] = prev + 16
        self.q["sync"].append(
            lambda eng, o=out, i=in_, kw=kw, sem=sem: eng.dma_start(out=o, in_=i, **kw).then_inc(sem, 16))
        return (f"d{si}", self.dcnt[si])

    def mile(self, e):
        return (e, self.cnt[e])

    def self_sync(self, e):
        sem = self.sems[e]
        v = self.cnt[e]
        if v > 0:
            self.q[e].append(lambda eng, sem=sem, v=v: eng.wait_ge(sem, v))

    def fence(self):
        prods = [(p, self.cnt[p]) for p in ("pe", "act", "ve", "pool")]
        prods += [(f"d{i}", self.dcnt[i]) for i in range(self.N_DSEM)]
        for e in ENGINES:
            self.wait(e, *prods)


def bview(ap, inner):
    return ap.unsqueeze(2).broadcast_to([ap.shape[0], ap.shape[1], inner])


def pslots(ps, pitch, n, width, off=0):
    return bass.AP(tensor=ps.tensor, offset=ps.offset + off,
                   ap=[list(ps.ap[0]), [pitch, n], [1, width]])


def build_program(parts=None):
    if parts is None:
        parts = PARTS
    nc = bass.Bass("TRN2", target_bir_lowering=False)

    d_x = nc.dram_tensor("x_fm", [DM, T], F32R, kind="ExternalInput").ap()
    d_out = nc.dram_tensor("x_out", [DM, T], F32, kind="ExternalOutput").ap()
    d_qkmb = nc.dram_tensor("wqkv_qk_mb", [16 * 128, DM], F32, kind="ExternalInput").ap()
    d_wqkv_v = nc.dram_tensor("wqkv_v", [DM, DM], F32, kind="ExternalInput").ap()
    d_proj = nc.dram_tensor("projm", [128, 64], F32, kind="ExternalInput").ap()
    d_woutmb = nc.dram_tensor("wout_mb", [8 * 128, DM], F32, kind="ExternalInput").ap()
    d_wqmb = nc.dram_tensor("wq_mb", [8 * 128, DM], F32, kind="ExternalInput").ap()
    d_wkmb = nc.dram_tensor("wk_mb", [8 * 128, DM], F32, kind="ExternalInput").ap()
    d_wv = nc.dram_tensor("Wv", [DM, DM], F32, kind="ExternalInput").ap()
    d_womb = nc.dram_tensor("wo_mb", [8 * 128, DM], F32, kind="ExternalInput").ap()
    d_wcat = nc.dram_tensor("Wcat", [DM, 3 * DM], F32, kind="ExternalInput").ap()
    d_ocatmb = nc.dram_tensor("ocat_mb", [8 * 128, DM], F32, kind="ExternalInput").ap()
    d_sim = nc.dram_tensor("simc", [128, NC8 * 16], F32, kind="ExternalInput").ap()
    d_gates = nc.dram_tensor("gatesr", [1, 16], F32, kind="ExternalInput").ap()
    d_cb = nc.dram_tensor("biascols", [128, 4 * NC8], F32, kind="ExternalInput").ap()
    d_bv = nc.dram_tensor("bvrow", [1, DM], F32, kind="ExternalInput").ap()
    d_ident = nc.dram_tensor("identc", [128, 128], F32, kind="ExternalInput").ap()
    d_maska = nc.dram_tensor("maska", [128, 3 * 384], F32, kind="ExternalInput").ap()
    d_tri = nc.dram_tensor("tric", [128, 128], F32, kind="ExternalInput").ap()
    d_sel = nc.dram_tensor("selc", [16, NC8 * 128], F32, kind="ExternalInput").ap()
    d_onesr = nc.dram_tensor("onesr", [128, 2], F32R, kind="ExternalInput").ap()
    d_ones1 = nc.dram_tensor("ones1", [1, 128], F32, kind="ExternalInput").ap()
    d_ones1r = nc.dram_tensor("ones1r", [1, 128], F32R, kind="ExternalInput").ap()
    d_idf1 = nc.dram_tensor("idf1", [1, 1], F32, kind="ExternalInput").ap()

    stack = ExitStack()
    k = KB(nc)

    with stack:
        k.make_sems(stack)

        _names = {}

        def _uniq(name):
            n = _names.get(name, 0)
            _names[name] = n + 1
            return name if n == 0 else f"{name}_u{n}"

        def sb(name, shape, dt, st=stack):
            return st.enter_context(nc.sbuf_tensor(_uniq(name), shape, dt)).ap()

        DBGSC = [None]
        if DBG_WANT:
            DBGSC[0] = sb("dbg_scratch", [128, 2048], F32)

        def dbg(name, ap):
            if name not in DBG_WANT:
                return
            P, FR = ap.shape[0], ap.shape[1]
            dd = nc.dram_tensor("dbg_" + name, [P, FR], F32, kind="ExternalOutput").ap()
            k.fence()
            for lo in range(0, FR, 2048):
                w = min(2048, FR - lo)
                mcc = k.run("ve", lambda v, lo=lo, w=w, ap=ap, P=P: v.tensor_copy(
                    DBGSC[0][0:P, 0:w], ap[:, lo:lo + w]))
                k.wait("sync", mcc)
                md = k.dma(dd[:, lo:lo + w], DBGSC[0][0:P, 0:w])
                k.wait("ve", md)
            for e in ENGINES:
                for i in range(k.N_DSEM):
                    k.wait(e, (f"d{i}", k.dcnt[i]))

        X = sb("X", [128, NC8 * T], F32R)
        XF = X.bitcast(F32)
        IDENT = sb("IDENT", [128, 128], BF16)
        IDENTF = sb("IDENTF", [128, 128], F32)
        TRI = sb("TRI", [128, 128], BF16)
        TRIF = sb("TRIF", [128, 128], F32)
        ONESR = sb("ONESR", [128, 2], F32R)
        ONES1 = sb("ONES1", [1, 128], F32)
        ONES1R = sb("ONES1R", [1, 128], F32R)
        IDF1 = sb("IDF1", [1, 1], F32)
        CB = sb("CB", [128, 4 * NC8], F32)
        SIMC = sb("SIMC", [128, NC8 * 16], F32)
        PROJB = sb("PROJB", [128, 64], BF16)
        SGB = sb("SGB", [128, 16], F32)
        SNB = sb("SNB", [128, 16], F32)
        BVB = sb("BVB", [128, DM], F32)
        GNRC = sb("GNRC", [128, 16], F32)
        EPS1 = sb("EPS1", [1, 1], F32)
        EPS8 = sb("EPS8", [128, 1], F32)

        m_in = [k.dma(X[:, kc * T:(kc + 1) * T], d_x[kc * 128:(kc + 1) * 128, :]) for kc in range(NC8)]
        m_c = [k.dma(IDENTF, d_ident), k.dma(TRIF, d_tri),
               k.dma(ONESR, d_onesr), k.dma(ONES1, d_ones1),
               k.dma(ONES1R, d_ones1r), k.dma(IDF1, d_idf1), k.dma(CB, d_cb),
               k.dma(SIMC, d_sim)]
        with ExitStack() as ps_:
            pb = lambda n, s, d=F32: ps_.enter_context(nc.psum_tensor(n, s, d)).ap()
            PP = pb("prep_ps", [128, 512])
            PROJ32 = ps_.enter_context(nc.sbuf_tensor("PROJ32", [128, 64], F32)).ap()
            BVROW = ps_.enter_context(nc.sbuf_tensor("BVROW", [1, DM], F32)).ap()
            GROW = ps_.enter_context(nc.sbuf_tensor("GROW", [1, 16], F32)).ap()
            SSQ = ps_.enter_context(nc.sbuf_tensor("SSQ", [128, NC8 * 16], F32)).ap()
            m_pj = k.dma(PROJ32, d_proj)
            m_bv = k.dma(BVROW, d_bv)
            m_g = k.dma(GROW, d_gates)
            k.wait("ve", m_pj, m_c[0], m_c[1])
            k.run("ve", lambda v: v.tensor_copy(IDENT, IDENTF), inc=False)
            k.run("ve", lambda v: v.tensor_copy(TRI, TRIF), inc=False)
            k.run("ve", lambda v: v.tensor_copy(PROJB, PROJ32), inc=False)
            k.run("ve", lambda v: v.memset(EPS1, 1e-5), inc=False)
            k.run("ve", lambda v: v.memset(EPS8, 1e-8), inc=False)
            k.wait("act", m_g)
            m1 = k.run("act", lambda s: s.activation(GROW, GROW, AF.Sigmoid))
            k.wait("pe", m1, m_c[3])
            m2 = k.run("pe", lambda t: t.matmul(PP[:, 0:16], ONES1, GROW, start=True, stop=True,
                                                skip_group_check=True))
            k.wait("ve", m2)
            m3 = k.run("ve", lambda v: v.tensor_copy(SGB, PP[:, 0:16]))
            k.wait("pool", m_c[7])
            m4 = k.run("pool", lambda g: g.tensor_mul(SSQ, SIMC, SIMC))
            k.wait("pe", m3, m4, m_c[2])

            def f_sn(t):
                last = None
                for kc in range(NC8):
                    last = t.matmul(PP[0:1, 64:80], ONESR.bitcast(F32)[:, 0:1],
                                    SSQ[:, kc * 16:(kc + 1) * 16],
                                    start=(kc == 0), stop=(kc == NC8 - 1), skip_group_check=True)
                return last
            m5 = k.run("pe", f_sn)
            k.wait("act", m5)
            m6 = k.run("act", lambda s: s.activation(GROW, PP[0:1, 64:80], AF.Sqrt))
            k.wait("ve", m6)

            GROW2 = ps_.enter_context(nc.sbuf_tensor("GROW2", [1, 16], F32)).ap()

            k.run("ve", lambda v: v.tensor_scalar_max(GROW, GROW, 1e-12))
            k.self_sync("ve")
            m7 = k.run("ve", lambda v: v.reciprocal(GROW2, GROW))
            k.wait("pe", m7)
            m8 = k.run("pe", lambda t: t.matmul(PP[:, 32:48], ONES1, GROW2, start=True, stop=True,
                                                skip_group_check=True))
            k.wait("ve", m8)
            m9 = k.run("ve", lambda v: v.tensor_copy(SNB, PP[:, 32:48]))
            last_m = m9
            for hf in range(2):
                k.wait("pe", last_m, m_bv)
                ma = k.run("pe", lambda t, hf=hf: t.matmul(PP[:, 0:512], ONES1,
                                                           BVROW[:, hf * 512:(hf + 1) * 512],
                                                           start=True, stop=True, skip_group_check=True))
                k.wait("ve", ma)
                last_m = k.run("ve", lambda v, hf=hf: v.tensor_copy(BVB[:, hf * 512:(hf + 1) * 512],
                                                                    PP[:, 0:512]))
            k.wait("ve", *m_in, *m_c)
            k.run("ve", lambda v: v.engine_nop())
        k.fence()
        dbg("sgb", SGB)
        dbg("snb", SNB)
        dbg("c_ident", IDENT)
        dbg("c_tri", TRI)
        dbg("x0", X)

        def make_ln_scope(tag, ls, gating=False, nbuf=2):
            d = {}
            d["SQ"] = [sb(f"{tag}_sq{i}", [128, 512], F32R, ls) for i in range(nbuf)]
            d["TT"] = [sb(f"{tag}_tt0", [128, 512], F32, ls)]
            if gating:
                d["XSC"] = [sb(f"{tag}_xsc{i}", [128, 512], F32, ls) for i in range(2)]
            d["MR"] = sb(f"{tag}_mr", [1, 512], F32, ls)
            d["VRr"] = sb(f"{tag}_vr", [1, 512], F32, ls)
            d["RS"] = sb(f"{tag}_rs", [1, 512], F32, ls)
            d["NM"] = sb(f"{tag}_nm", [1, 512], F32, ls)
            if gating:
                d["G1"] = sb(f"{tag}_g1", [1, 512], F32, ls)
            stat = ls.enter_context(nc.psum_tensor(_uniq(f"{tag}_stat"), [128, 512], F32)).ap()
            stat2 = ls.enter_context(nc.psum_tensor(_uniq(f"{tag}_stat2"), [128, 512], F32)).ap()
            d["ps_s0"] = stat[0:1, :]
            d["ps_s1"] = stat2[0:1, :]
            d["ps_s0mm"] = stat[0:2, :]
            d["ps_s1mm"] = stat2[0:2, :]
            d["ps_ab"] = ls.enter_context(nc.psum_tensor(_uniq(f"{tag}_ab"), [128, 512], F32)).ap()
            d["ps_am"] = ls.enter_context(nc.psum_tensor(_uniq(f"{tag}_am"), [128, 512], F32)).ap()
            if gating:
                d["ps_gn"] = ls.enter_context(nc.psum_tensor(_uniq(f"{tag}_gn"), [128, 512], F32)).ap()[:, 0:4]
            d["dbg_rows"] = True
            d["sq_m"] = [None] * nbuf
            d["tt_m"] = [None]
            d["xsc_m"] = [None, None]
            d["row_m"] = None
            d["bc_rd"] = None
            d["stat_rd"] = None
            return d

        def ln_tb(d, tb, xn_at, gating_cb=None, gnrc=False):
            ts = tb * 512
            m_st = None
            NB = len(d["SQ"])
            for kc in range(NC8):
                b = kc % NB
                sq_eng = "pool" if kc % 2 == 0 else "act"
                k.wait(sq_eng, d["sq_m"][b])
                if sq_eng == "pool":
                    def f_sq(g, kc=kc, b=b, ts=ts):
                        return g.tensor_mul(d["SQ"][b], X[:, kc * T + ts: kc * T + ts + 512],
                                            X[:, kc * T + ts: kc * T + ts + 512])
                else:
                    def f_sq(g, kc=kc, b=b, ts=ts):
                        return g.activation(d["SQ"][b], X[:, kc * T + ts: kc * T + ts + 512],
                                            AF.Square)
                m_sq = k.run(sq_eng, f_sq)
                k.wait("pe", m_sq, d["stat_rd"] if kc == 0 else None)

                def f_stats(t, kc=kc, b=b, ts=ts):
                    t.matmul(d["ps_s0mm"], ONESR, X[:, kc * T + ts: kc * T + ts + 512],
                             start=(kc == 0), stop=(kc == NC8 - 1), skip_group_check=True)
                    return t.matmul(d["ps_s1mm"], ONESR, d["SQ"][b],
                                    start=(kc == 0), stop=(kc == NC8 - 1), skip_group_check=True)
                m_st = k.run("pe", f_stats)
                d["sq_m"][b] = m_st
            k.wait("ve", m_st)

            def f_rows(v):
                v.tensor_single_scalar(d["MR"], d["ps_s0"], 1.0 / DM, op=OP.mult)
                v.tensor_mul(d["VRr"], d["MR"], d["MR"])
                return v.scalar_tensor_tensor(d["VRr"], d["ps_s1"], 1.0 / DM, d["VRr"],
                                              op0=OP.mult, op1=OP.subtract)
            m_rows = k.run("ve", f_rows)
            d["stat_rd"] = m_rows
            k.wait("act", m_rows)

            def f_rstd(s):
                s.activation(d["VRr"], d["VRr"], AF.Ln, bias=EPS1)
                return s.activation(d["RS"], d["VRr"], AF.Exp, scale=-0.5)
            m_rstd = k.run("act", f_rstd)
            k.wait("ve", m_rstd)
            m_nm = k.run("ve", lambda v: v.scalar_tensor_tensor(
                d["NM"], d["MR"], -1.0, d["RS"], op0=OP.mult, op1=OP.mult))
            d["row_m"] = m_nm
            if tb == 0 and d.get("dbg_rows"):
                dbg("mr", d["MR"]); dbg("vr", d["VRr"]); dbg("rs", d["RS"]); dbg("nm", d["NM"])

            if gnrc:
                k.wait("act", m_rstd)
                mg1 = k.run("act", lambda s: s.activation(d["G1"], d["VRr"], AF.Exp))
                k.wait("ve", mg1)
                mg2 = k.run("ve", lambda v: v.tensor_single_scalar(d["G1"], d["G1"], 1e-5, op=OP.subtract))
                k.wait("act", mg2)
                mg3 = k.run("act", lambda s: s.activation(d["G1"], d["G1"], AF.Ln, scale=float(DM)))
                k.wait("ve", mg3)
                mg4 = k.run("ve", lambda v: v.tensor_sub(d["G1"], d["G1"], d["VRr"]))
                k.wait("act", mg4)
                mg5 = k.run("act", lambda s: s.activation(d["G1"], d["G1"], AF.Exp, scale=-0.5))
                tpm = None
                for tl in range(4):
                    k.wait("pe", mg5, tpm, d["row_m"] if tl == 0 else None)
                    mtp = k.run("pe", lambda t, tl=tl: t.transpose(
                        d["ps_gn"][:, tl:tl + 1], d["G1"][:, tl * 128:(tl + 1) * 128], IDF1))
                    tpm = mtp
                k.wait("ve", tpm)
                d["row_m"] = k.run("ve", lambda v, tb=tb: v.tensor_copy(
                    GNRC[:, tb * 4:(tb + 1) * 4], d["ps_gn"]))

            k.wait("pe", m_nm, d["bc_rd"])

            def f_bc(t):
                t.matmul(d["ps_ab"], ONES1, d["RS"], start=True, stop=True,
                         skip_group_check=True)
                return t.matmul(d["ps_am"], ONES1, d["NM"], start=True, stop=True,
                                skip_group_check=True)
            m_bc = k.run("pe", f_bc)

            last_ap = None
            NBT = len(d["TT"])
            for kc in range(NC8):
                b = kc % NBT
                k.wait("ve", m_bc, d["tt_m"][b])

                def f_a1(v, kc=kc, b=b, ts=ts):
                    return v.tensor_tensor(d["TT"][b], X[:, kc * T + ts: kc * T + ts + 512],
                                           d["ps_ab"], op=OP.mult)
                d["tt_m"][b] = k.run("ve", f_a1)
                if gating_cb is None:
                    def f_a2(v, kc=kc, b=b):
                        return v.tensor_tensor(xn_at(kc), d["TT"][b], d["ps_am"], op=OP.add)
                    d["tt_m"][b] = k.run("ve", f_a2)
                else:
                    bx = kc % 2
                    k.wait("ve", d["xsc_m"][bx])

                    def f_a2g(v, kc=kc, b=b, bx=bx):
                        return v.tensor_tensor(d["XSC"][bx], d["TT"][b], d["ps_am"], op=OP.add)
                    mx = k.run("ve", f_a2g)
                    d["tt_m"][b] = mx

                    def f_a3g(v, kc=kc, b=b, bx=bx):
                        return v.tensor_copy(xn_at(kc), d["XSC"][bx])
                    k.run("ve", f_a3g, inc=False)
                    d["xsc_m"][bx] = gating_cb(kc, d["XSC"][bx], mx)
                last_ap = d["tt_m"][b]
            d["bc_rd"] = later(*d["tt_m"])
            return last_ap

        def attn_a():
          with ExitStack() as a_scope:
              XN = sb("XN_a", [128, NC8 * T], BF16, a_scope)
              WV32 = [sb(f"a_wv{i}", [128, 512], F32, a_scope) for i in range(2)]
              WVB = sb("a_wvb", [128, NC8 * DM], BF16, a_scope)
              wvm = [None, None]
              cvm = []
              wv_dma = []

              def issue_vdma(kx):
                  kc, hf = kx // 2, kx % 2
                  k.wait("sync", wvm[hf])
                  wv_dma.append(k.dma(WV32[hf], d_wqkv_v[kc * 128:(kc + 1) * 128,
                                                         hf * 512:(hf + 1) * 512]))

              def emit_vcast(kx):
                  kc, hf = kx // 2, kx % 2
                  k.wait("act", wv_dma[kx], wvm[hf])
                  wvm[hf] = k.run("act", lambda s, kc=kc, hf=hf: s.activation(
                      WVB[:, kc * DM + hf * 512: kc * DM + (hf + 1) * 512], WV32[hf], AF.Copy))
                  cvm.append(wvm[hf])

              issue_vdma(0)
              issue_vdma(1)
              with ExitStack() as ls:
                  lnd = make_ln_scope("lna", ls, nbuf=4)
                  for tb in range(4):
                      ln_tb(lnd, tb, lambda kc, tb=tb: XN[:, kc * T + tb * 512: kc * T + (tb + 1) * 512])
                      for i_ in range(4):
                          kx = tb * 4 + i_
                          emit_vcast(kx)
                          if kx + 2 < 16:
                              issue_vdma(kx + 2)
                  k.fence()
              if "xnf" in DBG_WANT:
                  XNFDBG = sb("xnfdbg", [128, 2048], F32, a_scope)
                  dd = nc.dram_tensor("dbg_xnf", [128, 8 * 2048], F32, kind="ExternalOutput").ap()
                  k.fence()
                  for kc in range(NC8):
                      mcc = k.run("ve", lambda v, kc=kc: v.tensor_copy(XNFDBG, XN[:, kc * T:(kc + 1) * T]))
                      k.wait("sync", mcc)
                      md = k.dma(dd[:, kc * 2048:(kc + 1) * 2048] if False else dd.rearrange("p (c t) -> p c t", t=2048)[:, kc, :], XNFDBG)
                      for e in ENGINES:
                          k.wait(e, md)
              dbg("xn", XN)
              HW = 8
              def _half(H):
                  kt0 = max(0, 8 * H - 1)
                  kt1 = min(NT, 8 * H + 9)
                  nkt = kt1 - kt0
                  LT = nkt * 128
                  with ExitStack() as hs:
                      VAUG = sb("a_vaug", [128, nkt * HW * 129], BF16, hs)
                      PHI = sb("a_phi", [128, HW * LT], BF16, hs)
                      with ExitStack() as vs:
                          ps_v = vs.enter_context(nc.psum_tensor(_uniq("a_psv"), [128, 1024], F32)).ap()
                          mset = k.run("ve", lambda v: v.memset(VAUG, 1.0))
                          vv = None
                          for kti in range(nkt):
                              kt = kt0 + kti
                              k.wait("pe", vv, *cvm)

                              def f_vmm(t, kt=kt):
                                  last = None
                                  for hf in range(2):
                                      for kc in range(NC8):
                                          last = t.matmul(ps_v[:, hf * 512:(hf + 1) * 512],
                                                          XN[:, kc * T + kt * 128: kc * T + (kt + 1) * 128],
                                                          WVB[:, kc * DM + hf * 512: kc * DM + (hf + 1) * 512],
                                                          start=(kc == 0), stop=(kc == NC8 - 1),
                                                          skip_group_check=True)
                                  return last
                              mv = k.run("pe", f_vmm)
                              k.wait("ve", mv, mset)
                              dstv = VAUG.rearrange("p (kt h s) -> p kt h s", h=HW, s=129)[:, kti, :, 0:128]
                              vv = k.run("ve", lambda v, dstv=dstv: v.tensor_copy(
                                  dstv, ps_v.rearrange("p (h s) -> p h s", s=128)))
                          k.fence()
                          if H == 0:
                              dbg('vaug', VAUG)
                      with ExitStack() as qs:
                          QH = sb("a_qh", [128, HW * LT], BF16, qs)
                          KH = sb("a_kh", [128, HW * LT], BF16, qs)
                          WST = [sb(f"a_wst{i}", [128, DM], F32, qs) for i in range(2)]
                          WBB = [w.bitcast(BF16).rearrange("p (n two) -> p n two", two=2)[:, :, 1]
                                 for w in WST]
                          ps_qk = [qs.enter_context(nc.psum_tensor(_uniq(f"a_psqk{i}"), [128, 512], F32)).ap()
                                   for i in range(2)]
                          blks = [(0, 512), (512, 512), (1024, 128)]
                          wrd = [None, None]
                          evm = [None, None]
                          for mb in range(16):
                              b = mb % 2
                              dst = QH if mb < 8 else KH
                              hh = mb % 8
                              k.wait("sync", wrd[b])
                              md = k.dma(WST[b], d_qkmb[mb * 128:(mb + 1) * 128, :])
                              for bi, (lo, ln_) in enumerate(blks):
                                  pbk = bi % 2
                                  k.wait("pe", md, evm[pbk])

                                  def f_mm(t, b=b, lo=lo, ln_=ln_, pbk=pbk):
                                      last = None
                                      for kc in range(NC8):
                                          last = t.matmul(ps_qk[pbk][:, 0:ln_],
                                                          WBB[b][:, kc * 128:(kc + 1) * 128],
                                                          XN[:, kc * T + kt0 * 128 + lo: kc * T + kt0 * 128 + lo + ln_],
                                                          start=(kc == 0), stop=(kc == NC8 - 1),
                                                          skip_group_check=True)
                                      return last
                                  mm = k.run("pe", f_mm)
                                  wrd[b] = mm
                                  k.wait("ve", mm)
                                  evm[pbk] = k.run("ve", lambda v, dst=dst, hh=hh, lo=lo, ln_=ln_, pbk=pbk:
                                                   v.tensor_copy(dst[:, hh * LT + lo: hh * LT + lo + ln_],
                                                                 ps_qk[pbk][:, 0:ln_]))
                          ps_ph3 = qs.enter_context(nc.psum_tensor(_uniq("a_psph3"), [128, 512], F32)).ap()
                          PS3 = [ps_qk[0], ps_qk[1], ps_ph3]
                          E1 = [qs.enter_context(nc.psum_tensor(_uniq(f"a_e1p{i}"), [128, 512], F32)).ap()
                                for i in range(3)]
                          E2 = [sb(f"a_e2{i}", [128, 512], F32, qs) for i in range(2)]
                          e2_rd = [None, None]
                          phm = [evm[0], evm[1], None]
                          for hh in range(HW):
                              for bi, (lo, ln_) in enumerate(blks):
                                  pbk = (hh * 3 + bi) % 3
                                  k.wait("pe", phm[pbk],
                                         later(evm[0], evm[1]) if hh == 0 and bi == 0 else None)

                                  def f_phi(t, hh=hh, lo=lo, ln_=ln_, pbk=pbk):
                                      t.matmul(PS3[pbk][0:64, 0:ln_], PROJB,
                                               QH[:, hh * LT + lo: hh * LT + lo + ln_],
                                               start=True, stop=True, skip_group_check=True)
                                      return t.matmul(PS3[pbk][64:128, 0:ln_], PROJB,
                                                      KH[:, hh * LT + lo: hh * LT + lo + ln_],
                                                      start=True, stop=True, skip_group_check=True)
                                  mph = k.run("pe", f_phi)
                                  k.wait("ve", mph)
                                  me1 = k.run("ve", lambda v, pbk=pbk, ln_=ln_: v.tensor_single_scalar(
                                      E1[pbk][:, 0:ln_], PS3[pbk][:, 0:ln_], 0.0, op=OP.min))
                                  eb = (hh * 3 + bi) % 2
                                  k.wait("act", me1, e2_rd[eb])
                                  me2 = k.run("act", lambda s, pbk=pbk, eb=eb, ln_=ln_: s.activation(
                                      E2[eb][:, 0:ln_], E1[pbk][:, 0:ln_], AF.Exp))
                                  k.wait("ve", me2)
                                  phm[pbk] = k.run("ve", lambda v, hh=hh, lo=lo, ln_=ln_, pbk=pbk, eb=eb:
                                                   v.scalar_tensor_tensor(
                                                       PHI[:, hh * LT + lo: hh * LT + lo + ln_],
                                                       PS3[pbk][:, 0:ln_], 0.0, E2[eb][:, 0:ln_],
                                                       op0=OP.max, op1=OP.add))
                                  e2_rd[eb] = phm[pbk]
                          k.fence()
                          if H == 0:
                              dbg('qh', QH)
                              dbg('kh', KH)
                              dbg('phi', PHI)
                      with ExitStack() as ss:
                          MASKA = sb(f"maska{H}", [128, 3 * 384], F32, ss)
                          m_ma = k.dma(MASKA, d_maska)
                          k.wait("ve", m_ma)
                          PHIK0 = sb("a_phik0", [64, HW * LT], BF16, ss)
                          mpk = k.run("ve", lambda v: v.tensor_copy(PHIK0, PHI[64:128, :]))
                          k.wait("pe", mpk)
                          STM = [sb(f"a_stm{i}", [128, 384], BF16, ss) for i in range(2)]
                          ATG = [sb(f"a_atg{i}", [128, 3 * 128], BF16, ss) for i in range(2)]
                          Z3 = sb("a_z3", [128, 3], F32, ss)
                          Z3R = sb("a_z3r", [128, 3], F32, ss)
                          ATTNH = sb("a_attnh", [128, HW * 512], BF16, ss)
                          WOC32 = [sb(f"a_woc32{i}", [128, DM], F32, ss) for i in range(2)]
                          WOC = [sb(f"a_woc{i}", [128, DM], BF16, ss) for i in range(2)]
                          ps_st = [ss.enter_context(nc.psum_tensor(_uniq(f"a_psst{i}"), [128, 384], F32)).ap()
                                   for i in range(2)]
                          ps_av = [ss.enter_context(nc.psum_tensor(_uniq(f"a_psav{i}"), [128, 512], F32)).ap()
                                   for i in range(2)]
                          ps_tp = [ss.enter_context(nc.psum_tensor(_uniq(f"a_pstp{i}"), [128, 128], BF16)).ap()
                                   for i in range(2)]
                          ps_o = [ss.enter_context(nc.psum_tensor(_uniq(f"a_pso{i}"), [128, 512], F32)).ap()
                                  for i in range(2)]
                          mz0 = k.run("ve", lambda v: v.memset(ps_st[0], 0.0))
                          mz1 = k.run("ve", lambda v: v.memset(ps_st[1], 0.0))
                          k.wait("pe", mz0, mz1)
                          if H == 0:
                              def f_bias(v):
                                  last = None
                                  for kc in range(NC8):
                                      last = v.tensor_scalar_add(X[:, kc * T:(kc + 1) * T],
                                                                 X[:, kc * T:(kc + 1) * T],
                                                                 CB[:, kc:kc + 1])
                                  return last
                              k.run("ve", f_bias)
                          st_rd = [None, None]
                          woc_rd = [None, None]
                          stw = [None, None]
                          atg_rd = [None, None]
                          av_rd = [None, None]
                          tp_rd = [None, None]
                          attnh_rd = None
                          wocm = [None, None]
                          orm = [None, None]
                          for tg in range(2):
                              taus = [8 * H + tg * 4 + i for i in range(4)]
                              groups = [taus[0:3], taus[3:4]]
                              units = []
                              for hh in range(HW):
                                  for gi, grp in enumerate(groups):
                                      for si, tau in enumerate(grp):
                                          units.append((hh, gi, grp, si, tau))
                              NU = len(units)
                              mstm_l = [None] * NU
                              pend = [None]

                              def flush_pend(tg=tg):
                                  if pend[0] is None:
                                      return
                                  hh, gb, grp, msc2 = pend[0]
                                  pend[0] = None
                                  for si, tau in enumerate(grp):
                                      tpb = tau % 2
                                      k.wait("pe", msc2 if si == 0 else None, tp_rd[tpb])
                                      mtp = k.run("pe", lambda t, gb=gb, si=si, tpb=tpb: t.transpose(
                                          ps_tp[tpb], ATG[gb][:, si * 128:(si + 1) * 128], IDENT))
                                      atg_rd[gb] = mtp
                                      k.wait("ve", mtp, attnh_rd)
                                      tp_rd[tpb] = k.run("ve", lambda v, hh=hh, tau=tau, tpb=tpb, tg=tg:
                                                         v.tensor_copy(
                                                             ATTNH[:, hh * 512 + (tau - 8 * H - tg * 4) * 128:
                                                                   hh * 512 + (tau - 8 * H - tg * 4 + 1) * 128],
                                                             ps_tp[tpb]))

                              def emit_s(u):
                                  hh, gi, grp, si, tau = units[u]
                                  kappas = [x for x in (tau - 1, tau, tau + 1) if 0 <= x < NT]
                                  stb = tau % 2
                                  k.wait("pe", st_rd[stb], stw[stb])

                                  def f_st(t, hh=hh, tau=tau, kappas=kappas, stb=stb):
                                      last = None
                                      for j, kp in enumerate(kappas):
                                          js = kp - (tau - 1)
                                          last = t.matmul(
                                              ps_st[stb][:, js * 128:(js + 1) * 128],
                                              PHIK0[:, hh * LT + (kp - kt0) * 128: hh * LT + (kp - kt0 + 1) * 128],
                                              PHI[0:64, hh * LT + (tau - kt0) * 128: hh * LT + (tau - kt0 + 1) * 128],
                                              start=(j == 0), stop=(j == len(kappas) - 1),
                                              skip_group_check=True)
                                      return last
                                  mst = k.run("pe", f_st)
                                  mvar = 0 if tau == 0 else (2 if tau == NT - 1 else 1)
                                  k.wait("ve", mst, st_rd[stb])
                                  mstm_l[u] = k.run("ve", lambda v, stb=stb, mvar=mvar: v.tensor_tensor(
                                      STM[stb], ps_st[stb], MASKA[:, mvar * 384:(mvar + 1) * 384],
                                      op=OP.mult))
                                  stw[stb] = mstm_l[u]

                              emit_s(0)
                              for u in range(NU):
                                  hh, gi, grp, si, tau = units[u]
                                  if u + 1 < NU:
                                      emit_s(u + 1)
                                  gb = gi % 2
                                  stb = tau % 2
                                  kappas = [x for x in (tau - 1, tau, tau + 1) if 0 <= x < NT]
                                  k.wait("pe", mstm_l[u], av_rd[gb] if si == 0 else None)

                                  def f_av(t, kappas=kappas, tau=tau, stb=stb, si=si, gb=gb, hh=hh):
                                      last = None
                                      for j, kp in enumerate(kappas):
                                          last = t.matmul(
                                              ps_av[gb][:, si * 170: si * 170 + 129],
                                              STM[stb][:, (kp - (tau - 1)) * 128:(kp - (tau - 1) + 1) * 128],
                                              VAUG[:, (kp - kt0) * (HW * 129) + hh * 129:
                                                   (kp - kt0) * (HW * 129) + hh * 129 + 129],
                                              start=(si == 0 and j == 0), stop=False,
                                              skip_group_check=True)
                                      return last
                                  avm = k.run("pe", f_av)
                                  st_rd[stb] = avm
                                  if si == len(grp) - 1:
                                      flush_pend()
                                      ng = len(grp)
                                      k.wait("act", avm, atg_rd[gb])
                                      k.run("act", lambda s, gb=gb, ng=ng: s.activation(
                                          Z3[:, 0:ng], pslots(ps_av[gb], 170, ng, 1, 128),
                                          AF.Ln, bias=EPS8))
                                      mzr = k.run("act", lambda s, ng=ng: s.activation(
                                          Z3R[:, 0:ng], Z3[:, 0:ng], AF.Exp, scale=-1.0))
                                      k.wait("ve", avm, mzr)

                                      def f_sc(v, gb=gb, ng=ng):
                                          return v.tensor_tensor(
                                              ATG[gb].rearrange("p (s q) -> p s q", q=128)[:, 0:ng, :],
                                              pslots(ps_av[gb], 170, ng, 128),
                                              bview(Z3R[:, 0:ng], 128), op=OP.mult)
                                      msc2 = k.run("ve", f_sc)
                                      av_rd[gb] = msc2
                                      pend[0] = (hh, gb, grp, msc2)
                              flush_pend()
                              if H == 0 and tg == 0:
                                  dbg('attnh', ATTNH)
                              tglob = 8 * H * 128 + tg * 512
                              last_evs = []
                              for c in range(NC8):
                                  b = c % 2
                                  k.wait("sync", wocm[b])
                                  mdw = k.dma(WOC32[b], d_woutmb[c * 128:(c + 1) * 128, :])
                                  k.wait("act", mdw, wocm[b], woc_rd[b])
                                  wocm[b] = k.run("act", lambda s, b=b: s.activation(WOC[b], WOC32[b], AF.Copy))
                                  k.wait("pe", wocm[b], orm[b], tp_rd[0], tp_rd[1])

                                  def f_omm(t, c=c, b=b):
                                      last = None
                                      for kc in range(NC8):
                                          last = t.matmul(ps_o[b], WOC[b][:, kc * 128:(kc + 1) * 128],
                                                          ATTNH[:, kc * 512:(kc + 1) * 512],
                                                          start=(kc == 0), stop=(kc == NC8 - 1),
                                                          skip_group_check=True)
                                      return last
                                  mo = k.run("pe", f_omm)
                                  woc_rd[b] = mo
                                  k.wait("ve", mo)
                                  orm[b] = k.run("ve", lambda v, c=c, tglob=tglob, b=b: v.tensor_tensor(
                                      X[:, c * T + tglob: c * T + tglob + 512],
                                      X[:, c * T + tglob: c * T + tglob + 512], ps_o[b], op=OP.add))
                                  last_evs.append(orm[b])
                              attnh_rd = None
                              k.wait("ve", *last_evs)
                          k.fence()

              for H in range(2):
                  _half(H)

        def ffn(app):
            with ExitStack() as ls:
                XNF = sb(f"f{app}_xn", [128, NC8 * T], BF16, ls)
                RW = sb(f"f{app}_rw", [128, 256], F32, ls)
                RWB16 = sb(f"f{app}_rwb", [128, 256], BF16, ls)
                RWT = sb(f"f{app}_rwt", [16, T], BF16, ls)
                QKFQ = sb(f"f{app}_qkfq", [64, T], BF16, ls)
                QKFK = sb(f"f{app}_qkfk", [64, T], BF16, ls)
                VAUGF = sb(f"f{app}_vaugf", [128, NT * 65], BF16, ls)

                def v3(ap, e=16):
                    return ap.rearrange("p (s e) -> p s e", e=e)

                wc_cms = [nc.sbuf_tensor(_uniq(f"f{app}_wc{i}"), [128, 1536], F32) for i in range(2)]
                WC32 = [cm.__enter__().ap() for cm in wc_cms]
                wcb_cm = nc.sbuf_tensor(_uniq(f"f{app}_wcb"), [128, NC8 * 3 * DM], BF16)
                WCB = wcb_cm.__enter__().ap()
                wcm = [None, None]
                wc_all = []
                wc_dma = []
                lnm = {}

                def issue_wdma(kidx):
                    kc, hf = kidx // 2, kidx % 2
                    k.wait("sync", wcm[hf])
                    wc_dma.append(k.dma(WC32[hf], d_wcat[kc * 128:(kc + 1) * 128,
                                                         hf * 1536:(hf + 1) * 1536]))

                def emit_wcast(kidx):
                    kc, hf = kidx // 2, kidx % 2
                    k.wait("act", wc_dma[kidx], wcm[hf])
                    wcm[hf] = k.run("act", lambda s, kc=kc, hf=hf: s.activation(
                        WCB[:, kc * 3 * DM + hf * 1536: kc * 3 * DM + (hf + 1) * 1536],
                        WC32[hf], AF.Copy))
                    wc_all.append(wcm[hf])

                issue_wdma(0)
                issue_wdma(1)

                with ExitStack() as pa:
                    lnd = make_ln_scope(f"lnf{app}", pa, gating=True, nbuf=4)
                    LG = sb(f"f{app}_lg", [128, 64], F32, pa)
                    SC1 = sb(f"f{app}_sc1", [128, 64], F32, pa)
                    SC2 = sb(f"f{app}_sc2", [128, 64], F32, pa)
                    SC3 = sb(f"f{app}_sc3", [128, 64], F32, pa)
                    M1 = sb(f"f{app}_m1", [128, 4], F32, pa)
                    M2 = sb(f"f{app}_m2", [128, 4], F32, pa)
                    A1 = sb(f"f{app}_a1", [128, 4], F32, pa)
                    NA = sb(f"f{app}_na", [128, 4], F32, pa)
                    ps_gps = [pa.enter_context(nc.psum_tensor(_uniq(f"f{app}_gps{i}"), [128, 512], F32)).ap()[:, 0:64]
                              for i in range(2)]
                    gate_first = {}
                    gps_rd = [None, None]
                    for tb in range(4):
                        gp = tb % 2
                        gate_first[tb] = [True]

                        def gating_cb(kc, xsc, mx, tb=tb, gp=gp):
                            k.wait("pe", mx, gps_rd[gp] if kc == 0 else None)

                            def f_g(t, kc=kc, xsc=xsc, tb=tb, gp=gp):
                                last = None
                                for tl in range(4):
                                    last = t.matmul(ps_gps[gp][:, tl * 16:(tl + 1) * 16],
                                                    xsc[:, tl * 128:(tl + 1) * 128],
                                                    SIMC[:, kc * 16:(kc + 1) * 16],
                                                    start=gate_first[tb][0], stop=False,
                                                    skip_group_check=True)
                                    gate_first[tb][0] = False
                                return last
                            return k.run("pe", f_g)

                        ln_tb(lnd, tb, lambda kc, tb=tb: XNF[:, kc * T + tb * 512: kc * T + (tb + 1) * 512],
                              gating_cb=gating_cb, gnrc=True)
                        m_gmm = k.mile("pe")
                        if app == 0:
                            dbg(f"fgps{tb}", ps_gps[gp])
                        k.wait("ve", m_gmm)

                        def vstep(f, sync=True):
                            m = k.run("ve", f)
                            if sync:
                                k.self_sync("ve")
                            return m

                        gv = ps_gps[gp].rearrange("p (s e) -> p s e", e=16)
                        vstep(lambda v, gv=gv: v.tensor_tensor(v3(LG), gv, SNB.unsqueeze(1).broadcast_to([128, 4, 16]), op=OP.mult))
                        if tb == 0 and app == 0:
                            dbg("g_s1o", LG)
                        vstep(lambda v, tb=tb: v.tensor_tensor(v3(LG), v3(LG), bview(GNRC[:, tb * 4:(tb + 1) * 4], 16), op=OP.mult))
                        if tb == 0 and app == 0:
                            dbg("g_s2o", LG)
                        vstep(lambda v: v.tensor_tensor(v3(LG), v3(LG), SGB.unsqueeze(1).broadcast_to([128, 4, 16]), op=OP.subtract))
                        if tb == 0 and app == 0:
                            dbg("g_logits", LG)
                        vstep(lambda v: v.reduce_max(M1, v3(LG), axis=AX.X), sync=True)
                        vstep(lambda v: v.tensor_single_scalar(SC1, LG, 0.0, op=OP.is_gt))
                        vstep(lambda v: v.tensor_tensor(v3(SC2), v3(LG), bview(M1, 16), op=OP.is_ge))
                        vstep(lambda v: v.scalar_tensor_tensor(v3(SC3), v3(SC2), -1e9, v3(LG), op0=OP.mult, op1=OP.add))
                        vstep(lambda v: v.reduce_max(M2, v3(SC3), axis=AX.X), sync=True)
                        vstep(lambda v: v.tensor_tensor(v3(SC2), v3(LG), bview(M2, 16), op=OP.is_ge))
                        vstep(lambda v: v.tensor_single_scalar(A1, M1, 0.0, op=OP.is_gt))
                        vstep(lambda v: v.tensor_scalar(NA, A1, -1.0, 1.0, op0=OP.mult, op1=OP.add))
                        vstep(lambda v: v.tensor_tensor(v3(SC1), v3(SC1), bview(A1, 16), op=OP.mult))
                        vstep(lambda v: v.tensor_tensor(v3(SC3), v3(SC2), bview(NA, 16), op=OP.mult))
                        vstep(lambda v: v.tensor_add(SC1, SC1, SC3))
                        if tb == 3 and app == 0:
                            dbg("g_sc1f", SC1)
                            dbg("g_m1f", M1)
                            dbg("g_m2f", M2)
                            dbg("g_sc2f", SC2)
                        vstep(lambda v: v.tensor_single_scalar(M1, M1, 0.0, op=OP.max))
                        vstep(lambda v: v.tensor_single_scalar(LG, LG, 0.0, op=OP.max))
                        mgc = vstep(lambda v: v.tensor_tensor(v3(LG), v3(LG), bview(M1, 16), op=OP.subtract))
                        k.wait("act", mgc)
                        mge = k.run("act", lambda s: s.activation(LG, LG, AF.Exp))
                        k.wait("ve", mge)
                        vstep(lambda v: v.tensor_mul(LG, LG, SC1))
                        vstep(lambda v: v.reduce_sum(M2, v3(LG), axis=AX.X), sync=True)
                        vstep(lambda v: v.reciprocal(M1, M2), sync=True)
                        vstep(lambda v, tb=tb: v.tensor_tensor(v3(RW[:, tb * 64:(tb + 1) * 64]), v3(LG), bview(M1, 16), op=OP.mult))
                        gps_rd[gp] = vstep(lambda v, tb=tb: v.tensor_copy(RWB16[:, tb * 64:(tb + 1) * 64], RW[:, tb * 64:(tb + 1) * 64]))
                        lnm[tb] = k.mile("ve")
                        for i_ in range(4):
                            kidx = tb * 4 + i_
                            emit_wcast(kidx)
                            if kidx + 2 < 16:
                                issue_wdma(kidx + 2)
                        if tb == 3 and app == 0:
                            dbg("g_lg", LG); dbg("g_sc1", SC1); dbg("g_m1", M1); dbg("g_m2", M2)
                    dbg(f"frw{app}", RW)
                    dbg(f"fgnrc{app}", GNRC)
                    gps_done = k.mile("ve")

                with ExitStack() as pb:
                    QCT = [sb(f"f{app}_qct{i}", [128, 1536], BF16, pb) for i in range(4)]
                    DIAG2 = [sb(f"f{app}_diag{i}", [128, 16 * 128], BF16, pb) for i in range(2)]
                    QKT = sb(f"f{app}_qkt", [128, 128], BF16, pb)
                    ps_qc = [pb.enter_context(nc.psum_tensor(_uniq(f"f{app}_psqc{i}"), [128, 1536], F32)).ap()
                             for i in range(2)]
                    ps_qkvb = pb.enter_context(nc.psum_tensor(_uniq(f"f{app}_psqkv"), [128, 512], F32)).ap()
                    ps_qkv = ps_qkvb[:, 0:192]
                    ps_mis = pb.enter_context(nc.psum_tensor(_uniq(f"f{app}_psmis"), [128, 1024], BF16)).ap()
                    ps_tp = ps_mis[:, 0:128]
                    ps_rt = ps_mis[0:16, 128:256]

                    msetf = k.run("ve", lambda v: v.memset(VAUGF, 1.0))

                    qct_rd = [None, None, None, None]
                    qc_rd = [None, None]
                    diag_rd = [None, None]
                    qkv_rd = None
                    tp_rd = None
                    evs = {}
                    mdg_l = {}

                    def emit_qc(tau):
                        for hf in range(2):
                            si = (tau * 2 + hf) % 4
                            qb = hf
                            k.wait("pe", qc_rd[qb], lnm[tau // 4] if hf == 0 else None,
                                   *( wc_all + [gps_done] if tau == 0 and hf == 0 else []))

                            def f_qc(t, tau=tau, hf=hf, qb=qb):
                                last = None
                                for kc in range(NC8):
                                    for nb in range(3):
                                        last = t.matmul(
                                            ps_qc[qb][:, nb * 512:(nb + 1) * 512],
                                            XNF[:, kc * T + tau * 128: kc * T + (tau + 1) * 128],
                                            WCB[:, kc * 3 * DM + hf * 1536 + nb * 512:
                                                kc * 3 * DM + hf * 1536 + (nb + 1) * 512],
                                            start=(kc == 0), stop=(kc == NC8 - 1),
                                            skip_group_check=True)
                                return last
                            mqc = k.run("pe", f_qc)
                            k.wait("ve", mqc, qct_rd[si])
                            qc_rd[qb] = k.run("ve", lambda v, si=si, qb=qb: v.tensor_copy(QCT[si], ps_qc[qb]))
                            qct_rd[si] = qc_rd[qb]
                            evs.setdefault(tau, []).append(qc_rd[qb])

                    def emit_diag(tau):
                        db_ = tau % 2
                        k.wait("ve", diag_rd[db_])
                        mdg_l[tau] = k.run("ve", lambda v, tau=tau, db_=db_: v.tensor_tensor(
                            DIAG2[db_].rearrange("p (e q) -> p e q", q=128),
                            IDENT.unsqueeze(1).broadcast_to([128, 16, 128]),
                            bview(RW[:, tau * 16:(tau + 1) * 16], 128), op=OP.mult))

                    emit_qc(0)
                    emit_diag(0)
                    for tau in range(NT):
                        tb, tl = tau // 4, tau % 4
                        if tau + 1 < NT:
                            emit_qc(tau + 1)
                            emit_diag(tau + 1)
                        db_ = tau % 2
                        k.wait("pe", mdg_l[tau], *evs[tau], qkv_rd)

                        def f_qkv(t, tau=tau, db_=db_):
                            last = None
                            for e in range(16):
                                si = (tau * 2 + (e // 8)) % 4
                                eo = (e % 8) * 192
                                last = t.matmul(ps_qkv, DIAG2[db_][:, e * 128:(e + 1) * 128],
                                                QCT[si][:, eo: eo + 192],
                                                start=(e == 0), stop=(e == 15), skip_group_check=True)
                            return last
                        mqkv = k.run("pe", f_qkv)
                        diag_rd[db_] = mqkv
                        qct_rd[(tau * 2) % 4] = mqkv
                        qct_rd[(tau * 2 + 1) % 4] = mqkv
                        k.wait("act", mqkv, msetf if tau == 0 else None, tp_rd)

                        def f_ev(s, tau=tau):
                            s.activation(QKT, ps_qkv[:, 0:128], AF.Copy)
                            return s.activation(VAUGF[:, tau * 65: tau * 65 + 64],
                                                ps_qkv[:, 128:192], AF.Copy)
                        qkv_rd = k.run("act", f_ev)
                        k.wait("pe", qkv_rd, tp_rd)

                        def f_tp(t, tau=tau):
                            t.transpose(ps_tp[0:64, :], QKT[:, 0:64], IDENT)
                            t.transpose(ps_tp[64:128, :], QKT[:, 64:128], IDENT)
                            return t.transpose(ps_rt, RWB16[:, tau * 16:(tau + 1) * 16], IDENT)
                        mtp = k.run("pe", f_tp)
                        k.wait("act", mtp)

                        def f_tpe(s, tau=tau):
                            s.activation(QKFQ[:, tau * 128:(tau + 1) * 128], ps_tp[0:64, :], AF.Copy)
                            s.activation(QKFK[:, tau * 128:(tau + 1) * 128], ps_tp[64:128, :], AF.Copy)
                            return s.activation(RWT[:, tau * 128:(tau + 1) * 128], ps_rt, AF.Copy)
                        tp_rd = k.run("act", f_tpe)
                    dbg(f"fqkfq{app}", QKFQ)
                    dbg(f"fqkfk{app}", QKFK)
                    dbg(f"fvaug{app}", VAUGF)
                    dbg(f"frwt{app}", RWT)
                    k.fence()

                wcb_cm.__exit__(None, None, None)
                for cm_ in reversed(wc_cms):
                    cm_.__exit__(None, None, None)
                pf_cm = nc.sbuf_tensor(_uniq(f"f{app}_pf"), [128, T], BF16)
                PF = pf_cm.__enter__().ap()
                at2_cm = nc.sbuf_tensor(_uniq(f"f{app}_attn2"), [128, T], BF16)
                ATTN2 = at2_cm.__enter__().ap()

                with ExitStack() as pc:
                    ATG = sb(f"f{app}_atg", [128, 4 * 64], BF16, pc)
                    Z4 = sb(f"f{app}_z4", [128, 4], F32, pc)
                    Z4R = sb(f"f{app}_z4r", [128, 4], F32, pc)
                    ps_sc = pc.enter_context(nc.psum_tensor(_uniq(f"f{app}_pssc"), [128, 2048], F32)).ap()
                    ps_av = pc.enter_context(nc.psum_tensor(_uniq(f"f{app}_psav"), [128, 512], F32)).ap()
                    ps_atb = pc.enter_context(nc.psum_tensor(_uniq(f"f{app}_psat"), [128, 1024], BF16)).ap()
                    ps_at = ps_atb[0:64, 0:128]
                    mz_sc = k.run("ve", lambda v: v.memset(ps_sc, 0.0))
                    pf_rd = None
                    avz_rd = None
                    at2_rd = None
                    for tb in range(4):
                        for tl in range(4):
                            tau = tb * 4 + tl
                            nk = tau + 1
                            k.wait("pe", pf_rd, mz_sc if tau == 0 else None)

                            def f_scm(t, tau=tau, nk=nk):
                                last = None
                                for kp in range(nk):
                                    last = t.matmul(ps_sc[:, kp * 128:(kp + 1) * 128],
                                                    QKFK[:, kp * 128:(kp + 1) * 128],
                                                    QKFQ[:, tau * 128:(tau + 1) * 128],
                                                    start=(kp % 4 == 0), stop=(kp == nk - 1),
                                                    skip_group_check=True)
                                return last
                            msc = k.run("pe", f_scm)
                            nch = (nk + 3) // 4
                            for c in range(nch):
                                klo, khi = c * 4, min(nk, c * 4 + 4)
                                k.wait("act", msc if c == 0 else None, pf_rd if c == 0 else None)
                                mex = k.run("act", lambda s, klo=klo, khi=khi: s.activation(
                                    PF[:, klo * 128:khi * 128], ps_sc[:, klo * 128:khi * 128],
                                    AF.Exp, scale=0.125))
                                if khi == nk:
                                    k.wait("ve", mex)
                                    mex = k.run("ve", lambda v, tau=tau: v.tensor_tensor(
                                        PF[:, tau * 128:(tau + 1) * 128],
                                        PF[:, tau * 128:(tau + 1) * 128], TRI, op=OP.mult))
                                k.wait("pe", mex, avz_rd if tl == 0 and c == 0 else None)

                                def f_av(t, klo=klo, khi=khi, tl=tl):
                                    last = None
                                    for kp in range(klo, khi):
                                        last = t.matmul(ps_av[:, tl * 128: tl * 128 + 65],
                                                        PF[:, kp * 128:(kp + 1) * 128],
                                                        VAUGF[:, kp * 65:(kp + 1) * 65],
                                                        start=(tl == 0 and kp == 0), stop=False,
                                                        skip_group_check=True)
                                    return last
                                pf_rd = k.run("pe", f_av)
                        k.wait("ve", pf_rd)

                        k.run("ve", lambda v: v.tensor_copy(Z4, pslots(ps_av, 128, 4, 1, 64)))
                        k.self_sync("ve")
                        mz4 = k.run("ve", lambda v: v.reciprocal(Z4R, Z4))
                        k.self_sync("ve")

                        def f_zsc(v):
                            return v.tensor_tensor(ATG.rearrange("p (s q) -> p s q", q=64),
                                                   pslots(ps_av, 128, 4, 64), bview(Z4R, 64), op=OP.mult)
                        avz_rd = k.run("ve", f_zsc)
                        at_rd = None
                        for tl in range(4):
                            tau = tb * 4 + tl
                            k.wait("pe", avz_rd, at_rd)
                            mat = k.run("pe", lambda t, tl=tl: t.transpose(
                                ps_at, ATG[:, tl * 64:(tl + 1) * 64], IDENT))
                            k.wait("ve", mat, at2_rd)

                            def f_a2(v, tau=tau):
                                v.tensor_copy(ATTN2[0:64, tau * 128:(tau + 1) * 128], ps_at)
                                return v.tensor_copy(ATTN2[64:128, tau * 128:(tau + 1) * 128], ps_at)
                            at_rd = k.run("ve", f_a2)
                            at2_rd = at_rd
                        avz_rd = k.mile("pe")
                    dbg(f"fattn2{app}", ATTN2)
                    k.fence()

                with ExitStack() as pd:
                    SEL = sb(f"f{app}_sel", [16, NC8 * 128], BF16, pd)
                    SELF_ = sb(f"f{app}_self", [16, NC8 * 128], F32, pd)
                    m_self = k.dma(SELF_, d_sel)
                    k.wait("ve", m_self)
                    m_sel = k.run("ve", lambda v: v.tensor_copy(SEL, SELF_))
                    if app == 0:
                        dbg("c_sel", SEL)
                    k.wait("pe", m_sel)
                    OC32 = [sb(f"f{app}_oc{i}", [128, DM], F32, pd) for i in range(2)]
                    OCB = sb(f"f{app}_ocb", [128, NC8 * DM], BF16, pd)
                    MXD2 = [sb(f"f{app}_mxd{i}", [128, NC8 * 512], BF16, pd) for i in range(2)]
                    RWBB = sb(f"f{app}_rwbb", [128, 512], BF16, pd)
                    ps_rwb = [pd.enter_context(nc.psum_tensor(_uniq(f"f{app}_psrwb{i}"), [128, 512], F32)).ap()
                              for i in range(2)]
                    ps_o = [pd.enter_context(nc.psum_tensor(_uniq(f"f{app}_pso{i}"), [128, 512], F32)).ap()
                            for i in range(2)]
                    ocm = [None, None]
                    oc_all = [m_sel]
                    for mb in range(8):
                        b = mb % 2
                        k.wait("sync", ocm[b])
                        mdo = k.dma(OC32[b], d_ocatmb[mb * 128:(mb + 1) * 128, :])
                        k.wait("act", mdo, ocm[b])
                        ocm[b] = k.run("act", lambda s, mb=mb, b=b: s.activation(
                            OCB.rearrange("p (c m) -> p c m", m=DM)[:, :, mb * 128:(mb + 1) * 128],
                            OC32[b].rearrange("p (c m) -> p c m", m=128), AF.Copy))
                        oc_all.append(ocm[b])
                    rwb_rd = [None, None]
                    mxd_rd = [None, None]
                    orm = [None, None]
                    mx_all_l = [None] * 4

                    def emit_mx(tb):
                        mp = tb % 2
                        mx_all = []
                        for kc in range(NC8):
                            rb = kc % 2
                            k.wait("pe", rwb_rd[rb], *(oc_all if tb == 0 and kc == 0 else []))
                            mrwb = k.run("pe", lambda t, kc=kc, tb=tb, rb=rb: t.matmul(
                                ps_rwb[rb], SEL[:, kc * 128:(kc + 1) * 128], RWT[:, tb * 512:(tb + 1) * 512],
                                start=True, stop=True, skip_group_check=True))
                            k.wait("ve", mrwb, mxd_rd[mp] if kc == 0 else None)

                            def f_mx(v, kc=kc, tb=tb, rb=rb, mp=mp):
                                v.tensor_copy(RWBB, ps_rwb[rb])
                                return v.tensor_tensor(MXD2[mp][:, kc * 512:(kc + 1) * 512],
                                                       ATTN2[:, tb * 512:(tb + 1) * 512], RWBB,
                                                       op=OP.mult)
                            rwb_rd[rb] = k.run("ve", f_mx)
                            if tb == 0 and app == 0 and kc == 0:
                                dbg("g_mxd0", MXD2[0][:, 0:512])
                                dbg("g_rwbb", RWBB)
                            mx_all.append(rwb_rd[rb])
                        mx_all_l[tb] = mx_all

                    emit_mx(0)
                    for tb in range(4):
                        if tb + 1 < 4:
                            emit_mx(tb + 1)
                        mp = tb % 2
                        o_mms = []
                        for c in range(NC8):
                            b = c % 2
                            k.wait("pe", orm[b], *(mx_all_l[tb] if c == 0 else []))

                            def f_oc(t, c=c, b=b, mp=mp):
                                last = None
                                for kc in range(NC8):
                                    last = t.matmul(ps_o[b], OCB[:, kc * DM + c * 128: kc * DM + (c + 1) * 128],
                                                    MXD2[mp][:, kc * 512:(kc + 1) * 512],
                                                    start=(kc == 0), stop=(kc == NC8 - 1),
                                                    skip_group_check=True)
                                return last
                            moc = k.run("pe", f_oc)
                            o_mms.append(moc)
                            k.wait("ve", moc)
                            orm[b] = k.run("ve", lambda v, c=c, tb=tb, b=b: v.tensor_tensor(
                                X[:, c * T + tb * 512: c * T + (tb + 1) * 512],
                                X[:, c * T + tb * 512: c * T + (tb + 1) * 512], ps_o[b], op=OP.add))
                        mxd_rd[mp] = o_mms[-1]
                    k.fence()
                at2_cm.__exit__(None, None, None)
                pf_cm.__exit__(None, None, None)

        def attn_b():
            with ExitStack() as ls:
                QB = sb("b_qb", [128, NC8 * T], BF16, ls)
                KBf = sb("b_kb", [128, NC8 * T], BF16, ls)
                VAUGB = sb("b_vaug", [128, NT * 16 * 65], BF16, ls)
                msetb = k.run("ve", lambda v: v.memset(VAUGB, 1.0))
                with ExitStack() as qs:
                    XNB2 = sb("b_xnb2", [128, NC8 * 1024], BF16, qs)
                    ps_qk = [qs.enter_context(nc.psum_tensor(_uniq(f"b_psqk{i}"), [128, 512], F32)).ap()
                             for i in range(2)]
                    ps_v = qs.enter_context(nc.psum_tensor(_uniq("b_psv"), [128, 512], F32)).ap()
                    vv = None
                    wvb_rd = None
                    evm = [None, None]
                    wrd_b = [None, None]
                    for pr in range(2):
                        with ExitStack() as lns:
                            lnd = make_ln_scope(f"lnb{pr}", lns, nbuf=3)
                            for j in range(2):
                                tb = 2 * pr + j
                                ln_tb(lnd, tb, lambda kc, j=j: XNB2[:, kc * 1024 + j * 512:
                                                                    kc * 1024 + (j + 1) * 512])
                            k.fence()
                        with ExitStack() as ws:
                            WST = [sb(f"b_wst{pr}_{i}", [128, DM], F32, ws) for i in range(2)]
                            WBB = [w.bitcast(BF16).rearrange("p (n two) -> p n two", two=2)[:, :, 1]
                                   for w in WST]
                            WV32 = sb(f"b_wv{pr}", [128, 512], F32, ws)
                            WVB = sb(f"b_wvb{pr}", [128, NC8 * 512], BF16, ws)
                            wvm = [None]
                            wv_pre = {0: [], 1: []}

                            def emit_wv(hf, kc):
                                k.wait("sync", wvm[0])
                                mdv = k.dma(WV32, d_wv[kc * 128:(kc + 1) * 128,
                                                       hf * 512:(hf + 1) * 512])
                                k.wait("act", mdv, wvm[0], wvb_rd)
                                wvm[0] = k.run("act", lambda s, kc=kc: s.activation(
                                    WVB[:, kc * 512:(kc + 1) * 512], WV32, AF.Copy))
                                wv_pre[hf].append(wvm[0])

                            for mb in range(16):
                                b = mb % 2
                                dst = QB if mb < 8 else KBf
                                src = d_wqmb if mb < 8 else d_wkmb
                                if mb >= 8:
                                    emit_wv(0, mb - 8)
                                k.wait("sync", wrd_b[b])
                                md = k.dma(WST[b], src[(mb % 8) * 128:((mb % 8) + 1) * 128, :])
                                mm = None
                                for j in range(2):
                                    tb = 2 * pr + j
                                    k.wait("pe", md if j == 0 else None, evm[j])

                                    def f_mm(t, b=b, j=j):
                                        last = None
                                        for kc in range(NC8):
                                            last = t.matmul(ps_qk[j], WBB[b][:, kc * 128:(kc + 1) * 128],
                                                            XNB2[:, kc * 1024 + j * 512:
                                                                 kc * 1024 + (j + 1) * 512],
                                                            start=(kc == 0), stop=(kc == NC8 - 1),
                                                            skip_group_check=True)
                                        return last
                                    mm = k.run("pe", f_mm)
                                    k.wait("ve", mm)
                                    bcol = CB[:, (1 if mb < 8 else 2) * NC8 + (mb % 8):
                                              (1 if mb < 8 else 2) * NC8 + (mb % 8) + 1]
                                    evm[j] = k.run("ve", lambda v, dst=dst, mb=mb, tb=tb, j=j, bcol=bcol:
                                                   v.tensor_scalar(
                                                       dst[:, (mb % 8) * T + tb * 512: (mb % 8) * T + (tb + 1) * 512],
                                                       ps_qk[j], bcol, None, op0=OP.add))
                                wrd_b[b] = mm
                            for hf in range(2):
                                if not wv_pre[hf]:
                                    for kc in range(NC8):
                                        emit_wv(hf, kc)
                                wvc = wv_pre[hf]
                                mv = None
                                for j in range(2):
                                    for kti in range(4):
                                        kt = pr * 8 + j * 4 + kti
                                        k.wait("pe", vv, *wvc)

                                        def f_vmm(t, j=j, kti=kti, hf=hf):
                                            last = None
                                            for kc in range(NC8):
                                                last = t.matmul(ps_v,
                                                                XNB2[:, kc * 1024 + j * 512 + kti * 128:
                                                                     kc * 1024 + j * 512 + (kti + 1) * 128],
                                                                WVB[:, kc * 512:(kc + 1) * 512],
                                                                start=(kc == 0), stop=(kc == NC8 - 1),
                                                                skip_group_check=True)
                                            return last
                                        mv = k.run("pe", f_vmm)
                                        k.wait("ve", mv, msetb)
                                        dstv = VAUGB.rearrange("p (kt h s) -> p kt h s", h=16, s=65)[:, kt, hf * 8:(hf + 1) * 8, 0:64]
                                        vv = k.run("ve", lambda v, dstv=dstv, hf=hf: v.tensor_tensor(
                                            dstv, ps_v.rearrange("p (h s) -> p h s", s=64),
                                            BVB.rearrange("p (h s) -> p h s", s=64)[:, hf * 8:(hf + 1) * 8, :],
                                            op=OP.add))
                                wvb_rd = mv
                            k.fence()

                if B_PHASE < 2:
                    return
                with ExitStack() as asx:
                    PTB = [sb(f"b_ptb{i}", [128, 2 * 512], BF16, asx) for i in range(2)]
                    ATG = [sb(f"b_atg{i}", [128, 4 * 64], BF16, asx) for i in range(2)]
                    Z4 = sb("b_z4", [128, 4], F32, asx)
                    Z4R = sb("b_z4r", [128, 4], F32, asx)
                    ATTNB = sb("b_attnb", [128, NC8 * 512], BF16, asx)
                    WOC32 = [sb(f"b_woc32{i}", [128, DM], F32, asx) for i in range(2)]
                    WOC = [sb(f"b_woc{i}", [128, DM], BF16, asx) for i in range(2)]
                    ps_sc = [asx.enter_context(nc.psum_tensor(_uniq(f"b_pssc{i}"), [128, 1024], F32)).ap()
                             for i in range(2)]
                    ps_av = [asx.enter_context(nc.psum_tensor(_uniq(f"b_psav{i}"), [128, 512], F32)).ap()
                             for i in range(2)]
                    ps_at4 = [asx.enter_context(nc.psum_tensor(_uniq(f"b_psat{i}"), [128, 1024], BF16)).ap()
                              for i in range(2)]
                    ps_o = [ps_sc[0][:, 0:512], ps_sc[0][:, 512:1024]]

                    def f_bias(v):
                        last = None
                        for kc in range(NC8):
                            last = v.tensor_scalar_add(X[:, kc * T:(kc + 1) * T],
                                                       X[:, kc * T:(kc + 1) * T],
                                                       CB[:, 3 * NC8 + kc: 3 * NC8 + kc + 1])
                        return last
                    k.run("ve", f_bias)

                    kgroups = [[2 * i, 2 * i + 1] for i in range(8)]
                    sc_rd = [None, None]
                    pt_rd = [None, None]
                    av_rd = [None, None]
                    atg_rd = [None, None]
                    at_rd = [None, None]
                    attnb_rd = None
                    wocm = [None, None]
                    wocb_rd = [None, None]
                    orm = [None, None]
                    pend = [None]

                    def flush_pend():
                        if pend[0] is None:
                            return None
                        hh, mzs = pend[0]
                        pend[0] = None
                        rlo = (hh % 2) * 64
                        hc = hh // 2
                        pb = hc % 2
                        ab = hh % 2
                        mat = None
                        for tl in range(4):
                            k.wait("pe", mzs if tl == 0 else None,
                                   at_rd[pb] if (hh % 2 == 0 and tl == 0) else None)
                            mat = k.run("pe", lambda t, tl=tl, rlo=rlo, pb=pb, ab=ab: t.transpose(
                                ps_at4[pb][rlo:rlo + 64, tl * 128:(tl + 1) * 128],
                                ATG[ab][:, tl * 64:(tl + 1) * 64], IDENT))
                        atg_rd[ab] = mat
                        if hh % 2 == 1:
                            k.wait("ve", mat, attnb_rd)
                            at_rd[pb] = k.run("ve", lambda v, hc=hc, pb=pb: v.tensor_copy(
                                ATTNB[:, hc * 512:(hc + 1) * 512], ps_at4[pb][:, 0:512]))
                            return at_rd[pb]
                        return None

                    for tq in range(4):
                        atbm = None
                        for hh in range(16):
                            rlo = (hh % 2) * 64
                            hc = hh // 2
                            apb = hh % 2
                            first_av = [True]
                            avm = None
                            NG = len(kgroups)
                            mex_l = [None] * NG

                            def emit_sc(gi):
                                grp = kgroups[gi]
                                gb = gi % 2
                                ms = []
                                for j, kp in enumerate(grp):
                                    k.wait("pe", sc_rd[gb] if j == 0 else None)
                                    mj = k.run("pe", lambda t, kp=kp, j=j, gb=gb, hc=hc,
                                               rlo=rlo, tq=tq: t.matmul(
                                        ps_sc[gb][:, j * 512:(j + 1) * 512],
                                        KBf[rlo:rlo + 64, hc * T + kp * 128: hc * T + (kp + 1) * 128],
                                        QB[rlo:rlo + 64, hc * T + tq * 512: hc * T + (tq + 1) * 512],
                                        start=True, stop=True, skip_group_check=True))
                                    k.wait("act", mj, pt_rd[gb] if j == 0 else None)
                                    ms.append(k.run("act", lambda s, j=j, gb=gb: s.activation(
                                        PTB[gb][:, j * 512:(j + 1) * 512],
                                        ps_sc[gb][:, j * 512:(j + 1) * 512], AF.Exp, scale=0.125)))
                                mex_l[gi] = ms
                                sc_rd[gb] = ms[-1]

                            emit_sc(0)
                            for gi in range(NG):
                                if gi + 1 < NG:
                                    emit_sc(gi + 1)
                                if B_PHASE < 2.5:
                                    continue
                                grp = kgroups[gi]
                                gb = gi % 2
                                avm = None
                                for j, kp in enumerate(grp):
                                    k.wait("pe", mex_l[gi][j],
                                           av_rd[apb] if first_av[0] else None)

                                    def f_avg(t, kp=kp, j=j, gb=gb, hh=hh, fa=first_av, apb=apb):
                                        last = None
                                        for tl in range(4):
                                            last = t.matmul(
                                                ps_av[apb][:, tl * 128: tl * 128 + 65],
                                                PTB[gb][:, j * 512 + tl * 128: j * 512 + (tl + 1) * 128],
                                                VAUGB[:, kp * 1040 + hh * 65: kp * 1040 + hh * 65 + 65],
                                                start=(fa[0] and tl == 0), stop=False,
                                                skip_group_check=True)
                                            fa[0] = False
                                        return last
                                    avm = k.run("pe", f_avg)
                                pt_rd[gb] = avm
                            cp = flush_pend()
                            if cp is not None:
                                atbm = cp
                            if B_PHASE < 2.7:
                                continue
                            k.wait("ve", avm, atg_rd[apb])
                            k.run("ve", lambda v, apb=apb: v.tensor_copy(
                                Z4, pslots(ps_av[apb], 128, 4, 1, 64)))
                            k.self_sync("ve")
                            mz = k.run("ve", lambda v: v.reciprocal(Z4R, Z4))
                            k.self_sync("ve")

                            def f_zsc(v, apb=apb):
                                return v.tensor_tensor(ATG[apb].rearrange("p (s q) -> p s q", q=64),
                                                       pslots(ps_av[apb], 128, 4, 64), bview(Z4R, 64),
                                                       op=OP.mult)
                            mzs = k.run("ve", f_zsc)
                            av_rd[apb] = mzs
                            if B_PHASE < 2.9:
                                continue
                            pend[0] = (hh, mzs)
                        cp = flush_pend()
                        if cp is not None:
                            atbm = cp
                        if B_PHASE < 3:
                            continue
                        o_last = []
                        for c in range(NC8):
                            b = c % 2
                            k.wait("sync", wocm[b])
                            mdw = k.dma(WOC32[b], d_womb[c * 128:(c + 1) * 128, :])
                            k.wait("act", mdw, wocm[b], wocb_rd[b])
                            wocm[b] = k.run("act", lambda s, b=b: s.activation(WOC[b], WOC32[b], AF.Copy))
                            k.wait("pe", wocm[b], orm[b], atbm)

                            def f_oc(t, c=c, b=b):
                                last = None
                                for kc in range(NC8):
                                    last = t.matmul(ps_o[b], WOC[b][:, kc * 128:(kc + 1) * 128],
                                                    ATTNB[:, kc * 512:(kc + 1) * 512],
                                                    start=(kc == 0), stop=(kc == NC8 - 1),
                                                    skip_group_check=True)
                                return last
                            moc = k.run("pe", f_oc)
                            wocb_rd[b] = moc
                            o_last.append(moc)
                            k.wait("ve", moc)
                            orm[b] = k.run("ve", lambda v, c=c, tq=tq, b=b: v.tensor_tensor(
                                X[:, c * T + tq * 512: c * T + (tq + 1) * 512],
                                X[:, c * T + tq * 512: c * T + (tq + 1) * 512], ps_o[b], op=OP.add))
                        attnb_rd = o_last[-1]
                        sc_rd[0] = later(orm[0], orm[1])
                    k.fence()

        seq = [attn_a, lambda: ffn(0), attn_b, lambda: ffn(1)][:parts]
        for f in seq:
            f()

        out_m = [k.dma(d_out[kc * 128:(kc + 1) * 128, :], X[:, kc * T:(kc + 1) * T].bitcast(F32))
                 for kc in range(NC8)]
        k.wait("sync", *out_m)

        with nc.Block() as blk:
            @blk.tensor
            def _(eng):
                for f in k.q["pe"]:
                    f(eng)

            @blk.scalar
            def _(eng):
                for f in k.q["act"]:
                    f(eng)

            @blk.vector
            def _(eng):
                for f in k.q["ve"]:
                    f(eng)

            @blk.gpsimd
            def _(eng):
                for f in k.q["pool"]:
                    f(eng)

            @blk.sync
            def _(eng):
                for f in k.q["sync"]:
                    f(eng)

    return nc


def _hi16(w):
    return np.ascontiguousarray(w.astype(ml_dtypes.bfloat16).astype(np.float32))


def _mb_layout(w, nblk):
    out = np.empty((nblk * 128, DM), np.float32)
    for mb in range(nblk):
        blk = w[:, mb * 128:(mb + 1) * 128]
        out[mb * 128:(mb + 1) * 128] = blk.reshape(NC8, 128, 128).transpose(1, 0, 2).reshape(128, DM)
    return np.ascontiguousarray(out)


def _host_consts():
    ident = np.eye(128, dtype=np.float32)

    def band(koff):
        j = np.arange(128)[:, None]
        t = np.arange(128)[None, :]
        d = j + 128 * koff - t
        return ((d >= -3) & (d <= 3)).astype(np.float32)
    m_prev, m_diag, m_next = band(-1), band(0), band(1)
    zero = np.zeros((128, 128), np.float32)
    first = np.concatenate([zero, m_diag, m_next], axis=1)
    mid = np.concatenate([m_prev, m_diag, m_next], axis=1)
    last = np.concatenate([m_prev, m_diag, zero], axis=1)
    maska = np.concatenate([first, mid, last], axis=1).astype(np.float32)
    tri = (np.arange(128)[:, None] <= np.arange(128)[None, :]).astype(np.float32)
    sel = np.zeros((16, NC8 * 128), np.float32)
    for kc in range(NC8):
        sel[2 * kc, kc * 128: kc * 128 + 64] = 1.0
        sel[2 * kc + 1, kc * 128 + 64: kc * 128 + 128] = 1.0
    return ident, maska, tri, np.ascontiguousarray(sel)


def kernel(**inp):
    x = np.asarray(inp["L0_attn_a_x"], np.float32)
    w_qkv = np.asarray(inp["L0_attn_a_w_qkv"], np.float32)
    projm = np.ascontiguousarray(np.asarray(inp["L0_attn_a_proj"], np.float32))
    w_out = np.asarray(inp["L0_attn_a_w_out"], np.float32)
    b_out = np.asarray(inp["L0_attn_a_b_out"], np.float32)
    Wq = np.asarray(inp["L0_attn_b_Wq"], np.float32)
    bq = np.asarray(inp["L0_attn_b_bq"], np.float32)
    Wk = np.asarray(inp["L0_attn_b_Wk"], np.float32)
    bk = np.asarray(inp["L0_attn_b_bk"], np.float32)
    Wv = np.ascontiguousarray(np.asarray(inp["L0_attn_b_Wv"], np.float32))
    bv = np.asarray(inp["L0_attn_b_bv"], np.float32)
    Wo = np.asarray(inp["L0_attn_b_Wo"], np.float32)
    bo = np.asarray(inp["L0_attn_b_bo"], np.float32)
    sim = np.asarray(inp["L0_ffn_sim_matrix"], np.float32)
    gates = np.asarray(inp["L0_ffn_gates"], np.float32)
    q_proj = np.asarray(inp["L0_ffn_q_proj"], np.float32)
    k_proj = np.asarray(inp["L0_ffn_k_proj"], np.float32)
    v_proj = np.asarray(inp["L0_ffn_v_proj"], np.float32)
    o_proj = np.asarray(inp["L0_ffn_o_proj"], np.float32)

    wcat = np.ascontiguousarray(np.concatenate(
        [np.concatenate([q_proj[e], k_proj[e], v_proj[e]], axis=1) for e in range(16)], axis=1))
    ocat = o_proj.reshape(16 * 64, DM)
    simc = np.ascontiguousarray(sim.reshape(NC8, 128, 16).transpose(1, 0, 2).reshape(128, NC8 * 16))
    cb = np.ascontiguousarray(np.concatenate(
        [b_out.reshape(NC8, 128).T, bq.reshape(NC8, 128).T,
         bk.reshape(NC8, 128).T, bo.reshape(NC8, 128).T], axis=1).astype(np.float32))
    ident, maska, tri, sel = _host_consts()

    nc = build_program()
    common = {
        "wqkv_qk_mb": _hi16(_mb_layout(w_qkv[:, 0:2048], 16)),
        "wqkv_v": np.ascontiguousarray(w_qkv[:, 2048:3072]),
        "projm": projm,
        "wout_mb": _mb_layout(w_out, 8),
        "wq_mb": _hi16(_mb_layout(Wq, 8)), "wk_mb": _hi16(_mb_layout(Wk, 8)),
        "Wv": Wv, "wo_mb": _mb_layout(Wo, 8),
        "Wcat": wcat, "ocat_mb": _mb_layout(ocat, 8),
        "simc": simc, "gatesr": gates.reshape(1, 16),
        "biascols": cb, "bvrow": bv.reshape(1, DM),
        "identc": ident, "maska": maska, "tric": tri, "selc": sel,
        "onesr": np.ones((128, 2), np.float32),
        "ones1": np.ones((1, 128), np.float32),
        "ones1r": np.ones((1, 128), np.float32),
        "idf1": np.ones((1, 1), np.float32),
    }
    in_maps = []
    for b in range(8):
        m = dict(common)
        m["x_fm"] = np.ascontiguousarray(x[b].T)
        in_maps.append(m)
    res = run_bass_kernel_spmd(nc, in_maps, core_ids=list(range(8)))
    if DBG_RAW:
        return res.results
    out = np.stack([np.ascontiguousarray(res.results[b]["x_out"].T) for b in range(8)])
    return out.astype(np.float32)


if __name__ == "__main__":
    build_program()
    print("build ok")


# version 37
# speedup vs baseline: 1.0063x; 1.0063x over previous
import numpy as np
import ml_dtypes
import concourse.bass as bass
import concourse.mybir as mybir
from concourse.bass_utils import run_bass_kernel_spmd
from contextlib import ExitStack

F32 = mybir.dt.float32
F32R = mybir.dt.float32r
BF16 = mybir.dt.bfloat16
AF = mybir.ActivationFunctionType
OP = mybir.AluOpType
AX = mybir.AxisListType

T = 2048
DM = 1024
NC8 = 8
NT = 16
PARTS = 4
DBG_WANT = ()
DBG_RAW = False
B_PHASE = 3

ENGINES = ("pe", "act", "ve", "pool", "sync")


def later(*miles):
    best = None
    for m in miles:
        if m is None:
            continue
        if best is None or m[1] > best[1]:
            assert best is None or m[0] == best[0]
            best = m
    return best


class KB:
    def __init__(self, nc):
        self.nc = nc
        self.q = {e: [] for e in ENGINES}
        self.cnt = {e: 0 for e in ENGINES}
        self.cnt["sd"] = 0
        self.sems = {}
        self.lastw = {}

    N_DSEM = 8

    def make_sems(self, stack):
        for s in ("pe", "act", "ve", "pool", "sync", "sd"):
            self.sems[s] = stack.enter_context(self.nc.semaphore("sem_" + s))
        for i in range(self.N_DSEM):
            self.sems[f"d{i}"] = stack.enter_context(self.nc.semaphore(f"sem_d{i}"))
        self.dma_i = 0
        self.dcnt = [0] * self.N_DSEM

    def wait(self, e, *miles):
        for m in miles:
            if m is None:
                continue
            s, v = m
            if v <= 0 or s == e:
                continue
            key = (e, s)
            if self.lastw.get(key, 0) >= v:
                continue
            self.lastw[key] = v
            sem = self.sems[s]
            self.q[e].append(lambda eng, sem=sem, v=v: eng.wait_ge(sem, v))

    def run(self, e, f, inc=True):
        if inc:
            sem = self.sems[e]
            self.cnt[e] += 1

            def thunk(eng, f=f, sem=sem):
                last = f(eng)
                assert last is not None
                last.then_inc(sem, 1)

            self.q[e].append(thunk)
            return (e, self.cnt[e])
        self.q[e].append(lambda eng, f=f: f(eng))
        return None

    def dma(self, out, in_, **kw):
        si = self.dma_i % self.N_DSEM
        self.dma_i += 1
        sem = self.sems[f"d{si}"]
        prev = self.dcnt[si]
        if prev > 0:
            self.q["sync"].append(lambda eng, sem=sem, v=prev: eng.wait_ge(sem, v))
        self.dcnt[si] = prev + 16
        self.q["sync"].append(
            lambda eng, o=out, i=in_, kw=kw, sem=sem: eng.dma_start(out=o, in_=i, **kw).then_inc(sem, 16))
        return (f"d{si}", self.dcnt[si])

    def mile(self, e):
        return (e, self.cnt[e])

    def self_sync(self, e):
        sem = self.sems[e]
        v = self.cnt[e]
        if v > 0:
            self.q[e].append(lambda eng, sem=sem, v=v: eng.wait_ge(sem, v))

    def fence(self):
        prods = [(p, self.cnt[p]) for p in ("pe", "act", "ve", "pool")]
        prods += [(f"d{i}", self.dcnt[i]) for i in range(self.N_DSEM)]
        for e in ENGINES:
            self.wait(e, *prods)


def bview(ap, inner):
    return ap.unsqueeze(2).broadcast_to([ap.shape[0], ap.shape[1], inner])


def pslots(ps, pitch, n, width, off=0):
    return bass.AP(tensor=ps.tensor, offset=ps.offset + off,
                   ap=[list(ps.ap[0]), [pitch, n], [1, width]])


def build_program(parts=None):
    if parts is None:
        parts = PARTS
    nc = bass.Bass("TRN2", target_bir_lowering=False)

    d_x = nc.dram_tensor("x_fm", [DM, T], F32R, kind="ExternalInput").ap()
    d_out = nc.dram_tensor("x_out", [DM, T], F32, kind="ExternalOutput").ap()
    d_qkmb = nc.dram_tensor("wqkv_qk_mb", [16 * 128, DM], F32, kind="ExternalInput").ap()
    d_wqkv_v = nc.dram_tensor("wqkv_v", [DM, DM], F32, kind="ExternalInput").ap()
    d_proj = nc.dram_tensor("projm", [128, 64], F32, kind="ExternalInput").ap()
    d_woutmb = nc.dram_tensor("wout_mb", [8 * 128, DM], F32, kind="ExternalInput").ap()
    d_wqmb = nc.dram_tensor("wq_mb", [8 * 128, DM], F32, kind="ExternalInput").ap()
    d_wkmb = nc.dram_tensor("wk_mb", [8 * 128, DM], F32, kind="ExternalInput").ap()
    d_wv = nc.dram_tensor("Wv", [DM, DM], F32, kind="ExternalInput").ap()
    d_womb = nc.dram_tensor("wo_mb", [8 * 128, DM], F32, kind="ExternalInput").ap()
    d_wcat = nc.dram_tensor("Wcat", [DM, 3 * DM], F32, kind="ExternalInput").ap()
    d_ocatmb = nc.dram_tensor("ocat_mb", [8 * 128, DM], F32, kind="ExternalInput").ap()
    d_sim = nc.dram_tensor("simc", [128, NC8 * 16], F32, kind="ExternalInput").ap()
    d_gates = nc.dram_tensor("gatesr", [1, 16], F32, kind="ExternalInput").ap()
    d_cb = nc.dram_tensor("biascols", [128, 4 * NC8], F32, kind="ExternalInput").ap()
    d_bv = nc.dram_tensor("bvrow", [1, DM], F32, kind="ExternalInput").ap()
    d_ident = nc.dram_tensor("identc", [128, 128], F32, kind="ExternalInput").ap()
    d_maska = nc.dram_tensor("maska", [128, 3 * 384], F32, kind="ExternalInput").ap()
    d_tri = nc.dram_tensor("tric", [128, 128], F32, kind="ExternalInput").ap()
    d_sel = nc.dram_tensor("selc", [16, NC8 * 128], F32, kind="ExternalInput").ap()
    d_onesr = nc.dram_tensor("onesr", [128, 2], F32R, kind="ExternalInput").ap()
    d_ones1 = nc.dram_tensor("ones1", [1, 128], F32, kind="ExternalInput").ap()
    d_ones1r = nc.dram_tensor("ones1r", [1, 128], F32R, kind="ExternalInput").ap()
    d_idf1 = nc.dram_tensor("idf1", [1, 1], F32, kind="ExternalInput").ap()

    stack = ExitStack()
    k = KB(nc)

    with stack:
        k.make_sems(stack)

        _names = {}

        def _uniq(name):
            n = _names.get(name, 0)
            _names[name] = n + 1
            return name if n == 0 else f"{name}_u{n}"

        def sb(name, shape, dt, st=stack):
            return st.enter_context(nc.sbuf_tensor(_uniq(name), shape, dt)).ap()

        DBGSC = [None]
        if DBG_WANT:
            DBGSC[0] = sb("dbg_scratch", [128, 2048], F32)

        def dbg(name, ap):
            if name not in DBG_WANT:
                return
            P, FR = ap.shape[0], ap.shape[1]
            dd = nc.dram_tensor("dbg_" + name, [P, FR], F32, kind="ExternalOutput").ap()
            k.fence()
            for lo in range(0, FR, 2048):
                w = min(2048, FR - lo)
                mcc = k.run("ve", lambda v, lo=lo, w=w, ap=ap, P=P: v.tensor_copy(
                    DBGSC[0][0:P, 0:w], ap[:, lo:lo + w]))
                k.wait("sync", mcc)
                md = k.dma(dd[:, lo:lo + w], DBGSC[0][0:P, 0:w])
                k.wait("ve", md)
            for e in ENGINES:
                for i in range(k.N_DSEM):
                    k.wait(e, (f"d{i}", k.dcnt[i]))

        X = sb("X", [128, NC8 * T], F32R)
        XF = X.bitcast(F32)
        IDENT = sb("IDENT", [128, 128], BF16)
        IDENTF = sb("IDENTF", [128, 128], F32)
        TRI = sb("TRI", [128, 128], BF16)
        TRIF = sb("TRIF", [128, 128], F32)
        ONESR = sb("ONESR", [128, 2], F32R)
        ONES1 = sb("ONES1", [1, 128], F32)
        ONES1R = sb("ONES1R", [1, 128], F32R)
        IDF1 = sb("IDF1", [1, 1], F32)
        CB = sb("CB", [128, 4 * NC8], F32)
        SIMC = sb("SIMC", [128, NC8 * 16], F32)
        PROJB = sb("PROJB", [128, 64], BF16)
        SGB = sb("SGB", [128, 16], F32)
        SNB = sb("SNB", [128, 16], F32)
        BVB = sb("BVB", [128, DM], F32)
        GNRC = sb("GNRC", [128, 16], F32)
        EPS1 = sb("EPS1", [1, 1], F32)
        EPS8 = sb("EPS8", [128, 1], F32)

        m_in = [k.dma(X[:, kc * T:(kc + 1) * T], d_x[kc * 128:(kc + 1) * 128, :]) for kc in range(NC8)]
        m_c = [k.dma(IDENTF, d_ident), k.dma(TRIF, d_tri),
               k.dma(ONESR, d_onesr), k.dma(ONES1, d_ones1),
               k.dma(ONES1R, d_ones1r), k.dma(IDF1, d_idf1), k.dma(CB, d_cb),
               k.dma(SIMC, d_sim)]
        with ExitStack() as ps_:
            pb = lambda n, s, d=F32: ps_.enter_context(nc.psum_tensor(n, s, d)).ap()
            PP = pb("prep_ps", [128, 512])
            PROJ32 = ps_.enter_context(nc.sbuf_tensor("PROJ32", [128, 64], F32)).ap()
            BVROW = ps_.enter_context(nc.sbuf_tensor("BVROW", [1, DM], F32)).ap()
            GROW = ps_.enter_context(nc.sbuf_tensor("GROW", [1, 16], F32)).ap()
            SSQ = ps_.enter_context(nc.sbuf_tensor("SSQ", [128, NC8 * 16], F32)).ap()
            m_pj = k.dma(PROJ32, d_proj)
            m_bv = k.dma(BVROW, d_bv)
            m_g = k.dma(GROW, d_gates)
            k.wait("ve", m_pj, m_c[0], m_c[1])
            k.run("ve", lambda v: v.tensor_copy(IDENT, IDENTF), inc=False)
            k.run("ve", lambda v: v.tensor_copy(TRI, TRIF), inc=False)
            k.run("ve", lambda v: v.tensor_copy(PROJB, PROJ32), inc=False)
            k.run("ve", lambda v: v.memset(EPS1, 1e-5), inc=False)
            k.run("ve", lambda v: v.memset(EPS8, 1e-8), inc=False)
            k.wait("act", m_g)
            m1 = k.run("act", lambda s: s.activation(GROW, GROW, AF.Sigmoid))
            k.wait("pe", m1, m_c[3])
            m2 = k.run("pe", lambda t: t.matmul(PP[:, 0:16], ONES1, GROW, start=True, stop=True,
                                                skip_group_check=True))
            k.wait("ve", m2)
            m3 = k.run("ve", lambda v: v.tensor_copy(SGB, PP[:, 0:16]))
            k.wait("pool", m_c[7])
            m4 = k.run("pool", lambda g: g.tensor_mul(SSQ, SIMC, SIMC))
            k.wait("pe", m3, m4, m_c[2])

            def f_sn(t):
                last = None
                for kc in range(NC8):
                    last = t.matmul(PP[0:1, 64:80], ONESR.bitcast(F32)[:, 0:1],
                                    SSQ[:, kc * 16:(kc + 1) * 16],
                                    start=(kc == 0), stop=(kc == NC8 - 1), skip_group_check=True)
                return last
            m5 = k.run("pe", f_sn)
            k.wait("act", m5)
            m6 = k.run("act", lambda s: s.activation(GROW, PP[0:1, 64:80], AF.Sqrt))
            k.wait("ve", m6)

            GROW2 = ps_.enter_context(nc.sbuf_tensor("GROW2", [1, 16], F32)).ap()

            k.run("ve", lambda v: v.tensor_scalar_max(GROW, GROW, 1e-12))
            k.self_sync("ve")
            m7 = k.run("ve", lambda v: v.reciprocal(GROW2, GROW))
            k.wait("pe", m7)
            m8 = k.run("pe", lambda t: t.matmul(PP[:, 32:48], ONES1, GROW2, start=True, stop=True,
                                                skip_group_check=True))
            k.wait("ve", m8)
            m9 = k.run("ve", lambda v: v.tensor_copy(SNB, PP[:, 32:48]))
            last_m = m9
            for hf in range(2):
                k.wait("pe", last_m, m_bv)
                ma = k.run("pe", lambda t, hf=hf: t.matmul(PP[:, 0:512], ONES1,
                                                           BVROW[:, hf * 512:(hf + 1) * 512],
                                                           start=True, stop=True, skip_group_check=True))
                k.wait("ve", ma)
                last_m = k.run("ve", lambda v, hf=hf: v.tensor_copy(BVB[:, hf * 512:(hf + 1) * 512],
                                                                    PP[:, 0:512]))
            k.wait("ve", *m_in, *m_c)
            k.run("ve", lambda v: v.engine_nop())
        k.fence()
        dbg("sgb", SGB)
        dbg("snb", SNB)
        dbg("c_ident", IDENT)
        dbg("c_tri", TRI)
        dbg("x0", X)

        def make_ln_scope(tag, ls, gating=False, nbuf=2):
            d = {}
            d["SQ"] = [sb(f"{tag}_sq{i}", [128, 512], F32R, ls) for i in range(nbuf)]
            d["TT"] = [sb(f"{tag}_tt0", [128, 512], F32, ls)]
            if gating:
                d["XSC"] = [sb(f"{tag}_xsc{i}", [128, 512], F32, ls) for i in range(2)]
            d["MR"] = sb(f"{tag}_mr", [1, 512], F32, ls)
            d["VRr"] = sb(f"{tag}_vr", [1, 512], F32, ls)
            d["RS"] = sb(f"{tag}_rs", [1, 512], F32R, ls)
            d["NM"] = sb(f"{tag}_nm", [1, 512], F32R, ls)
            if gating:
                d["G1"] = sb(f"{tag}_g1", [1, 512], F32, ls)
            stat = ls.enter_context(nc.psum_tensor(_uniq(f"{tag}_stat"), [128, 512], F32)).ap()
            stat2 = ls.enter_context(nc.psum_tensor(_uniq(f"{tag}_stat2"), [128, 512], F32)).ap()
            d["ps_s0"] = stat[0:1, :]
            d["ps_s1"] = stat2[0:1, :]
            d["ps_s0mm"] = stat[0:2, :]
            d["ps_s1mm"] = stat2[0:2, :]
            d["ps_ab"] = ls.enter_context(nc.psum_tensor(_uniq(f"{tag}_ab"), [128, 512], F32)).ap()
            d["ps_am"] = ls.enter_context(nc.psum_tensor(_uniq(f"{tag}_am"), [128, 512], F32)).ap()
            if gating:
                d["ps_gn"] = ls.enter_context(nc.psum_tensor(_uniq(f"{tag}_gn"), [128, 512], F32)).ap()[:, 0:4]
            d["dbg_rows"] = True
            d["sq_m"] = [None] * nbuf
            d["tt_m"] = [None]
            d["xsc_m"] = [None, None]
            d["row_m"] = None
            d["bc_rd"] = None
            d["stat_rd"] = None
            return d

        def ln_tb(d, tb, xn_at, gating_cb=None, gnrc=False):
            ts = tb * 512
            m_st = None
            NB = len(d["SQ"])
            for kc in range(NC8):
                b = kc % NB
                sq_eng = "pool" if kc % 2 == 0 else "act"
                k.wait(sq_eng, d["sq_m"][b])
                if sq_eng == "pool":
                    def f_sq(g, kc=kc, b=b, ts=ts):
                        return g.tensor_mul(d["SQ"][b], X[:, kc * T + ts: kc * T + ts + 512],
                                            X[:, kc * T + ts: kc * T + ts + 512])
                else:
                    def f_sq(g, kc=kc, b=b, ts=ts):
                        return g.activation(d["SQ"][b], X[:, kc * T + ts: kc * T + ts + 512],
                                            AF.Square)
                m_sq = k.run(sq_eng, f_sq)
                k.wait("pe", m_sq, d["stat_rd"] if kc == 0 else None)

                def f_stats(t, kc=kc, b=b, ts=ts):
                    t.matmul(d["ps_s0mm"], ONESR, X[:, kc * T + ts: kc * T + ts + 512],
                             start=(kc == 0), stop=(kc == NC8 - 1), skip_group_check=True)
                    return t.matmul(d["ps_s1mm"], ONESR, d["SQ"][b],
                                    start=(kc == 0), stop=(kc == NC8 - 1), skip_group_check=True)
                m_st = k.run("pe", f_stats)
                d["sq_m"][b] = m_st
            k.wait("ve", m_st)

            def f_rows(v):
                v.tensor_single_scalar(d["MR"], d["ps_s0"], 1.0 / DM, op=OP.mult)
                v.tensor_mul(d["VRr"], d["MR"], d["MR"])
                return v.scalar_tensor_tensor(d["VRr"], d["ps_s1"], 1.0 / DM, d["VRr"],
                                              op0=OP.mult, op1=OP.subtract)
            m_rows = k.run("ve", f_rows)
            d["stat_rd"] = m_rows
            k.wait("act", m_rows)

            def f_rstd(s):
                s.activation(d["VRr"], d["VRr"], AF.Ln, bias=EPS1)
                return s.activation(d["RS"], d["VRr"], AF.Exp, scale=-0.5)
            m_rstd = k.run("act", f_rstd)
            k.wait("ve", m_rstd)
            m_nm = k.run("ve", lambda v: v.scalar_tensor_tensor(
                d["NM"], d["MR"], -1.0, d["RS"].bitcast(F32), op0=OP.mult, op1=OP.mult))
            d["row_m"] = m_nm
            if tb == 0 and d.get("dbg_rows"):
                dbg("mr", d["MR"]); dbg("vr", d["VRr"])
                dbg("rs", d["RS"].bitcast(F32)); dbg("nm", d["NM"].bitcast(F32))

            if gnrc:
                k.wait("act", m_rstd)
                mg1 = k.run("act", lambda s: s.activation(d["G1"], d["VRr"], AF.Exp))
                k.wait("ve", mg1)
                mg2 = k.run("ve", lambda v: v.tensor_single_scalar(d["G1"], d["G1"], 1e-5, op=OP.subtract))
                k.wait("act", mg2)
                mg3 = k.run("act", lambda s: s.activation(d["G1"], d["G1"], AF.Ln, scale=float(DM)))
                k.wait("ve", mg3)
                mg4 = k.run("ve", lambda v: v.tensor_sub(d["G1"], d["G1"], d["VRr"]))
                k.wait("act", mg4)
                mg5 = k.run("act", lambda s: s.activation(d["G1"], d["G1"], AF.Exp, scale=-0.5))
                tpm = None
                for tl in range(4):
                    k.wait("pe", mg5, tpm, d["row_m"] if tl == 0 else None)
                    mtp = k.run("pe", lambda t, tl=tl: t.transpose(
                        d["ps_gn"][:, tl:tl + 1], d["G1"][:, tl * 128:(tl + 1) * 128], IDF1))
                    tpm = mtp
                k.wait("ve", tpm)
                d["row_m"] = k.run("ve", lambda v, tb=tb: v.tensor_copy(
                    GNRC[:, tb * 4:(tb + 1) * 4], d["ps_gn"]))

            k.wait("pe", m_nm, d["bc_rd"])

            def f_bc(t):
                t.matmul(d["ps_ab"], ONES1R, d["RS"], start=True, stop=True,
                         skip_group_check=True)
                return t.matmul(d["ps_am"], ONES1R, d["NM"], start=True, stop=True,
                                skip_group_check=True)
            m_bc = k.run("pe", f_bc)

            last_ap = None
            NBT = len(d["TT"])
            for kc in range(NC8):
                b = kc % NBT
                k.wait("ve", m_bc, d["tt_m"][b])

                def f_a1(v, kc=kc, b=b, ts=ts):
                    return v.tensor_tensor(d["TT"][b], X[:, kc * T + ts: kc * T + ts + 512],
                                           d["ps_ab"], op=OP.mult)
                d["tt_m"][b] = k.run("ve", f_a1)
                if gating_cb is None:
                    def f_a2(v, kc=kc, b=b):
                        return v.tensor_tensor(xn_at(kc), d["TT"][b], d["ps_am"], op=OP.add)
                    d["tt_m"][b] = k.run("ve", f_a2)
                else:
                    bx = kc % 2
                    k.wait("ve", d["xsc_m"][bx])

                    def f_a2g(v, kc=kc, b=b, bx=bx):
                        return v.tensor_tensor(d["XSC"][bx], d["TT"][b], d["ps_am"], op=OP.add)
                    mx = k.run("ve", f_a2g)
                    d["tt_m"][b] = mx

                    def f_a3g(v, kc=kc, b=b, bx=bx):
                        return v.tensor_copy(xn_at(kc), d["XSC"][bx])
                    k.run("ve", f_a3g, inc=False)
                    d["xsc_m"][bx] = gating_cb(kc, d["XSC"][bx], mx)
                last_ap = d["tt_m"][b]
            d["bc_rd"] = later(*d["tt_m"])
            return last_ap

        def attn_a():
          with ExitStack() as a_scope:
              XN = sb("XN_a", [128, NC8 * T], BF16, a_scope)
              WV32 = [sb(f"a_wv{i}", [128, 512], F32, a_scope) for i in range(2)]
              WVB = sb("a_wvb", [128, NC8 * DM], BF16, a_scope)
              wvm = [None, None]
              cvm = []
              wv_dma = []

              def issue_vdma(kx):
                  kc, hf = kx // 2, kx % 2
                  k.wait("sync", wvm[hf])
                  wv_dma.append(k.dma(WV32[hf], d_wqkv_v[kc * 128:(kc + 1) * 128,
                                                         hf * 512:(hf + 1) * 512]))

              def emit_vcast(kx):
                  kc, hf = kx // 2, kx % 2
                  k.wait("act", wv_dma[kx], wvm[hf])
                  wvm[hf] = k.run("act", lambda s, kc=kc, hf=hf: s.activation(
                      WVB[:, kc * DM + hf * 512: kc * DM + (hf + 1) * 512], WV32[hf], AF.Copy))
                  cvm.append(wvm[hf])

              issue_vdma(0)
              issue_vdma(1)
              with ExitStack() as ls:
                  lnd = make_ln_scope("lna", ls, nbuf=4)
                  for tb in range(4):
                      ln_tb(lnd, tb, lambda kc, tb=tb: XN[:, kc * T + tb * 512: kc * T + (tb + 1) * 512])
                      for i_ in range(4):
                          kx = tb * 4 + i_
                          emit_vcast(kx)
                          if kx + 2 < 16:
                              issue_vdma(kx + 2)
                  k.fence()
              if "xnf" in DBG_WANT:
                  XNFDBG = sb("xnfdbg", [128, 2048], F32, a_scope)
                  dd = nc.dram_tensor("dbg_xnf", [128, 8 * 2048], F32, kind="ExternalOutput").ap()
                  k.fence()
                  for kc in range(NC8):
                      mcc = k.run("ve", lambda v, kc=kc: v.tensor_copy(XNFDBG, XN[:, kc * T:(kc + 1) * T]))
                      k.wait("sync", mcc)
                      md = k.dma(dd[:, kc * 2048:(kc + 1) * 2048] if False else dd.rearrange("p (c t) -> p c t", t=2048)[:, kc, :], XNFDBG)
                      for e in ENGINES:
                          k.wait(e, md)
              dbg("xn", XN)
              HW = 8
              def _half(H):
                  kt0 = max(0, 8 * H - 1)
                  kt1 = min(NT, 8 * H + 9)
                  nkt = kt1 - kt0
                  LT = nkt * 128
                  with ExitStack() as hs:
                      VAUG = sb("a_vaug", [128, nkt * HW * 129], BF16, hs)
                      PHI = sb("a_phi", [128, HW * LT], BF16, hs)
                      with ExitStack() as vs:
                          ps_v = vs.enter_context(nc.psum_tensor(_uniq("a_psv"), [128, 1024], F32)).ap()
                          mset = k.run("ve", lambda v: v.memset(VAUG, 1.0))
                          vv = None
                          for kti in range(nkt):
                              kt = kt0 + kti
                              k.wait("pe", vv, *cvm)

                              def f_vmm(t, kt=kt):
                                  last = None
                                  for hf in range(2):
                                      for kc in range(NC8):
                                          last = t.matmul(ps_v[:, hf * 512:(hf + 1) * 512],
                                                          XN[:, kc * T + kt * 128: kc * T + (kt + 1) * 128],
                                                          WVB[:, kc * DM + hf * 512: kc * DM + (hf + 1) * 512],
                                                          start=(kc == 0), stop=(kc == NC8 - 1),
                                                          skip_group_check=True)
                                  return last
                              mv = k.run("pe", f_vmm)
                              k.wait("ve", mv, mset)
                              dstv = VAUG.rearrange("p (kt h s) -> p kt h s", h=HW, s=129)[:, kti, :, 0:128]
                              vv = k.run("ve", lambda v, dstv=dstv: v.tensor_copy(
                                  dstv, ps_v.rearrange("p (h s) -> p h s", s=128)))
                          k.fence()
                          if H == 0:
                              dbg('vaug', VAUG)
                      with ExitStack() as qs:
                          QH = sb("a_qh", [128, HW * LT], BF16, qs)
                          KH = sb("a_kh", [128, HW * LT], BF16, qs)
                          WST = [sb(f"a_wst{i}", [128, DM], F32, qs) for i in range(2)]
                          WBB = [w.bitcast(BF16).rearrange("p (n two) -> p n two", two=2)[:, :, 1]
                                 for w in WST]
                          ps_qk = [qs.enter_context(nc.psum_tensor(_uniq(f"a_psqk{i}"), [128, 512], F32)).ap()
                                   for i in range(2)]
                          blks = [(0, 512), (512, 512), (1024, 128)]
                          wrd = [None, None]
                          evm = [None, None]
                          for mb in range(16):
                              b = mb % 2
                              dst = QH if mb < 8 else KH
                              hh = mb % 8
                              k.wait("sync", wrd[b])
                              md = k.dma(WST[b], d_qkmb[mb * 128:(mb + 1) * 128, :])
                              for bi, (lo, ln_) in enumerate(blks):
                                  pbk = bi % 2
                                  k.wait("pe", md, evm[pbk])

                                  def f_mm(t, b=b, lo=lo, ln_=ln_, pbk=pbk):
                                      last = None
                                      for kc in range(NC8):
                                          last = t.matmul(ps_qk[pbk][:, 0:ln_],
                                                          WBB[b][:, kc * 128:(kc + 1) * 128],
                                                          XN[:, kc * T + kt0 * 128 + lo: kc * T + kt0 * 128 + lo + ln_],
                                                          start=(kc == 0), stop=(kc == NC8 - 1),
                                                          skip_group_check=True)
                                      return last
                                  mm = k.run("pe", f_mm)
                                  wrd[b] = mm
                                  k.wait("ve", mm)
                                  evm[pbk] = k.run("ve", lambda v, dst=dst, hh=hh, lo=lo, ln_=ln_, pbk=pbk:
                                                   v.tensor_copy(dst[:, hh * LT + lo: hh * LT + lo + ln_],
                                                                 ps_qk[pbk][:, 0:ln_]))
                          ps_ph3 = qs.enter_context(nc.psum_tensor(_uniq("a_psph3"), [128, 512], F32)).ap()
                          PS3 = [ps_qk[0], ps_qk[1], ps_ph3]
                          E1 = [qs.enter_context(nc.psum_tensor(_uniq(f"a_e1p{i}"), [128, 512], F32)).ap()
                                for i in range(3)]
                          E2 = [sb(f"a_e2{i}", [128, 512], F32, qs) for i in range(2)]
                          e2_rd = [None, None]
                          phm = [evm[0], evm[1], None]
                          for hh in range(HW):
                              for bi, (lo, ln_) in enumerate(blks):
                                  pbk = (hh * 3 + bi) % 3
                                  k.wait("pe", phm[pbk],
                                         later(evm[0], evm[1]) if hh == 0 and bi == 0 else None)

                                  def f_phi(t, hh=hh, lo=lo, ln_=ln_, pbk=pbk):
                                      t.matmul(PS3[pbk][0:64, 0:ln_], PROJB,
                                               QH[:, hh * LT + lo: hh * LT + lo + ln_],
                                               start=True, stop=True, skip_group_check=True)
                                      return t.matmul(PS3[pbk][64:128, 0:ln_], PROJB,
                                                      KH[:, hh * LT + lo: hh * LT + lo + ln_],
                                                      start=True, stop=True, skip_group_check=True)
                                  mph = k.run("pe", f_phi)
                                  k.wait("ve", mph)
                                  me1 = k.run("ve", lambda v, pbk=pbk, ln_=ln_: v.tensor_single_scalar(
                                      E1[pbk][:, 0:ln_], PS3[pbk][:, 0:ln_], 0.0, op=OP.min))
                                  eb = (hh * 3 + bi) % 2
                                  k.wait("act", me1, e2_rd[eb])
                                  me2 = k.run("act", lambda s, pbk=pbk, eb=eb, ln_=ln_: s.activation(
                                      E2[eb][:, 0:ln_], E1[pbk][:, 0:ln_], AF.Exp))
                                  k.wait("ve", me2)
                                  phm[pbk] = k.run("ve", lambda v, hh=hh, lo=lo, ln_=ln_, pbk=pbk, eb=eb:
                                                   v.scalar_tensor_tensor(
                                                       PHI[:, hh * LT + lo: hh * LT + lo + ln_],
                                                       PS3[pbk][:, 0:ln_], 0.0, E2[eb][:, 0:ln_],
                                                       op0=OP.max, op1=OP.add))
                                  e2_rd[eb] = phm[pbk]
                          k.fence()
                          if H == 0:
                              dbg('qh', QH)
                              dbg('kh', KH)
                              dbg('phi', PHI)
                      with ExitStack() as ss:
                          MASKA = sb(f"maska{H}", [128, 3 * 384], F32, ss)
                          m_ma = k.dma(MASKA, d_maska)
                          k.wait("ve", m_ma)
                          PHIK0 = sb("a_phik0", [64, HW * LT], BF16, ss)
                          mpk = k.run("ve", lambda v: v.tensor_copy(PHIK0, PHI[64:128, :]))
                          k.wait("pe", mpk)
                          STM = [sb(f"a_stm{i}", [128, 384], BF16, ss) for i in range(2)]
                          ATG = [sb(f"a_atg{i}", [128, 3 * 128], BF16, ss) for i in range(2)]
                          Z3 = sb("a_z3", [128, 3], F32, ss)
                          Z3R = sb("a_z3r", [128, 3], F32, ss)
                          ATTNH = sb("a_attnh", [128, HW * 512], BF16, ss)
                          WOC32 = [sb(f"a_woc32{i}", [128, DM], F32, ss) for i in range(2)]
                          WOC = [sb(f"a_woc{i}", [128, DM], BF16, ss) for i in range(2)]
                          ps_st = [ss.enter_context(nc.psum_tensor(_uniq(f"a_psst{i}"), [128, 384], F32)).ap()
                                   for i in range(2)]
                          ps_av = [ss.enter_context(nc.psum_tensor(_uniq(f"a_psav{i}"), [128, 512], F32)).ap()
                                   for i in range(2)]
                          ps_tp = [ss.enter_context(nc.psum_tensor(_uniq(f"a_pstp{i}"), [128, 128], BF16)).ap()
                                   for i in range(2)]
                          ps_o = [ss.enter_context(nc.psum_tensor(_uniq(f"a_pso{i}"), [128, 512], F32)).ap()
                                  for i in range(2)]
                          mz0 = k.run("ve", lambda v: v.memset(ps_st[0], 0.0))
                          mz1 = k.run("ve", lambda v: v.memset(ps_st[1], 0.0))
                          k.wait("pe", mz0, mz1)
                          if H == 0:
                              def f_bias(v):
                                  last = None
                                  for kc in range(NC8):
                                      last = v.tensor_scalar_add(X[:, kc * T:(kc + 1) * T],
                                                                 X[:, kc * T:(kc + 1) * T],
                                                                 CB[:, kc:kc + 1])
                                  return last
                              k.run("ve", f_bias)
                          st_rd = [None, None]
                          woc_rd = [None, None]
                          stw = [None, None]
                          atg_rd = [None, None]
                          av_rd = [None, None]
                          tp_rd = [None, None]
                          attnh_rd = None
                          wocm = [None, None]
                          orm = [None, None]
                          for tg in range(2):
                              taus = [8 * H + tg * 4 + i for i in range(4)]
                              groups = [taus[0:3], taus[3:4]]
                              units = []
                              for hh in range(HW):
                                  for gi, grp in enumerate(groups):
                                      for si, tau in enumerate(grp):
                                          units.append((hh, gi, grp, si, tau))
                              NU = len(units)
                              mstm_l = [None] * NU
                              pend = [None]

                              def flush_pend(tg=tg):
                                  if pend[0] is None:
                                      return
                                  hh, gb, grp, msc2 = pend[0]
                                  pend[0] = None
                                  for si, tau in enumerate(grp):
                                      tpb = tau % 2
                                      k.wait("pe", msc2 if si == 0 else None, tp_rd[tpb])
                                      mtp = k.run("pe", lambda t, gb=gb, si=si, tpb=tpb: t.transpose(
                                          ps_tp[tpb], ATG[gb][:, si * 128:(si + 1) * 128], IDENT))
                                      atg_rd[gb] = mtp
                                      k.wait("ve", mtp, attnh_rd)
                                      tp_rd[tpb] = k.run("ve", lambda v, hh=hh, tau=tau, tpb=tpb, tg=tg:
                                                         v.tensor_copy(
                                                             ATTNH[:, hh * 512 + (tau - 8 * H - tg * 4) * 128:
                                                                   hh * 512 + (tau - 8 * H - tg * 4 + 1) * 128],
                                                             ps_tp[tpb]))

                              def emit_s(u):
                                  hh, gi, grp, si, tau = units[u]
                                  kappas = [x for x in (tau - 1, tau, tau + 1) if 0 <= x < NT]
                                  stb = tau % 2
                                  k.wait("pe", st_rd[stb], stw[stb])

                                  def f_st(t, hh=hh, tau=tau, kappas=kappas, stb=stb):
                                      last = None
                                      for j, kp in enumerate(kappas):
                                          js = kp - (tau - 1)
                                          last = t.matmul(
                                              ps_st[stb][:, js * 128:(js + 1) * 128],
                                              PHIK0[:, hh * LT + (kp - kt0) * 128: hh * LT + (kp - kt0 + 1) * 128],
                                              PHI[0:64, hh * LT + (tau - kt0) * 128: hh * LT + (tau - kt0 + 1) * 128],
                                              start=(j == 0), stop=(j == len(kappas) - 1),
                                              skip_group_check=True)
                                      return last
                                  mst = k.run("pe", f_st)
                                  mvar = 0 if tau == 0 else (2 if tau == NT - 1 else 1)
                                  k.wait("ve", mst, st_rd[stb])
                                  mstm_l[u] = k.run("ve", lambda v, stb=stb, mvar=mvar: v.tensor_tensor(
                                      STM[stb], ps_st[stb], MASKA[:, mvar * 384:(mvar + 1) * 384],
                                      op=OP.mult))
                                  stw[stb] = mstm_l[u]

                              emit_s(0)
                              for u in range(NU):
                                  hh, gi, grp, si, tau = units[u]
                                  if u + 1 < NU:
                                      emit_s(u + 1)
                                  gb = gi % 2
                                  stb = tau % 2
                                  kappas = [x for x in (tau - 1, tau, tau + 1) if 0 <= x < NT]
                                  k.wait("pe", mstm_l[u], av_rd[gb] if si == 0 else None)

                                  def f_av(t, kappas=kappas, tau=tau, stb=stb, si=si, gb=gb, hh=hh):
                                      last = None
                                      for j, kp in enumerate(kappas):
                                          last = t.matmul(
                                              ps_av[gb][:, si * 170: si * 170 + 129],
                                              STM[stb][:, (kp - (tau - 1)) * 128:(kp - (tau - 1) + 1) * 128],
                                              VAUG[:, (kp - kt0) * (HW * 129) + hh * 129:
                                                   (kp - kt0) * (HW * 129) + hh * 129 + 129],
                                              start=(si == 0 and j == 0), stop=False,
                                              skip_group_check=True)
                                      return last
                                  avm = k.run("pe", f_av)
                                  st_rd[stb] = avm
                                  if si == len(grp) - 1:
                                      flush_pend()
                                      ng = len(grp)
                                      k.wait("act", avm, atg_rd[gb])
                                      k.run("act", lambda s, gb=gb, ng=ng: s.activation(
                                          Z3[:, 0:ng], pslots(ps_av[gb], 170, ng, 1, 128),
                                          AF.Ln, bias=EPS8))
                                      mzr = k.run("act", lambda s, ng=ng: s.activation(
                                          Z3R[:, 0:ng], Z3[:, 0:ng], AF.Exp, scale=-1.0))
                                      k.wait("ve", avm, mzr)

                                      def f_sc(v, gb=gb, ng=ng):
                                          return v.tensor_tensor(
                                              ATG[gb].rearrange("p (s q) -> p s q", q=128)[:, 0:ng, :],
                                              pslots(ps_av[gb], 170, ng, 128),
                                              bview(Z3R[:, 0:ng], 128), op=OP.mult)
                                      msc2 = k.run("ve", f_sc)
                                      av_rd[gb] = msc2
                                      pend[0] = (hh, gb, grp, msc2)
                              flush_pend()
                              if H == 0 and tg == 0:
                                  dbg('attnh', ATTNH)
                              tglob = 8 * H * 128 + tg * 512
                              last_evs = []
                              for c in range(NC8):
                                  b = c % 2
                                  k.wait("sync", wocm[b])
                                  mdw = k.dma(WOC32[b], d_woutmb[c * 128:(c + 1) * 128, :])
                                  k.wait("act", mdw, wocm[b], woc_rd[b])
                                  wocm[b] = k.run("act", lambda s, b=b: s.activation(WOC[b], WOC32[b], AF.Copy))
                                  k.wait("pe", wocm[b], orm[b], tp_rd[0], tp_rd[1])

                                  def f_omm(t, c=c, b=b):
                                      last = None
                                      for kc in range(NC8):
                                          last = t.matmul(ps_o[b], WOC[b][:, kc * 128:(kc + 1) * 128],
                                                          ATTNH[:, kc * 512:(kc + 1) * 512],
                                                          start=(kc == 0), stop=(kc == NC8 - 1),
                                                          skip_group_check=True)
                                      return last
                                  mo = k.run("pe", f_omm)
                                  woc_rd[b] = mo
                                  k.wait("ve", mo)
                                  orm[b] = k.run("ve", lambda v, c=c, tglob=tglob, b=b: v.tensor_tensor(
                                      X[:, c * T + tglob: c * T + tglob + 512],
                                      X[:, c * T + tglob: c * T + tglob + 512], ps_o[b], op=OP.add))
                                  last_evs.append(orm[b])
                              attnh_rd = None
                              k.wait("ve", *last_evs)
                          k.fence()

              for H in range(2):
                  _half(H)

        def ffn(app):
            with ExitStack() as ls:
                XNF = sb(f"f{app}_xn", [128, NC8 * T], BF16, ls)
                RW = sb(f"f{app}_rw", [128, 256], F32, ls)
                RWB16 = sb(f"f{app}_rwb", [128, 256], BF16, ls)
                RWT = sb(f"f{app}_rwt", [16, T], BF16, ls)
                QKFQ = sb(f"f{app}_qkfq", [64, T], BF16, ls)
                QKFK = sb(f"f{app}_qkfk", [64, T], BF16, ls)
                VAUGF = sb(f"f{app}_vaugf", [128, NT * 65], BF16, ls)

                def v3(ap, e=16):
                    return ap.rearrange("p (s e) -> p s e", e=e)

                wc_cms = [nc.sbuf_tensor(_uniq(f"f{app}_wc{i}"), [128, 1536], F32) for i in range(2)]
                WC32 = [cm.__enter__().ap() for cm in wc_cms]
                wcb_cm = nc.sbuf_tensor(_uniq(f"f{app}_wcb"), [128, NC8 * 3 * DM], BF16)
                WCB = wcb_cm.__enter__().ap()
                wcm = [None, None]
                wc_all = []
                wc_dma = []
                lnm = {}

                def issue_wdma(kidx):
                    kc, hf = kidx // 2, kidx % 2
                    k.wait("sync", wcm[hf])
                    wc_dma.append(k.dma(WC32[hf], d_wcat[kc * 128:(kc + 1) * 128,
                                                         hf * 1536:(hf + 1) * 1536]))

                def emit_wcast(kidx):
                    kc, hf = kidx // 2, kidx % 2
                    k.wait("act", wc_dma[kidx], wcm[hf])
                    wcm[hf] = k.run("act", lambda s, kc=kc, hf=hf: s.activation(
                        WCB[:, kc * 3 * DM + hf * 1536: kc * 3 * DM + (hf + 1) * 1536],
                        WC32[hf], AF.Copy))
                    wc_all.append(wcm[hf])

                issue_wdma(0)
                issue_wdma(1)

                with ExitStack() as pa:
                    lnd = make_ln_scope(f"lnf{app}", pa, gating=True, nbuf=4)
                    LG = sb(f"f{app}_lg", [128, 64], F32, pa)
                    SC1 = sb(f"f{app}_sc1", [128, 64], F32, pa)
                    SC2 = sb(f"f{app}_sc2", [128, 64], F32, pa)
                    SC3 = sb(f"f{app}_sc3", [128, 64], F32, pa)
                    M1 = sb(f"f{app}_m1", [128, 4], F32, pa)
                    M2 = sb(f"f{app}_m2", [128, 4], F32, pa)
                    A1 = sb(f"f{app}_a1", [128, 4], F32, pa)
                    NA = sb(f"f{app}_na", [128, 4], F32, pa)
                    ps_gps = [pa.enter_context(nc.psum_tensor(_uniq(f"f{app}_gps{i}"), [128, 512], F32)).ap()[:, 0:64]
                              for i in range(2)]
                    gate_first = {}
                    gps_rd = [None, None]
                    for tb in range(4):
                        gp = tb % 2
                        gate_first[tb] = [True]

                        def gating_cb(kc, xsc, mx, tb=tb, gp=gp):
                            k.wait("pe", mx, gps_rd[gp] if kc == 0 else None)

                            def f_g(t, kc=kc, xsc=xsc, tb=tb, gp=gp):
                                last = None
                                for tl in range(4):
                                    last = t.matmul(ps_gps[gp][:, tl * 16:(tl + 1) * 16],
                                                    xsc[:, tl * 128:(tl + 1) * 128],
                                                    SIMC[:, kc * 16:(kc + 1) * 16],
                                                    start=gate_first[tb][0], stop=False,
                                                    skip_group_check=True)
                                    gate_first[tb][0] = False
                                return last
                            return k.run("pe", f_g)

                        ln_tb(lnd, tb, lambda kc, tb=tb: XNF[:, kc * T + tb * 512: kc * T + (tb + 1) * 512],
                              gating_cb=gating_cb, gnrc=True)
                        m_gmm = k.mile("pe")
                        if app == 0:
                            dbg(f"fgps{tb}", ps_gps[gp])
                        k.wait("ve", m_gmm)

                        def vstep(f, sync=True):
                            m = k.run("ve", f)
                            if sync:
                                k.self_sync("ve")
                            return m

                        gv = ps_gps[gp].rearrange("p (s e) -> p s e", e=16)
                        vstep(lambda v, gv=gv: v.tensor_tensor(v3(LG), gv, SNB.unsqueeze(1).broadcast_to([128, 4, 16]), op=OP.mult))
                        if tb == 0 and app == 0:
                            dbg("g_s1o", LG)
                        vstep(lambda v, tb=tb: v.tensor_tensor(v3(LG), v3(LG), bview(GNRC[:, tb * 4:(tb + 1) * 4], 16), op=OP.mult))
                        if tb == 0 and app == 0:
                            dbg("g_s2o", LG)
                        vstep(lambda v: v.tensor_tensor(v3(LG), v3(LG), SGB.unsqueeze(1).broadcast_to([128, 4, 16]), op=OP.subtract))
                        if tb == 0 and app == 0:
                            dbg("g_logits", LG)
                        vstep(lambda v: v.reduce_max(M1, v3(LG), axis=AX.X), sync=True)
                        vstep(lambda v: v.tensor_single_scalar(SC1, LG, 0.0, op=OP.is_gt))
                        vstep(lambda v: v.tensor_tensor(v3(SC2), v3(LG), bview(M1, 16), op=OP.is_ge))
                        vstep(lambda v: v.scalar_tensor_tensor(v3(SC3), v3(SC2), -1e9, v3(LG), op0=OP.mult, op1=OP.add))
                        vstep(lambda v: v.reduce_max(M2, v3(SC3), axis=AX.X), sync=True)
                        vstep(lambda v: v.tensor_tensor(v3(SC2), v3(LG), bview(M2, 16), op=OP.is_ge))
                        vstep(lambda v: v.tensor_single_scalar(A1, M1, 0.0, op=OP.is_gt))
                        vstep(lambda v: v.tensor_scalar(NA, A1, -1.0, 1.0, op0=OP.mult, op1=OP.add))
                        vstep(lambda v: v.tensor_tensor(v3(SC1), v3(SC1), bview(A1, 16), op=OP.mult))
                        vstep(lambda v: v.tensor_tensor(v3(SC3), v3(SC2), bview(NA, 16), op=OP.mult))
                        vstep(lambda v: v.tensor_add(SC1, SC1, SC3))
                        if tb == 3 and app == 0:
                            dbg("g_sc1f", SC1)
                            dbg("g_m1f", M1)
                            dbg("g_m2f", M2)
                            dbg("g_sc2f", SC2)
                        vstep(lambda v: v.tensor_single_scalar(M1, M1, 0.0, op=OP.max))
                        vstep(lambda v: v.tensor_single_scalar(LG, LG, 0.0, op=OP.max))
                        mgc = vstep(lambda v: v.tensor_tensor(v3(LG), v3(LG), bview(M1, 16), op=OP.subtract))
                        k.wait("act", mgc)
                        mge = k.run("act", lambda s: s.activation(LG, LG, AF.Exp))
                        k.wait("ve", mge)
                        vstep(lambda v: v.tensor_mul(LG, LG, SC1))
                        vstep(lambda v: v.reduce_sum(M2, v3(LG), axis=AX.X), sync=True)
                        vstep(lambda v: v.reciprocal(M1, M2), sync=True)
                        vstep(lambda v, tb=tb: v.tensor_tensor(v3(RW[:, tb * 64:(tb + 1) * 64]), v3(LG), bview(M1, 16), op=OP.mult))
                        gps_rd[gp] = vstep(lambda v, tb=tb: v.tensor_copy(RWB16[:, tb * 64:(tb + 1) * 64], RW[:, tb * 64:(tb + 1) * 64]))
                        lnm[tb] = k.mile("ve")
                        for i_ in range(4):
                            kidx = tb * 4 + i_
                            emit_wcast(kidx)
                            if kidx + 2 < 16:
                                issue_wdma(kidx + 2)
                        if tb == 3 and app == 0:
                            dbg("g_lg", LG); dbg("g_sc1", SC1); dbg("g_m1", M1); dbg("g_m2", M2)
                    dbg(f"frw{app}", RW)
                    dbg(f"fgnrc{app}", GNRC)
                    gps_done = k.mile("ve")

                with ExitStack() as pb:
                    QCT = [sb(f"f{app}_qct{i}", [128, 1536], BF16, pb) for i in range(4)]
                    DIAG2 = [sb(f"f{app}_diag{i}", [128, 16 * 128], BF16, pb) for i in range(2)]
                    QKT = sb(f"f{app}_qkt", [128, 128], BF16, pb)
                    ps_qc = [pb.enter_context(nc.psum_tensor(_uniq(f"f{app}_psqc{i}"), [128, 1536], F32)).ap()
                             for i in range(2)]
                    ps_qkvb = pb.enter_context(nc.psum_tensor(_uniq(f"f{app}_psqkv"), [128, 512], F32)).ap()
                    ps_qkv = ps_qkvb[:, 0:192]
                    ps_mis = pb.enter_context(nc.psum_tensor(_uniq(f"f{app}_psmis"), [128, 1024], BF16)).ap()
                    ps_tp = ps_mis[:, 0:128]
                    ps_rt = ps_mis[0:16, 128:256]

                    msetf = k.run("ve", lambda v: v.memset(VAUGF, 1.0))

                    qct_rd = [None, None, None, None]
                    qc_rd = [None, None]
                    diag_rd = [None, None]
                    qkv_rd = None
                    tp_rd = None
                    evs = {}
                    mdg_l = {}

                    def emit_qc(tau):
                        for hf in range(2):
                            si = (tau * 2 + hf) % 4
                            qb = hf
                            k.wait("pe", qc_rd[qb], lnm[tau // 4] if hf == 0 else None,
                                   *( wc_all + [gps_done] if tau == 0 and hf == 0 else []))

                            def f_qc(t, tau=tau, hf=hf, qb=qb):
                                last = None
                                for kc in range(NC8):
                                    for nb in range(3):
                                        last = t.matmul(
                                            ps_qc[qb][:, nb * 512:(nb + 1) * 512],
                                            XNF[:, kc * T + tau * 128: kc * T + (tau + 1) * 128],
                                            WCB[:, kc * 3 * DM + hf * 1536 + nb * 512:
                                                kc * 3 * DM + hf * 1536 + (nb + 1) * 512],
                                            start=(kc == 0), stop=(kc == NC8 - 1),
                                            skip_group_check=True)
                                return last
                            mqc = k.run("pe", f_qc)
                            k.wait("ve", mqc, qct_rd[si])
                            qc_rd[qb] = k.run("ve", lambda v, si=si, qb=qb: v.tensor_copy(QCT[si], ps_qc[qb]))
                            qct_rd[si] = qc_rd[qb]
                            evs.setdefault(tau, []).append(qc_rd[qb])

                    def emit_diag(tau):
                        db_ = tau % 2
                        k.wait("ve", diag_rd[db_])
                        mdg_l[tau] = k.run("ve", lambda v, tau=tau, db_=db_: v.tensor_tensor(
                            DIAG2[db_].rearrange("p (e q) -> p e q", q=128),
                            IDENT.unsqueeze(1).broadcast_to([128, 16, 128]),
                            bview(RW[:, tau * 16:(tau + 1) * 16], 128), op=OP.mult))

                    emit_qc(0)
                    emit_diag(0)
                    for tau in range(NT):
                        tb, tl = tau // 4, tau % 4
                        if tau + 1 < NT:
                            emit_qc(tau + 1)
                            emit_diag(tau + 1)
                        db_ = tau % 2
                        k.wait("pe", mdg_l[tau], *evs[tau], qkv_rd)

                        def f_qkv(t, tau=tau, db_=db_):
                            last = None
                            for e in range(16):
                                si = (tau * 2 + (e // 8)) % 4
                                eo = (e % 8) * 192
                                last = t.matmul(ps_qkv, DIAG2[db_][:, e * 128:(e + 1) * 128],
                                                QCT[si][:, eo: eo + 192],
                                                start=(e == 0), stop=(e == 15), skip_group_check=True)
                            return last
                        mqkv = k.run("pe", f_qkv)
                        diag_rd[db_] = mqkv
                        qct_rd[(tau * 2) % 4] = mqkv
                        qct_rd[(tau * 2 + 1) % 4] = mqkv
                        k.wait("act", mqkv, msetf if tau == 0 else None, tp_rd)

                        def f_ev(s, tau=tau):
                            s.activation(QKT, ps_qkv[:, 0:128], AF.Copy)
                            return s.activation(VAUGF[:, tau * 65: tau * 65 + 64],
                                                ps_qkv[:, 128:192], AF.Copy)
                        qkv_rd = k.run("act", f_ev)
                        k.wait("pe", qkv_rd, tp_rd)

                        def f_tp(t, tau=tau):
                            t.transpose(ps_tp[0:64, :], QKT[:, 0:64], IDENT)
                            t.transpose(ps_tp[64:128, :], QKT[:, 64:128], IDENT)
                            return t.transpose(ps_rt, RWB16[:, tau * 16:(tau + 1) * 16], IDENT)
                        mtp = k.run("pe", f_tp)
                        k.wait("act", mtp)

                        def f_tpe(s, tau=tau):
                            s.activation(QKFQ[:, tau * 128:(tau + 1) * 128], ps_tp[0:64, :], AF.Copy)
                            s.activation(QKFK[:, tau * 128:(tau + 1) * 128], ps_tp[64:128, :], AF.Copy)
                            return s.activation(RWT[:, tau * 128:(tau + 1) * 128], ps_rt, AF.Copy)
                        tp_rd = k.run("act", f_tpe)
                    dbg(f"fqkfq{app}", QKFQ)
                    dbg(f"fqkfk{app}", QKFK)
                    dbg(f"fvaug{app}", VAUGF)
                    dbg(f"frwt{app}", RWT)
                    k.fence()

                wcb_cm.__exit__(None, None, None)
                for cm_ in reversed(wc_cms):
                    cm_.__exit__(None, None, None)
                pf_cm = nc.sbuf_tensor(_uniq(f"f{app}_pf"), [128, T], BF16)
                PF = pf_cm.__enter__().ap()
                at2_cm = nc.sbuf_tensor(_uniq(f"f{app}_attn2"), [128, T], BF16)
                ATTN2 = at2_cm.__enter__().ap()

                with ExitStack() as pc:
                    ATG = sb(f"f{app}_atg", [128, 4 * 64], BF16, pc)
                    Z4 = sb(f"f{app}_z4", [128, 4], F32, pc)
                    Z4R = sb(f"f{app}_z4r", [128, 4], F32, pc)
                    ps_sc = pc.enter_context(nc.psum_tensor(_uniq(f"f{app}_pssc"), [128, 2048], F32)).ap()
                    ps_av = pc.enter_context(nc.psum_tensor(_uniq(f"f{app}_psav"), [128, 512], F32)).ap()
                    ps_atb = pc.enter_context(nc.psum_tensor(_uniq(f"f{app}_psat"), [128, 1024], BF16)).ap()
                    ps_at = ps_atb[0:64, 0:128]
                    mz_sc = k.run("ve", lambda v: v.memset(ps_sc, 0.0))
                    pf_rd = None
                    avz_rd = None
                    at2_rd = None
                    for tb in range(4):
                        for tl in range(4):
                            tau = tb * 4 + tl
                            nk = tau + 1
                            k.wait("pe", pf_rd, mz_sc if tau == 0 else None)

                            def f_scm(t, tau=tau, nk=nk):
                                last = None
                                for kp in range(nk):
                                    last = t.matmul(ps_sc[:, kp * 128:(kp + 1) * 128],
                                                    QKFK[:, kp * 128:(kp + 1) * 128],
                                                    QKFQ[:, tau * 128:(tau + 1) * 128],
                                                    start=(kp % 4 == 0), stop=(kp == nk - 1),
                                                    skip_group_check=True)
                                return last
                            msc = k.run("pe", f_scm)
                            nch = (nk + 3) // 4
                            for c in range(nch):
                                klo, khi = c * 4, min(nk, c * 4 + 4)
                                k.wait("act", msc if c == 0 else None, pf_rd if c == 0 else None)
                                mex = k.run("act", lambda s, klo=klo, khi=khi: s.activation(
                                    PF[:, klo * 128:khi * 128], ps_sc[:, klo * 128:khi * 128],
                                    AF.Exp, scale=0.125))
                                if khi == nk:
                                    k.wait("ve", mex)
                                    mex = k.run("ve", lambda v, tau=tau: v.tensor_tensor(
                                        PF[:, tau * 128:(tau + 1) * 128],
                                        PF[:, tau * 128:(tau + 1) * 128], TRI, op=OP.mult))
                                k.wait("pe", mex, avz_rd if tl == 0 and c == 0 else None)

                                def f_av(t, klo=klo, khi=khi, tl=tl):
                                    last = None
                                    for kp in range(klo, khi):
                                        last = t.matmul(ps_av[:, tl * 128: tl * 128 + 65],
                                                        PF[:, kp * 128:(kp + 1) * 128],
                                                        VAUGF[:, kp * 65:(kp + 1) * 65],
                                                        start=(tl == 0 and kp == 0), stop=False,
                                                        skip_group_check=True)
                                    return last
                                pf_rd = k.run("pe", f_av)
                        k.wait("ve", pf_rd)

                        k.run("ve", lambda v: v.tensor_copy(Z4, pslots(ps_av, 128, 4, 1, 64)))
                        k.self_sync("ve")
                        mz4 = k.run("ve", lambda v: v.reciprocal(Z4R, Z4))
                        k.self_sync("ve")

                        def f_zsc(v):
                            return v.tensor_tensor(ATG.rearrange("p (s q) -> p s q", q=64),
                                                   pslots(ps_av, 128, 4, 64), bview(Z4R, 64), op=OP.mult)
                        avz_rd = k.run("ve", f_zsc)
                        at_rd = None
                        for tl in range(4):
                            tau = tb * 4 + tl
                            k.wait("pe", avz_rd, at_rd)
                            mat = k.run("pe", lambda t, tl=tl: t.transpose(
                                ps_at, ATG[:, tl * 64:(tl + 1) * 64], IDENT))
                            k.wait("ve", mat, at2_rd)

                            def f_a2(v, tau=tau):
                                v.tensor_copy(ATTN2[0:64, tau * 128:(tau + 1) * 128], ps_at)
                                return v.tensor_copy(ATTN2[64:128, tau * 128:(tau + 1) * 128], ps_at)
                            at_rd = k.run("ve", f_a2)
                            at2_rd = at_rd
                        avz_rd = k.mile("pe")
                    dbg(f"fattn2{app}", ATTN2)
                    k.fence()

                with ExitStack() as pd:
                    SEL = sb(f"f{app}_sel", [16, NC8 * 128], BF16, pd)
                    SELF_ = sb(f"f{app}_self", [16, NC8 * 128], F32, pd)
                    m_self = k.dma(SELF_, d_sel)
                    k.wait("ve", m_self)
                    m_sel = k.run("ve", lambda v: v.tensor_copy(SEL, SELF_))
                    if app == 0:
                        dbg("c_sel", SEL)
                    k.wait("pe", m_sel)
                    OC32 = [sb(f"f{app}_oc{i}", [128, DM], F32, pd) for i in range(2)]
                    OCB = sb(f"f{app}_ocb", [128, NC8 * DM], BF16, pd)
                    MXD2 = [sb(f"f{app}_mxd{i}", [128, NC8 * 512], BF16, pd) for i in range(2)]
                    RWBB = sb(f"f{app}_rwbb", [128, 512], BF16, pd)
                    ps_rwb = [pd.enter_context(nc.psum_tensor(_uniq(f"f{app}_psrwb{i}"), [128, 512], F32)).ap()
                              for i in range(2)]
                    ps_o = [pd.enter_context(nc.psum_tensor(_uniq(f"f{app}_pso{i}"), [128, 512], F32)).ap()
                            for i in range(2)]
                    ocm = [None, None]
                    oc_all = [m_sel]
                    for mb in range(8):
                        b = mb % 2
                        k.wait("sync", ocm[b])
                        mdo = k.dma(OC32[b], d_ocatmb[mb * 128:(mb + 1) * 128, :])
                        k.wait("act", mdo, ocm[b])
                        ocm[b] = k.run("act", lambda s, mb=mb, b=b: s.activation(
                            OCB.rearrange("p (c m) -> p c m", m=DM)[:, :, mb * 128:(mb + 1) * 128],
                            OC32[b].rearrange("p (c m) -> p c m", m=128), AF.Copy))
                        oc_all.append(ocm[b])
                    rwb_rd = [None, None]
                    mxd_rd = [None, None]
                    orm = [None, None]
                    mx_all_l = [None] * 4

                    def emit_mx(tb):
                        mp = tb % 2
                        mx_all = []
                        for kc in range(NC8):
                            rb = kc % 2
                            k.wait("pe", rwb_rd[rb], *(oc_all if tb == 0 and kc == 0 else []))
                            mrwb = k.run("pe", lambda t, kc=kc, tb=tb, rb=rb: t.matmul(
                                ps_rwb[rb], SEL[:, kc * 128:(kc + 1) * 128], RWT[:, tb * 512:(tb + 1) * 512],
                                start=True, stop=True, skip_group_check=True))
                            k.wait("ve", mrwb, mxd_rd[mp] if kc == 0 else None)

                            def f_mx(v, kc=kc, tb=tb, rb=rb, mp=mp):
                                v.tensor_copy(RWBB, ps_rwb[rb])
                                return v.tensor_tensor(MXD2[mp][:, kc * 512:(kc + 1) * 512],
                                                       ATTN2[:, tb * 512:(tb + 1) * 512], RWBB,
                                                       op=OP.mult)
                            rwb_rd[rb] = k.run("ve", f_mx)
                            if tb == 0 and app == 0 and kc == 0:
                                dbg("g_mxd0", MXD2[0][:, 0:512])
                                dbg("g_rwbb", RWBB)
                            mx_all.append(rwb_rd[rb])
                        mx_all_l[tb] = mx_all

                    emit_mx(0)
                    for tb in range(4):
                        if tb + 1 < 4:
                            emit_mx(tb + 1)
                        mp = tb % 2
                        o_mms = []
                        for c in range(NC8):
                            b = c % 2
                            k.wait("pe", orm[b], *(mx_all_l[tb] if c == 0 else []))

                            def f_oc(t, c=c, b=b, mp=mp):
                                last = None
                                for kc in range(NC8):
                                    last = t.matmul(ps_o[b], OCB[:, kc * DM + c * 128: kc * DM + (c + 1) * 128],
                                                    MXD2[mp][:, kc * 512:(kc + 1) * 512],
                                                    start=(kc == 0), stop=(kc == NC8 - 1),
                                                    skip_group_check=True)
                                return last
                            moc = k.run("pe", f_oc)
                            o_mms.append(moc)
                            k.wait("ve", moc)
                            orm[b] = k.run("ve", lambda v, c=c, tb=tb, b=b: v.tensor_tensor(
                                X[:, c * T + tb * 512: c * T + (tb + 1) * 512],
                                X[:, c * T + tb * 512: c * T + (tb + 1) * 512], ps_o[b], op=OP.add))
                        mxd_rd[mp] = o_mms[-1]
                    k.fence()
                at2_cm.__exit__(None, None, None)
                pf_cm.__exit__(None, None, None)

        def attn_b():
            with ExitStack() as ls:
                QB = sb("b_qb", [128, NC8 * T], BF16, ls)
                KBf = sb("b_kb", [128, NC8 * T], BF16, ls)
                VAUGB = sb("b_vaug", [128, NT * 16 * 65], BF16, ls)
                msetb = k.run("ve", lambda v: v.memset(VAUGB, 1.0))
                with ExitStack() as qs:
                    XNB2 = sb("b_xnb2", [128, NC8 * 1024], BF16, qs)
                    ps_qk = [qs.enter_context(nc.psum_tensor(_uniq(f"b_psqk{i}"), [128, 512], F32)).ap()
                             for i in range(2)]
                    ps_v = qs.enter_context(nc.psum_tensor(_uniq("b_psv"), [128, 512], F32)).ap()
                    vv = None
                    wvb_rd = None
                    evm = [None, None]
                    wrd_b = [None, None]
                    for pr in range(2):
                        with ExitStack() as lns:
                            lnd = make_ln_scope(f"lnb{pr}", lns, nbuf=3)
                            for j in range(2):
                                tb = 2 * pr + j
                                ln_tb(lnd, tb, lambda kc, j=j: XNB2[:, kc * 1024 + j * 512:
                                                                    kc * 1024 + (j + 1) * 512])
                            k.fence()
                        with ExitStack() as ws:
                            WST = [sb(f"b_wst{pr}_{i}", [128, DM], F32, ws) for i in range(2)]
                            WBB = [w.bitcast(BF16).rearrange("p (n two) -> p n two", two=2)[:, :, 1]
                                   for w in WST]
                            WV32 = sb(f"b_wv{pr}", [128, 512], F32, ws)
                            WVB = sb(f"b_wvb{pr}", [128, NC8 * 512], BF16, ws)
                            wvm = [None]
                            wv_pre = {0: [], 1: []}

                            def emit_wv(hf, kc):
                                k.wait("sync", wvm[0])
                                mdv = k.dma(WV32, d_wv[kc * 128:(kc + 1) * 128,
                                                       hf * 512:(hf + 1) * 512])
                                k.wait("act", mdv, wvm[0], wvb_rd)
                                wvm[0] = k.run("act", lambda s, kc=kc: s.activation(
                                    WVB[:, kc * 512:(kc + 1) * 512], WV32, AF.Copy))
                                wv_pre[hf].append(wvm[0])

                            for mb in range(16):
                                b = mb % 2
                                dst = QB if mb < 8 else KBf
                                src = d_wqmb if mb < 8 else d_wkmb
                                if mb >= 8:
                                    emit_wv(0, mb - 8)
                                k.wait("sync", wrd_b[b])
                                md = k.dma(WST[b], src[(mb % 8) * 128:((mb % 8) + 1) * 128, :])
                                mm = None
                                for j in range(2):
                                    tb = 2 * pr + j
                                    k.wait("pe", md if j == 0 else None, evm[j])

                                    def f_mm(t, b=b, j=j):
                                        last = None
                                        for kc in range(NC8):
                                            last = t.matmul(ps_qk[j], WBB[b][:, kc * 128:(kc + 1) * 128],
                                                            XNB2[:, kc * 1024 + j * 512:
                                                                 kc * 1024 + (j + 1) * 512],
                                                            start=(kc == 0), stop=(kc == NC8 - 1),
                                                            skip_group_check=True)
                                        return last
                                    mm = k.run("pe", f_mm)
                                    k.wait("ve", mm)
                                    bcol = CB[:, (1 if mb < 8 else 2) * NC8 + (mb % 8):
                                              (1 if mb < 8 else 2) * NC8 + (mb % 8) + 1]
                                    evm[j] = k.run("ve", lambda v, dst=dst, mb=mb, tb=tb, j=j, bcol=bcol:
                                                   v.tensor_scalar(
                                                       dst[:, (mb % 8) * T + tb * 512: (mb % 8) * T + (tb + 1) * 512],
                                                       ps_qk[j], bcol, None, op0=OP.add))
                                wrd_b[b] = mm
                            for hf in range(2):
                                if not wv_pre[hf]:
                                    for kc in range(NC8):
                                        emit_wv(hf, kc)
                                wvc = wv_pre[hf]
                                mv = None
                                for j in range(2):
                                    for kti in range(4):
                                        kt = pr * 8 + j * 4 + kti
                                        k.wait("pe", vv, *wvc)

                                        def f_vmm(t, j=j, kti=kti, hf=hf):
                                            last = None
                                            for kc in range(NC8):
                                                last = t.matmul(ps_v,
                                                                XNB2[:, kc * 1024 + j * 512 + kti * 128:
                                                                     kc * 1024 + j * 512 + (kti + 1) * 128],
                                                                WVB[:, kc * 512:(kc + 1) * 512],
                                                                start=(kc == 0), stop=(kc == NC8 - 1),
                                                                skip_group_check=True)
                                            return last
                                        mv = k.run("pe", f_vmm)
                                        k.wait("ve", mv, msetb)
                                        dstv = VAUGB.rearrange("p (kt h s) -> p kt h s", h=16, s=65)[:, kt, hf * 8:(hf + 1) * 8, 0:64]
                                        vv = k.run("ve", lambda v, dstv=dstv, hf=hf: v.tensor_tensor(
                                            dstv, ps_v.rearrange("p (h s) -> p h s", s=64),
                                            BVB.rearrange("p (h s) -> p h s", s=64)[:, hf * 8:(hf + 1) * 8, :],
                                            op=OP.add))
                                wvb_rd = mv
                            k.fence()

                if B_PHASE < 2:
                    return
                with ExitStack() as asx:
                    PTB = [sb(f"b_ptb{i}", [128, 2 * 512], BF16, asx) for i in range(2)]
                    ATG = [sb(f"b_atg{i}", [128, 4 * 64], BF16, asx) for i in range(2)]
                    Z4 = sb("b_z4", [128, 4], F32, asx)
                    Z4R = sb("b_z4r", [128, 4], F32, asx)
                    ATTNB = sb("b_attnb", [128, NC8 * 512], BF16, asx)
                    WOC32 = [sb(f"b_woc32{i}", [128, DM], F32, asx) for i in range(2)]
                    WOC = [sb(f"b_woc{i}", [128, DM], BF16, asx) for i in range(2)]
                    ps_sc = [asx.enter_context(nc.psum_tensor(_uniq(f"b_pssc{i}"), [128, 1024], F32)).ap()
                             for i in range(2)]
                    ps_av = [asx.enter_context(nc.psum_tensor(_uniq(f"b_psav{i}"), [128, 512], F32)).ap()
                             for i in range(2)]
                    ps_at4 = [asx.enter_context(nc.psum_tensor(_uniq(f"b_psat{i}"), [128, 1024], BF16)).ap()
                              for i in range(2)]
                    ps_o = [ps_sc[0][:, 0:512], ps_sc[0][:, 512:1024]]

                    def f_bias(v):
                        last = None
                        for kc in range(NC8):
                            last = v.tensor_scalar_add(X[:, kc * T:(kc + 1) * T],
                                                       X[:, kc * T:(kc + 1) * T],
                                                       CB[:, 3 * NC8 + kc: 3 * NC8 + kc + 1])
                        return last
                    k.run("ve", f_bias)

                    kgroups = [[2 * i, 2 * i + 1] for i in range(8)]
                    sc_rd = [None, None]
                    pt_rd = [None, None]
                    av_rd = [None, None]
                    atg_rd = [None, None]
                    at_rd = [None, None]
                    attnb_rd = None
                    wocm = [None, None]
                    wocb_rd = [None, None]
                    orm = [None, None]
                    pend = [None]

                    def flush_pend():
                        if pend[0] is None:
                            return None
                        hh, mzs = pend[0]
                        pend[0] = None
                        rlo = (hh % 2) * 64
                        hc = hh // 2
                        pb = hc % 2
                        ab = hh % 2
                        mat = None
                        for tl in range(4):
                            k.wait("pe", mzs if tl == 0 else None,
                                   at_rd[pb] if (hh % 2 == 0 and tl == 0) else None)
                            mat = k.run("pe", lambda t, tl=tl, rlo=rlo, pb=pb, ab=ab: t.transpose(
                                ps_at4[pb][rlo:rlo + 64, tl * 128:(tl + 1) * 128],
                                ATG[ab][:, tl * 64:(tl + 1) * 64], IDENT))
                        atg_rd[ab] = mat
                        if hh % 2 == 1:
                            k.wait("ve", mat, attnb_rd)
                            at_rd[pb] = k.run("ve", lambda v, hc=hc, pb=pb: v.tensor_copy(
                                ATTNB[:, hc * 512:(hc + 1) * 512], ps_at4[pb][:, 0:512]))
                            return at_rd[pb]
                        return None

                    for tq in range(4):
                        atbm = None
                        for hh in range(16):
                            rlo = (hh % 2) * 64
                            hc = hh // 2
                            apb = hh % 2
                            first_av = [True]
                            avm = None
                            NG = len(kgroups)
                            mex_l = [None] * NG

                            def emit_sc(gi):
                                grp = kgroups[gi]
                                gb = gi % 2
                                ms = []
                                for j, kp in enumerate(grp):
                                    k.wait("pe", sc_rd[gb] if j == 0 else None)
                                    mj = k.run("pe", lambda t, kp=kp, j=j, gb=gb, hc=hc,
                                               rlo=rlo, tq=tq: t.matmul(
                                        ps_sc[gb][:, j * 512:(j + 1) * 512],
                                        KBf[rlo:rlo + 64, hc * T + kp * 128: hc * T + (kp + 1) * 128],
                                        QB[rlo:rlo + 64, hc * T + tq * 512: hc * T + (tq + 1) * 512],
                                        start=True, stop=True, skip_group_check=True))
                                    k.wait("act", mj, pt_rd[gb] if j == 0 else None)
                                    ms.append(k.run("act", lambda s, j=j, gb=gb: s.activation(
                                        PTB[gb][:, j * 512:(j + 1) * 512],
                                        ps_sc[gb][:, j * 512:(j + 1) * 512], AF.Exp, scale=0.125)))
                                mex_l[gi] = ms
                                sc_rd[gb] = ms[-1]

                            emit_sc(0)
                            for gi in range(NG):
                                if gi + 1 < NG:
                                    emit_sc(gi + 1)
                                if B_PHASE < 2.5:
                                    continue
                                grp = kgroups[gi]
                                gb = gi % 2
                                avm = None
                                for j, kp in enumerate(grp):
                                    k.wait("pe", mex_l[gi][j],
                                           av_rd[apb] if first_av[0] else None)

                                    def f_avg(t, kp=kp, j=j, gb=gb, hh=hh, fa=first_av, apb=apb):
                                        last = None
                                        for tl in range(4):
                                            last = t.matmul(
                                                ps_av[apb][:, tl * 128: tl * 128 + 65],
                                                PTB[gb][:, j * 512 + tl * 128: j * 512 + (tl + 1) * 128],
                                                VAUGB[:, kp * 1040 + hh * 65: kp * 1040 + hh * 65 + 65],
                                                start=(fa[0] and tl == 0), stop=False,
                                                skip_group_check=True)
                                            fa[0] = False
                                        return last
                                    avm = k.run("pe", f_avg)
                                pt_rd[gb] = avm
                            cp = flush_pend()
                            if cp is not None:
                                atbm = cp
                            if B_PHASE < 2.7:
                                continue
                            k.wait("ve", avm, atg_rd[apb])
                            k.run("ve", lambda v, apb=apb: v.tensor_copy(
                                Z4, pslots(ps_av[apb], 128, 4, 1, 64)))
                            k.self_sync("ve")
                            mz = k.run("ve", lambda v: v.reciprocal(Z4R, Z4))
                            k.self_sync("ve")

                            def f_zsc(v, apb=apb):
                                return v.tensor_tensor(ATG[apb].rearrange("p (s q) -> p s q", q=64),
                                                       pslots(ps_av[apb], 128, 4, 64), bview(Z4R, 64),
                                                       op=OP.mult)
                            mzs = k.run("ve", f_zsc)
                            av_rd[apb] = mzs
                            if B_PHASE < 2.9:
                                continue
                            pend[0] = (hh, mzs)
                        cp = flush_pend()
                        if cp is not None:
                            atbm = cp
                        if B_PHASE < 3:
                            continue
                        o_last = []
                        for c in range(NC8):
                            b = c % 2
                            k.wait("sync", wocm[b])
                            mdw = k.dma(WOC32[b], d_womb[c * 128:(c + 1) * 128, :])
                            k.wait("act", mdw, wocm[b], wocb_rd[b])
                            wocm[b] = k.run("act", lambda s, b=b: s.activation(WOC[b], WOC32[b], AF.Copy))
                            k.wait("pe", wocm[b], orm[b], atbm)

                            def f_oc(t, c=c, b=b):
                                last = None
                                for kc in range(NC8):
                                    last = t.matmul(ps_o[b], WOC[b][:, kc * 128:(kc + 1) * 128],
                                                    ATTNB[:, kc * 512:(kc + 1) * 512],
                                                    start=(kc == 0), stop=(kc == NC8 - 1),
                                                    skip_group_check=True)
                                return last
                            moc = k.run("pe", f_oc)
                            wocb_rd[b] = moc
                            o_last.append(moc)
                            k.wait("ve", moc)
                            orm[b] = k.run("ve", lambda v, c=c, tq=tq, b=b: v.tensor_tensor(
                                X[:, c * T + tq * 512: c * T + (tq + 1) * 512],
                                X[:, c * T + tq * 512: c * T + (tq + 1) * 512], ps_o[b], op=OP.add))
                        attnb_rd = o_last[-1]
                        sc_rd[0] = later(orm[0], orm[1])
                    k.fence()

        seq = [attn_a, lambda: ffn(0), attn_b, lambda: ffn(1)][:parts]
        for f in seq:
            f()

        out_m = [k.dma(d_out[kc * 128:(kc + 1) * 128, :], X[:, kc * T:(kc + 1) * T].bitcast(F32))
                 for kc in range(NC8)]
        k.wait("sync", *out_m)

        with nc.Block() as blk:
            @blk.tensor
            def _(eng):
                for f in k.q["pe"]:
                    f(eng)

            @blk.scalar
            def _(eng):
                for f in k.q["act"]:
                    f(eng)

            @blk.vector
            def _(eng):
                for f in k.q["ve"]:
                    f(eng)

            @blk.gpsimd
            def _(eng):
                for f in k.q["pool"]:
                    f(eng)

            @blk.sync
            def _(eng):
                for f in k.q["sync"]:
                    f(eng)

    return nc


def _hi16(w):
    return np.ascontiguousarray(w.astype(ml_dtypes.bfloat16).astype(np.float32))


def _mb_layout(w, nblk):
    out = np.empty((nblk * 128, DM), np.float32)
    for mb in range(nblk):
        blk = w[:, mb * 128:(mb + 1) * 128]
        out[mb * 128:(mb + 1) * 128] = blk.reshape(NC8, 128, 128).transpose(1, 0, 2).reshape(128, DM)
    return np.ascontiguousarray(out)


def _host_consts():
    ident = np.eye(128, dtype=np.float32)

    def band(koff):
        j = np.arange(128)[:, None]
        t = np.arange(128)[None, :]
        d = j + 128 * koff - t
        return ((d >= -3) & (d <= 3)).astype(np.float32)
    m_prev, m_diag, m_next = band(-1), band(0), band(1)
    zero = np.zeros((128, 128), np.float32)
    first = np.concatenate([zero, m_diag, m_next], axis=1)
    mid = np.concatenate([m_prev, m_diag, m_next], axis=1)
    last = np.concatenate([m_prev, m_diag, zero], axis=1)
    maska = np.concatenate([first, mid, last], axis=1).astype(np.float32)
    tri = (np.arange(128)[:, None] <= np.arange(128)[None, :]).astype(np.float32)
    sel = np.zeros((16, NC8 * 128), np.float32)
    for kc in range(NC8):
        sel[2 * kc, kc * 128: kc * 128 + 64] = 1.0
        sel[2 * kc + 1, kc * 128 + 64: kc * 128 + 128] = 1.0
    return ident, maska, tri, np.ascontiguousarray(sel)


def kernel(**inp):
    x = np.asarray(inp["L0_attn_a_x"], np.float32)
    w_qkv = np.asarray(inp["L0_attn_a_w_qkv"], np.float32)
    projm = np.ascontiguousarray(np.asarray(inp["L0_attn_a_proj"], np.float32))
    w_out = np.asarray(inp["L0_attn_a_w_out"], np.float32)
    b_out = np.asarray(inp["L0_attn_a_b_out"], np.float32)
    Wq = np.asarray(inp["L0_attn_b_Wq"], np.float32)
    bq = np.asarray(inp["L0_attn_b_bq"], np.float32)
    Wk = np.asarray(inp["L0_attn_b_Wk"], np.float32)
    bk = np.asarray(inp["L0_attn_b_bk"], np.float32)
    Wv = np.ascontiguousarray(np.asarray(inp["L0_attn_b_Wv"], np.float32))
    bv = np.asarray(inp["L0_attn_b_bv"], np.float32)
    Wo = np.asarray(inp["L0_attn_b_Wo"], np.float32)
    bo = np.asarray(inp["L0_attn_b_bo"], np.float32)
    sim = np.asarray(inp["L0_ffn_sim_matrix"], np.float32)
    gates = np.asarray(inp["L0_ffn_gates"], np.float32)
    q_proj = np.asarray(inp["L0_ffn_q_proj"], np.float32)
    k_proj = np.asarray(inp["L0_ffn_k_proj"], np.float32)
    v_proj = np.asarray(inp["L0_ffn_v_proj"], np.float32)
    o_proj = np.asarray(inp["L0_ffn_o_proj"], np.float32)

    wcat = np.ascontiguousarray(np.concatenate(
        [np.concatenate([q_proj[e], k_proj[e], v_proj[e]], axis=1) for e in range(16)], axis=1))
    ocat = o_proj.reshape(16 * 64, DM)
    simc = np.ascontiguousarray(sim.reshape(NC8, 128, 16).transpose(1, 0, 2).reshape(128, NC8 * 16))
    cb = np.ascontiguousarray(np.concatenate(
        [b_out.reshape(NC8, 128).T, bq.reshape(NC8, 128).T,
         bk.reshape(NC8, 128).T, bo.reshape(NC8, 128).T], axis=1).astype(np.float32))
    ident, maska, tri, sel = _host_consts()

    nc = build_program()
    common = {
        "wqkv_qk_mb": _hi16(_mb_layout(w_qkv[:, 0:2048], 16)),
        "wqkv_v": np.ascontiguousarray(w_qkv[:, 2048:3072]),
        "projm": projm,
        "wout_mb": _mb_layout(w_out, 8),
        "wq_mb": _hi16(_mb_layout(Wq, 8)), "wk_mb": _hi16(_mb_layout(Wk, 8)),
        "Wv": Wv, "wo_mb": _mb_layout(Wo, 8),
        "Wcat": wcat, "ocat_mb": _mb_layout(ocat, 8),
        "simc": simc, "gatesr": gates.reshape(1, 16),
        "biascols": cb, "bvrow": bv.reshape(1, DM),
        "identc": ident, "maska": maska, "tric": tri, "selc": sel,
        "onesr": np.ones((128, 2), np.float32),
        "ones1": np.ones((1, 128), np.float32),
        "ones1r": np.ones((1, 128), np.float32),
        "idf1": np.ones((1, 1), np.float32),
    }
    in_maps = []
    for b in range(8):
        m = dict(common)
        m["x_fm"] = np.ascontiguousarray(x[b].T)
        in_maps.append(m)
    res = run_bass_kernel_spmd(nc, in_maps, core_ids=list(range(8)))
    if DBG_RAW:
        return res.results
    out = np.stack([np.ascontiguousarray(res.results[b]["x_out"].T) for b in range(8)])
    return out.astype(np.float32)


if __name__ == "__main__":
    build_program()
    print("build ok")


# version 38
# speedup vs baseline: 1.0206x; 1.0142x over previous
import numpy as np
import ml_dtypes
import concourse.bass as bass
import concourse.mybir as mybir
from concourse.bass_utils import run_bass_kernel_spmd
from contextlib import ExitStack

F32 = mybir.dt.float32
F32R = mybir.dt.float32r
BF16 = mybir.dt.bfloat16
AF = mybir.ActivationFunctionType
OP = mybir.AluOpType
AX = mybir.AxisListType

T = 2048
DM = 1024
NC8 = 8
NT = 16
PARTS = 4
DBG_WANT = ()
DBG_RAW = False
B_PHASE = 3

ENGINES = ("pe", "act", "ve", "pool", "sync")


def later(*miles):
    best = None
    for m in miles:
        if m is None:
            continue
        if best is None or m[1] > best[1]:
            assert best is None or m[0] == best[0]
            best = m
    return best


class KB:
    def __init__(self, nc):
        self.nc = nc
        self.q = {e: [] for e in ENGINES}
        self.cnt = {e: 0 for e in ENGINES}
        self.cnt["sd"] = 0
        self.sems = {}
        self.lastw = {}

    N_DSEM = 8

    def make_sems(self, stack):
        for s in ("pe", "act", "ve", "pool", "sync", "sd"):
            self.sems[s] = stack.enter_context(self.nc.semaphore("sem_" + s))
        for i in range(self.N_DSEM):
            self.sems[f"d{i}"] = stack.enter_context(self.nc.semaphore(f"sem_d{i}"))
        self.dma_i = 0
        self.dcnt = [0] * self.N_DSEM

    def wait(self, e, *miles):
        for m in miles:
            if m is None:
                continue
            s, v = m
            if v <= 0 or s == e:
                continue
            key = (e, s)
            if self.lastw.get(key, 0) >= v:
                continue
            self.lastw[key] = v
            sem = self.sems[s]
            self.q[e].append(lambda eng, sem=sem, v=v: eng.wait_ge(sem, v))

    def run(self, e, f, inc=True):
        if inc:
            sem = self.sems[e]
            self.cnt[e] += 1

            def thunk(eng, f=f, sem=sem):
                last = f(eng)
                assert last is not None
                last.then_inc(sem, 1)

            self.q[e].append(thunk)
            return (e, self.cnt[e])
        self.q[e].append(lambda eng, f=f: f(eng))
        return None

    def dma(self, out, in_, **kw):
        si = self.dma_i % self.N_DSEM
        self.dma_i += 1
        sem = self.sems[f"d{si}"]
        prev = self.dcnt[si]
        if prev > 0:
            self.q["sync"].append(lambda eng, sem=sem, v=prev: eng.wait_ge(sem, v))
        self.dcnt[si] = prev + 16
        self.q["sync"].append(
            lambda eng, o=out, i=in_, kw=kw, sem=sem: eng.dma_start(out=o, in_=i, **kw).then_inc(sem, 16))
        return (f"d{si}", self.dcnt[si])

    def mile(self, e):
        return (e, self.cnt[e])

    def self_sync(self, e):
        sem = self.sems[e]
        v = self.cnt[e]
        if v > 0:
            self.q[e].append(lambda eng, sem=sem, v=v: eng.wait_ge(sem, v))

    def fence(self):
        prods = [(p, self.cnt[p]) for p in ("pe", "act", "ve", "pool")]
        prods += [(f"d{i}", self.dcnt[i]) for i in range(self.N_DSEM)]
        for e in ENGINES:
            self.wait(e, *prods)


def bview(ap, inner):
    return ap.unsqueeze(2).broadcast_to([ap.shape[0], ap.shape[1], inner])


def pslots(ps, pitch, n, width, off=0):
    return bass.AP(tensor=ps.tensor, offset=ps.offset + off,
                   ap=[list(ps.ap[0]), [pitch, n], [1, width]])


def build_program(parts=None):
    if parts is None:
        parts = PARTS
    nc = bass.Bass("TRN2", target_bir_lowering=False)

    d_x = nc.dram_tensor("x_fm", [DM, T], F32R, kind="ExternalInput").ap()
    d_out = nc.dram_tensor("x_out", [DM, T], F32, kind="ExternalOutput").ap()
    d_qkmb = nc.dram_tensor("wqkv_qk_mb", [16 * 128, DM], F32, kind="ExternalInput").ap()
    d_wqkv_v = nc.dram_tensor("wqkv_v", [DM, DM], F32, kind="ExternalInput").ap()
    d_proj = nc.dram_tensor("projm", [128, 64], F32, kind="ExternalInput").ap()
    d_woutmb = nc.dram_tensor("wout_mb", [8 * 128, DM], F32, kind="ExternalInput").ap()
    d_wqmb = nc.dram_tensor("wq_mb", [8 * 128, DM], F32, kind="ExternalInput").ap()
    d_wkmb = nc.dram_tensor("wk_mb", [8 * 128, DM], F32, kind="ExternalInput").ap()
    d_wv = nc.dram_tensor("Wv", [DM, DM], F32, kind="ExternalInput").ap()
    d_womb = nc.dram_tensor("wo_mb", [8 * 128, DM], F32, kind="ExternalInput").ap()
    d_wcat = nc.dram_tensor("Wcat", [DM, 3 * DM], F32, kind="ExternalInput").ap()
    d_ocatmb = nc.dram_tensor("ocat_mb", [8 * 128, DM], F32, kind="ExternalInput").ap()
    d_sim = nc.dram_tensor("simc", [128, NC8 * 16], F32, kind="ExternalInput").ap()
    d_gates = nc.dram_tensor("gatesr", [1, 16], F32, kind="ExternalInput").ap()
    d_cb = nc.dram_tensor("biascols", [128, 4 * NC8], F32, kind="ExternalInput").ap()
    d_bv = nc.dram_tensor("bvrow", [1, DM], F32, kind="ExternalInput").ap()
    d_ident = nc.dram_tensor("identc", [128, 128], F32, kind="ExternalInput").ap()
    d_maska = nc.dram_tensor("maska", [128, 3 * 384], F32, kind="ExternalInput").ap()
    d_tri = nc.dram_tensor("tric", [128, 128], F32, kind="ExternalInput").ap()
    d_sel = nc.dram_tensor("selc", [16, NC8 * 128], F32, kind="ExternalInput").ap()
    d_onesr = nc.dram_tensor("onesr", [128, 2], F32R, kind="ExternalInput").ap()
    d_ones1 = nc.dram_tensor("ones1", [1, 128], F32, kind="ExternalInput").ap()
    d_ones1r = nc.dram_tensor("ones1r", [1, 128], F32R, kind="ExternalInput").ap()
    d_idf1 = nc.dram_tensor("idf1", [1, 1], F32, kind="ExternalInput").ap()

    stack = ExitStack()
    k = KB(nc)

    with stack:
        k.make_sems(stack)

        _names = {}

        def _uniq(name):
            n = _names.get(name, 0)
            _names[name] = n + 1
            return name if n == 0 else f"{name}_u{n}"

        def sb(name, shape, dt, st=stack):
            return st.enter_context(nc.sbuf_tensor(_uniq(name), shape, dt)).ap()

        DBGSC = [None]
        if DBG_WANT:
            DBGSC[0] = sb("dbg_scratch", [128, 2048], F32)

        def dbg(name, ap):
            if name not in DBG_WANT:
                return
            P, FR = ap.shape[0], ap.shape[1]
            dd = nc.dram_tensor("dbg_" + name, [P, FR], F32, kind="ExternalOutput").ap()
            k.fence()
            for lo in range(0, FR, 2048):
                w = min(2048, FR - lo)
                mcc = k.run("ve", lambda v, lo=lo, w=w, ap=ap, P=P: v.tensor_copy(
                    DBGSC[0][0:P, 0:w], ap[:, lo:lo + w]))
                k.wait("sync", mcc)
                md = k.dma(dd[:, lo:lo + w], DBGSC[0][0:P, 0:w])
                k.wait("ve", md)
            for e in ENGINES:
                for i in range(k.N_DSEM):
                    k.wait(e, (f"d{i}", k.dcnt[i]))

        X = sb("X", [128, NC8 * T], F32R)
        XF = X.bitcast(F32)
        IDENT = sb("IDENT", [128, 128], BF16)
        IDENTF = sb("IDENTF", [128, 128], F32)
        TRI = sb("TRI", [128, 128], BF16)
        TRIF = sb("TRIF", [128, 128], F32)
        ONESR = sb("ONESR", [128, 2], F32R)
        ONES1 = sb("ONES1", [1, 128], F32)
        ONES1R = sb("ONES1R", [1, 128], F32R)
        IDF1 = sb("IDF1", [1, 1], F32)
        CB = sb("CB", [128, 4 * NC8], F32)
        SIMC = sb("SIMC", [128, NC8 * 16], F32)
        PROJB = sb("PROJB", [128, 64], BF16)
        SGB = sb("SGB", [128, 16], F32)
        SNB = sb("SNB", [128, 16], F32)
        BVB = sb("BVB", [128, DM], F32)
        GNRC = sb("GNRC", [128, 16], F32)
        EPS1 = sb("EPS1", [1, 1], F32)
        EPS8 = sb("EPS8", [128, 1], F32)

        m_in = [k.dma(X[:, kc * T:(kc + 1) * T], d_x[kc * 128:(kc + 1) * 128, :]) for kc in range(NC8)]
        m_c = [k.dma(IDENTF, d_ident), k.dma(TRIF, d_tri),
               k.dma(ONESR, d_onesr), k.dma(ONES1, d_ones1),
               k.dma(ONES1R, d_ones1r), k.dma(IDF1, d_idf1), k.dma(CB, d_cb),
               k.dma(SIMC, d_sim)]
        with ExitStack() as ps_:
            pb = lambda n, s, d=F32: ps_.enter_context(nc.psum_tensor(n, s, d)).ap()
            PP = pb("prep_ps", [128, 512])
            PROJ32 = ps_.enter_context(nc.sbuf_tensor("PROJ32", [128, 64], F32)).ap()
            BVROW = ps_.enter_context(nc.sbuf_tensor("BVROW", [1, DM], F32)).ap()
            GROW = ps_.enter_context(nc.sbuf_tensor("GROW", [1, 16], F32)).ap()
            SSQ = ps_.enter_context(nc.sbuf_tensor("SSQ", [128, NC8 * 16], F32)).ap()
            m_pj = k.dma(PROJ32, d_proj)
            m_bv = k.dma(BVROW, d_bv)
            m_g = k.dma(GROW, d_gates)
            k.wait("ve", m_pj, m_c[0], m_c[1])
            k.run("ve", lambda v: v.tensor_copy(IDENT, IDENTF), inc=False)
            k.run("ve", lambda v: v.tensor_copy(TRI, TRIF), inc=False)
            k.run("ve", lambda v: v.tensor_copy(PROJB, PROJ32), inc=False)
            k.run("ve", lambda v: v.memset(EPS1, 1e-5), inc=False)
            k.run("ve", lambda v: v.memset(EPS8, 1e-8), inc=False)
            k.wait("act", m_g)
            m1 = k.run("act", lambda s: s.activation(GROW, GROW, AF.Sigmoid))
            k.wait("pe", m1, m_c[3])
            m2 = k.run("pe", lambda t: t.matmul(PP[:, 0:16], ONES1, GROW, start=True, stop=True,
                                                skip_group_check=True))
            k.wait("ve", m2)
            m3 = k.run("ve", lambda v: v.tensor_copy(SGB, PP[:, 0:16]))
            k.wait("pool", m_c[7])
            m4 = k.run("pool", lambda g: g.tensor_mul(SSQ, SIMC, SIMC))
            k.wait("pe", m3, m4, m_c[2])

            def f_sn(t):
                last = None
                for kc in range(NC8):
                    last = t.matmul(PP[0:1, 64:80], ONESR.bitcast(F32)[:, 0:1],
                                    SSQ[:, kc * 16:(kc + 1) * 16],
                                    start=(kc == 0), stop=(kc == NC8 - 1), skip_group_check=True)
                return last
            m5 = k.run("pe", f_sn)
            k.wait("act", m5)
            m6 = k.run("act", lambda s: s.activation(GROW, PP[0:1, 64:80], AF.Sqrt))
            k.wait("ve", m6)

            GROW2 = ps_.enter_context(nc.sbuf_tensor("GROW2", [1, 16], F32)).ap()

            k.run("ve", lambda v: v.tensor_scalar_max(GROW, GROW, 1e-12))
            k.self_sync("ve")
            m7 = k.run("ve", lambda v: v.reciprocal(GROW2, GROW))
            k.wait("pe", m7)
            m8 = k.run("pe", lambda t: t.matmul(PP[:, 32:48], ONES1, GROW2, start=True, stop=True,
                                                skip_group_check=True))
            k.wait("ve", m8)
            m9 = k.run("ve", lambda v: v.tensor_copy(SNB, PP[:, 32:48]))
            last_m = m9
            for hf in range(2):
                k.wait("pe", last_m, m_bv)
                ma = k.run("pe", lambda t, hf=hf: t.matmul(PP[:, 0:512], ONES1,
                                                           BVROW[:, hf * 512:(hf + 1) * 512],
                                                           start=True, stop=True, skip_group_check=True))
                k.wait("ve", ma)
                last_m = k.run("ve", lambda v, hf=hf: v.tensor_copy(BVB[:, hf * 512:(hf + 1) * 512],
                                                                    PP[:, 0:512]))
            k.wait("ve", *m_in, *m_c)
            k.run("ve", lambda v: v.engine_nop())
        k.fence()
        dbg("sgb", SGB)
        dbg("snb", SNB)
        dbg("c_ident", IDENT)
        dbg("c_tri", TRI)
        dbg("x0", X)

        def make_ln_scope(tag, ls, gating=False, nbuf=2):
            d = {}
            d["SQ"] = [sb(f"{tag}_sq{i}", [128, 512], F32R, ls) for i in range(nbuf)]
            d["TT"] = [sb(f"{tag}_tt0", [128, 512], F32, ls)]
            if gating:
                d["XSC"] = [sb(f"{tag}_xsc{i}", [128, 512], F32, ls) for i in range(2)]
            d["MR"] = sb(f"{tag}_mr", [1, 512], F32, ls)
            d["VRr"] = sb(f"{tag}_vr", [1, 512], F32, ls)
            d["RS"] = sb(f"{tag}_rs", [1, 512], F32R, ls)
            d["NM"] = sb(f"{tag}_nm", [1, 512], F32R, ls)
            if gating:
                d["G1"] = sb(f"{tag}_g1", [1, 512], F32, ls)
            stat = ls.enter_context(nc.psum_tensor(_uniq(f"{tag}_stat"), [128, 512], F32)).ap()
            stat2 = ls.enter_context(nc.psum_tensor(_uniq(f"{tag}_stat2"), [128, 512], F32)).ap()
            d["ps_s0"] = stat[0:1, :]
            d["ps_s1"] = stat2[0:1, :]
            d["ps_s0mm"] = stat[0:2, :]
            d["ps_s1mm"] = stat2[0:2, :]
            d["ps_ab"] = ls.enter_context(nc.psum_tensor(_uniq(f"{tag}_ab"), [128, 512], F32)).ap()
            d["ps_am"] = ls.enter_context(nc.psum_tensor(_uniq(f"{tag}_am"), [128, 512], F32)).ap()
            if gating:
                d["ps_gn"] = ls.enter_context(nc.psum_tensor(_uniq(f"{tag}_gn"), [128, 512], F32)).ap()[:, 0:4]
            d["dbg_rows"] = True
            d["sq_m"] = [None] * nbuf
            d["tt_m"] = [None]
            d["xsc_m"] = [None, None]
            d["row_m"] = None
            d["bc_rd"] = None
            d["stat_rd"] = None
            return d

        def ln_tb(d, tb, xn_at, gating_cb=None, gnrc=False):
            ts = tb * 512
            m_st = None
            NB = len(d["SQ"])
            for kc in range(NC8):
                b = kc % NB
                sq_eng = "pool" if kc % 2 == 0 else "act"
                k.wait(sq_eng, d["sq_m"][b])
                if sq_eng == "pool":
                    def f_sq(g, kc=kc, b=b, ts=ts):
                        return g.tensor_mul(d["SQ"][b], X[:, kc * T + ts: kc * T + ts + 512],
                                            X[:, kc * T + ts: kc * T + ts + 512])
                else:
                    def f_sq(g, kc=kc, b=b, ts=ts):
                        return g.activation(d["SQ"][b], X[:, kc * T + ts: kc * T + ts + 512],
                                            AF.Square)
                m_sq = k.run(sq_eng, f_sq)
                k.wait("pe", m_sq, d["stat_rd"] if kc == 0 else None)

                def f_stats(t, kc=kc, b=b, ts=ts):
                    t.matmul(d["ps_s0mm"], ONESR, X[:, kc * T + ts: kc * T + ts + 512],
                             start=(kc == 0), stop=(kc == NC8 - 1), skip_group_check=True)
                    return t.matmul(d["ps_s1mm"], ONESR, d["SQ"][b],
                                    start=(kc == 0), stop=(kc == NC8 - 1), skip_group_check=True)
                m_st = k.run("pe", f_stats)
                d["sq_m"][b] = m_st
            k.wait("ve", m_st)

            def f_rows(v):
                v.tensor_single_scalar(d["MR"], d["ps_s0"], 1.0 / DM, op=OP.mult)
                v.tensor_mul(d["VRr"], d["MR"], d["MR"])
                return v.scalar_tensor_tensor(d["VRr"], d["ps_s1"], 1.0 / DM, d["VRr"],
                                              op0=OP.mult, op1=OP.subtract)
            m_rows = k.run("ve", f_rows)
            d["stat_rd"] = m_rows
            k.wait("act", m_rows)

            def f_rstd(s):
                s.activation(d["VRr"], d["VRr"], AF.Ln, bias=EPS1)
                return s.activation(d["RS"], d["VRr"], AF.Exp, scale=-0.5)
            m_rstd = k.run("act", f_rstd)
            k.wait("ve", m_rstd)
            m_nm = k.run("ve", lambda v: v.scalar_tensor_tensor(
                d["NM"], d["MR"], -1.0, d["RS"].bitcast(F32), op0=OP.mult, op1=OP.mult))
            d["row_m"] = m_nm
            if tb == 0 and d.get("dbg_rows"):
                dbg("mr", d["MR"]); dbg("vr", d["VRr"])
                dbg("rs", d["RS"].bitcast(F32)); dbg("nm", d["NM"].bitcast(F32))

            if gnrc:
                k.wait("act", m_rstd)
                mg1 = k.run("act", lambda s: s.activation(d["G1"], d["VRr"], AF.Exp))
                k.wait("ve", mg1)
                mg2 = k.run("ve", lambda v: v.tensor_single_scalar(d["G1"], d["G1"], 1e-5, op=OP.subtract))
                k.wait("act", mg2)
                mg3 = k.run("act", lambda s: s.activation(d["G1"], d["G1"], AF.Ln, scale=float(DM)))
                k.wait("ve", mg3)
                mg4 = k.run("ve", lambda v: v.tensor_sub(d["G1"], d["G1"], d["VRr"]))
                k.wait("act", mg4)
                mg5 = k.run("act", lambda s: s.activation(d["G1"], d["G1"], AF.Exp, scale=-0.5))
                tpm = None
                for tl in range(4):
                    k.wait("pe", mg5, tpm, d["row_m"] if tl == 0 else None)
                    mtp = k.run("pe", lambda t, tl=tl: t.transpose(
                        d["ps_gn"][:, tl:tl + 1], d["G1"][:, tl * 128:(tl + 1) * 128], IDF1))
                    tpm = mtp
                k.wait("ve", tpm)
                d["row_m"] = k.run("ve", lambda v, tb=tb: v.tensor_copy(
                    GNRC[:, tb * 4:(tb + 1) * 4], d["ps_gn"]))

            k.wait("pe", m_nm, d["bc_rd"])

            def f_bc(t):
                t.matmul(d["ps_ab"], ONES1R, d["RS"], start=True, stop=True,
                         skip_group_check=True)
                return t.matmul(d["ps_am"], ONES1R, d["NM"], start=True, stop=True,
                                skip_group_check=True)
            m_bc = k.run("pe", f_bc)

            last_ap = None
            NBT = len(d["TT"])
            for kc in range(NC8):
                b = kc % NBT
                k.wait("ve", m_bc, d["tt_m"][b])

                def f_a1(v, kc=kc, b=b, ts=ts):
                    return v.tensor_tensor(d["TT"][b], X[:, kc * T + ts: kc * T + ts + 512],
                                           d["ps_ab"], op=OP.mult)
                d["tt_m"][b] = k.run("ve", f_a1)
                if gating_cb is None:
                    def f_a2(v, kc=kc, b=b):
                        return v.tensor_tensor(xn_at(kc), d["TT"][b], d["ps_am"], op=OP.add)
                    d["tt_m"][b] = k.run("ve", f_a2)
                else:
                    bx = kc % 2
                    k.wait("ve", d["xsc_m"][bx])

                    def f_a2g(v, kc=kc, b=b, bx=bx):
                        return v.tensor_tensor(d["XSC"][bx], d["TT"][b], d["ps_am"], op=OP.add)
                    mx = k.run("ve", f_a2g)
                    d["tt_m"][b] = mx

                    def f_a3g(v, kc=kc, b=b, bx=bx):
                        return v.tensor_copy(xn_at(kc), d["XSC"][bx])
                    k.run("ve", f_a3g, inc=False)
                    d["xsc_m"][bx] = gating_cb(kc, d["XSC"][bx], mx)
                last_ap = d["tt_m"][b]
            d["bc_rd"] = later(*d["tt_m"])
            return last_ap

        def attn_a():
          with ExitStack() as a_scope:
              XN = sb("XN_a", [128, NC8 * T], BF16, a_scope)
              WV32 = [sb(f"a_wv{i}", [128, 512], F32, a_scope) for i in range(2)]
              WVB = sb("a_wvb", [128, NC8 * DM], BF16, a_scope)
              wvm = [None, None]
              cvm = []
              wv_dma = []

              def issue_vdma(kx):
                  kc, hf = kx // 2, kx % 2
                  k.wait("sync", wvm[hf])
                  wv_dma.append(k.dma(WV32[hf], d_wqkv_v[kc * 128:(kc + 1) * 128,
                                                         hf * 512:(hf + 1) * 512]))

              def emit_vcast(kx):
                  kc, hf = kx // 2, kx % 2
                  k.wait("act", wv_dma[kx], wvm[hf])
                  wvm[hf] = k.run("act", lambda s, kc=kc, hf=hf: s.activation(
                      WVB[:, kc * DM + hf * 512: kc * DM + (hf + 1) * 512], WV32[hf], AF.Copy))
                  cvm.append(wvm[hf])

              issue_vdma(0)
              issue_vdma(1)
              with ExitStack() as ls:
                  lnd = make_ln_scope("lna", ls, nbuf=4)
                  for tb in range(4):
                      ln_tb(lnd, tb, lambda kc, tb=tb: XN[:, kc * T + tb * 512: kc * T + (tb + 1) * 512])
                      for i_ in range(4):
                          kx = tb * 4 + i_
                          emit_vcast(kx)
                          if kx + 2 < 16:
                              issue_vdma(kx + 2)
                  k.fence()
              if "xnf" in DBG_WANT:
                  XNFDBG = sb("xnfdbg", [128, 2048], F32, a_scope)
                  dd = nc.dram_tensor("dbg_xnf", [128, 8 * 2048], F32, kind="ExternalOutput").ap()
                  k.fence()
                  for kc in range(NC8):
                      mcc = k.run("ve", lambda v, kc=kc: v.tensor_copy(XNFDBG, XN[:, kc * T:(kc + 1) * T]))
                      k.wait("sync", mcc)
                      md = k.dma(dd[:, kc * 2048:(kc + 1) * 2048] if False else dd.rearrange("p (c t) -> p c t", t=2048)[:, kc, :], XNFDBG)
                      for e in ENGINES:
                          k.wait(e, md)
              dbg("xn", XN)
              HW = 8
              def _half(H):
                  kt0 = max(0, 8 * H - 1)
                  kt1 = min(NT, 8 * H + 9)
                  nkt = kt1 - kt0
                  LT = nkt * 128
                  with ExitStack() as hs:
                      VAUG = sb("a_vaug", [128, nkt * HW * 129], BF16, hs)
                      PHI = sb("a_phi", [128, HW * LT], BF16, hs)
                      with ExitStack() as vs:
                          ps_v = vs.enter_context(nc.psum_tensor(_uniq("a_psv"), [128, 1024], F32)).ap()
                          mset = k.run("ve", lambda v: v.memset(VAUG, 1.0))
                          vv = None
                          for kti in range(nkt):
                              kt = kt0 + kti
                              k.wait("pe", vv, *cvm)

                              def f_vmm(t, kt=kt):
                                  last = None
                                  for hf in range(2):
                                      for kc in range(NC8):
                                          last = t.matmul(ps_v[:, hf * 512:(hf + 1) * 512],
                                                          XN[:, kc * T + kt * 128: kc * T + (kt + 1) * 128],
                                                          WVB[:, kc * DM + hf * 512: kc * DM + (hf + 1) * 512],
                                                          start=(kc == 0), stop=(kc == NC8 - 1),
                                                          skip_group_check=True)
                                  return last
                              mv = k.run("pe", f_vmm)
                              k.wait("ve", mv, mset)
                              dstv = VAUG.rearrange("p (kt h s) -> p kt h s", h=HW, s=129)[:, kti, :, 0:128]
                              vv = k.run("ve", lambda v, dstv=dstv: v.tensor_copy(
                                  dstv, ps_v.rearrange("p (h s) -> p h s", s=128)))
                          k.fence()
                          if H == 0:
                              dbg('vaug', VAUG)
                      with ExitStack() as qs:
                          QH = sb("a_qh", [128, HW * LT], BF16, qs)
                          KH = sb("a_kh", [128, HW * LT], BF16, qs)
                          WST = [sb(f"a_wst{i}", [128, DM], F32, qs) for i in range(2)]
                          WBB = [w.bitcast(BF16).rearrange("p (n two) -> p n two", two=2)[:, :, 1]
                                 for w in WST]
                          ps_qk = [qs.enter_context(nc.psum_tensor(_uniq(f"a_psqk{i}"), [128, 512], F32)).ap()
                                   for i in range(2)]
                          blks = [(0, 512), (512, 512), (1024, 128)]
                          wrd = [None, None]
                          evm = [None, None]
                          for mb in range(16):
                              b = mb % 2
                              dst = QH if mb < 8 else KH
                              hh = mb % 8
                              k.wait("sync", wrd[b])
                              md = k.dma(WST[b], d_qkmb[mb * 128:(mb + 1) * 128, :])
                              for bi, (lo, ln_) in enumerate(blks):
                                  pbk = bi % 2
                                  k.wait("pe", md, evm[pbk])

                                  def f_mm(t, b=b, lo=lo, ln_=ln_, pbk=pbk):
                                      last = None
                                      for kc in range(NC8):
                                          last = t.matmul(ps_qk[pbk][:, 0:ln_],
                                                          WBB[b][:, kc * 128:(kc + 1) * 128],
                                                          XN[:, kc * T + kt0 * 128 + lo: kc * T + kt0 * 128 + lo + ln_],
                                                          start=(kc == 0), stop=(kc == NC8 - 1),
                                                          skip_group_check=True)
                                      return last
                                  mm = k.run("pe", f_mm)
                                  wrd[b] = mm
                                  k.wait("ve", mm)
                                  evm[pbk] = k.run("ve", lambda v, dst=dst, hh=hh, lo=lo, ln_=ln_, pbk=pbk:
                                                   v.tensor_copy(dst[:, hh * LT + lo: hh * LT + lo + ln_],
                                                                 ps_qk[pbk][:, 0:ln_]))
                          ps_ph3 = qs.enter_context(nc.psum_tensor(_uniq("a_psph3"), [128, 512], F32)).ap()
                          PS3 = [ps_qk[0], ps_qk[1], ps_ph3]
                          E1 = [qs.enter_context(nc.psum_tensor(_uniq(f"a_e1p{i}"), [128, 512], F32)).ap()
                                for i in range(3)]
                          E2 = [sb(f"a_e2{i}", [128, 512], F32, qs) for i in range(2)]
                          e2_rd = [None, None]
                          phm = [evm[0], evm[1], None]
                          for hh in range(HW):
                              for bi, (lo, ln_) in enumerate(blks):
                                  pbk = (hh * 3 + bi) % 3
                                  k.wait("pe", phm[pbk],
                                         later(evm[0], evm[1]) if hh == 0 and bi == 0 else None)

                                  def f_phi(t, hh=hh, lo=lo, ln_=ln_, pbk=pbk):
                                      t.matmul(PS3[pbk][0:64, 0:ln_], PROJB,
                                               QH[:, hh * LT + lo: hh * LT + lo + ln_],
                                               start=True, stop=True, skip_group_check=True)
                                      return t.matmul(PS3[pbk][64:128, 0:ln_], PROJB,
                                                      KH[:, hh * LT + lo: hh * LT + lo + ln_],
                                                      start=True, stop=True, skip_group_check=True)
                                  mph = k.run("pe", f_phi)
                                  k.wait("ve", mph)
                                  me1 = k.run("ve", lambda v, pbk=pbk, ln_=ln_: v.tensor_single_scalar(
                                      E1[pbk][:, 0:ln_], PS3[pbk][:, 0:ln_], 0.0, op=OP.min))
                                  eb = (hh * 3 + bi) % 2
                                  k.wait("act", me1, e2_rd[eb])
                                  me2 = k.run("act", lambda s, pbk=pbk, eb=eb, ln_=ln_: s.activation(
                                      E2[eb][:, 0:ln_], E1[pbk][:, 0:ln_], AF.Exp))
                                  k.wait("ve", me2)
                                  phm[pbk] = k.run("ve", lambda v, hh=hh, lo=lo, ln_=ln_, pbk=pbk, eb=eb:
                                                   v.scalar_tensor_tensor(
                                                       PHI[:, hh * LT + lo: hh * LT + lo + ln_],
                                                       PS3[pbk][:, 0:ln_], 0.0, E2[eb][:, 0:ln_],
                                                       op0=OP.max, op1=OP.add))
                                  e2_rd[eb] = phm[pbk]
                          k.fence()
                          if H == 0:
                              dbg('qh', QH)
                              dbg('kh', KH)
                              dbg('phi', PHI)
                      with ExitStack() as ss:
                          MASKA = sb(f"maska{H}", [128, 3 * 384], F32, ss)
                          m_ma = k.dma(MASKA, d_maska)
                          k.wait("ve", m_ma)
                          PHIK0 = sb("a_phik0", [64, HW * LT], BF16, ss)
                          mpk = k.run("ve", lambda v: v.tensor_copy(PHIK0, PHI[64:128, :]))
                          k.wait("pe", mpk)
                          STM = [sb(f"a_stm{i}", [128, 384], BF16, ss) for i in range(2)]
                          ATG = [sb(f"a_atg{i}", [128, 3 * 128], BF16, ss) for i in range(2)]
                          Z3 = sb("a_z3", [128, 3], F32, ss)
                          Z3R = sb("a_z3r", [128, 3], F32, ss)
                          ATTNH = sb("a_attnh", [128, HW * 512], BF16, ss)
                          WOC32 = [sb(f"a_woc32{i}", [128, DM], F32, ss) for i in range(2)]
                          WOC = [sb(f"a_woc{i}", [128, DM], BF16, ss) for i in range(2)]
                          ps_st = [ss.enter_context(nc.psum_tensor(_uniq(f"a_psst{i}"), [128, 384], F32)).ap()
                                   for i in range(2)]
                          ps_av = [ss.enter_context(nc.psum_tensor(_uniq(f"a_psav{i}"), [128, 512], F32)).ap()
                                   for i in range(2)]
                          ps_tp = [ss.enter_context(nc.psum_tensor(_uniq(f"a_pstp{i}"), [128, 128], BF16)).ap()
                                   for i in range(2)]
                          ps_o = [ss.enter_context(nc.psum_tensor(_uniq(f"a_pso{i}"), [128, 512], F32)).ap()
                                  for i in range(2)]
                          mz0 = k.run("ve", lambda v: v.memset(ps_st[0], 0.0))
                          mz1 = k.run("ve", lambda v: v.memset(ps_st[1], 0.0))
                          k.wait("pe", mz0, mz1)
                          if H == 0:
                              def f_bias(v):
                                  last = None
                                  for kc in range(NC8):
                                      last = v.tensor_scalar_add(X[:, kc * T:(kc + 1) * T],
                                                                 X[:, kc * T:(kc + 1) * T],
                                                                 CB[:, kc:kc + 1])
                                  return last
                              k.run("ve", f_bias)
                          st_rd = [None, None]
                          woc_rd = [None, None]
                          stw = [None, None]
                          atg_rd = [None, None]
                          av_rd = [None, None]
                          tp_rd = [None, None]
                          attnh_rd = None
                          wocm = [None, None]
                          orm = [None, None]
                          for tg in range(2):
                              taus = [8 * H + tg * 4 + i for i in range(4)]
                              groups = [taus[0:3], taus[3:4]]
                              units = []
                              for hh in range(HW):
                                  for gi, grp in enumerate(groups):
                                      for si, tau in enumerate(grp):
                                          units.append((hh, gi, grp, si, tau))
                              NU = len(units)
                              mstm_l = [None] * NU
                              pend = [None]

                              def flush_pend(tg=tg):
                                  if pend[0] is None:
                                      return
                                  hh, gb, grp, msc2 = pend[0]
                                  pend[0] = None
                                  for si, tau in enumerate(grp):
                                      tpb = tau % 2
                                      k.wait("pe", msc2 if si == 0 else None, tp_rd[tpb])
                                      mtp = k.run("pe", lambda t, gb=gb, si=si, tpb=tpb: t.transpose(
                                          ps_tp[tpb], ATG[gb][:, si * 128:(si + 1) * 128], IDENT))
                                      atg_rd[gb] = mtp
                                      k.wait("ve", mtp, attnh_rd)
                                      tp_rd[tpb] = k.run("ve", lambda v, hh=hh, tau=tau, tpb=tpb, tg=tg:
                                                         v.tensor_copy(
                                                             ATTNH[:, hh * 512 + (tau - 8 * H - tg * 4) * 128:
                                                                   hh * 512 + (tau - 8 * H - tg * 4 + 1) * 128],
                                                             ps_tp[tpb]))

                              def emit_s(u):
                                  hh, gi, grp, si, tau = units[u]
                                  kappas = [x for x in (tau - 1, tau, tau + 1) if 0 <= x < NT]
                                  stb = tau % 2
                                  k.wait("pe", st_rd[stb], stw[stb])

                                  def f_st(t, hh=hh, tau=tau, kappas=kappas, stb=stb):
                                      last = None
                                      for j, kp in enumerate(kappas):
                                          js = kp - (tau - 1)
                                          last = t.matmul(
                                              ps_st[stb][:, js * 128:(js + 1) * 128],
                                              PHIK0[:, hh * LT + (kp - kt0) * 128: hh * LT + (kp - kt0 + 1) * 128],
                                              PHI[0:64, hh * LT + (tau - kt0) * 128: hh * LT + (tau - kt0 + 1) * 128],
                                              start=(j == 0), stop=(j == len(kappas) - 1),
                                              skip_group_check=True)
                                      return last
                                  mst = k.run("pe", f_st)
                                  mvar = 0 if tau == 0 else (2 if tau == NT - 1 else 1)
                                  k.wait("ve", mst, st_rd[stb])
                                  mstm_l[u] = k.run("ve", lambda v, stb=stb, mvar=mvar: v.tensor_tensor(
                                      STM[stb], ps_st[stb], MASKA[:, mvar * 384:(mvar + 1) * 384],
                                      op=OP.mult))
                                  stw[stb] = mstm_l[u]

                              emit_s(0)
                              for u in range(NU):
                                  hh, gi, grp, si, tau = units[u]
                                  if u + 1 < NU:
                                      emit_s(u + 1)
                                  gb = gi % 2
                                  stb = tau % 2
                                  kappas = [x for x in (tau - 1, tau, tau + 1) if 0 <= x < NT]
                                  k.wait("pe", mstm_l[u], av_rd[gb] if si == 0 else None)

                                  def f_av(t, kappas=kappas, tau=tau, stb=stb, si=si, gb=gb, hh=hh):
                                      last = None
                                      for j, kp in enumerate(kappas):
                                          last = t.matmul(
                                              ps_av[gb][:, si * 170: si * 170 + 129],
                                              STM[stb][:, (kp - (tau - 1)) * 128:(kp - (tau - 1) + 1) * 128],
                                              VAUG[:, (kp - kt0) * (HW * 129) + hh * 129:
                                                   (kp - kt0) * (HW * 129) + hh * 129 + 129],
                                              start=(si == 0 and j == 0), stop=False,
                                              skip_group_check=True)
                                      return last
                                  avm = k.run("pe", f_av)
                                  st_rd[stb] = avm
                                  if si == len(grp) - 1:
                                      flush_pend()
                                      ng = len(grp)
                                      k.wait("act", avm, atg_rd[gb])
                                      k.run("act", lambda s, gb=gb, ng=ng: s.activation(
                                          Z3[:, 0:ng], pslots(ps_av[gb], 170, ng, 1, 128),
                                          AF.Ln, bias=EPS8))
                                      mzr = k.run("act", lambda s, ng=ng: s.activation(
                                          Z3R[:, 0:ng], Z3[:, 0:ng], AF.Exp, scale=-1.0))
                                      k.wait("ve", avm, mzr)

                                      def f_sc(v, gb=gb, ng=ng):
                                          return v.tensor_tensor(
                                              ATG[gb].rearrange("p (s q) -> p s q", q=128)[:, 0:ng, :],
                                              pslots(ps_av[gb], 170, ng, 128),
                                              bview(Z3R[:, 0:ng], 128), op=OP.mult)
                                      msc2 = k.run("ve", f_sc)
                                      av_rd[gb] = msc2
                                      pend[0] = (hh, gb, grp, msc2)
                              flush_pend()
                              if H == 0 and tg == 0:
                                  dbg('attnh', ATTNH)
                              tglob = 8 * H * 128 + tg * 512
                              last_evs = []
                              for c in range(NC8):
                                  b = c % 2
                                  k.wait("sync", wocm[b])
                                  mdw = k.dma(WOC32[b], d_woutmb[c * 128:(c + 1) * 128, :])
                                  k.wait("act", mdw, wocm[b], woc_rd[b])
                                  wocm[b] = k.run("act", lambda s, b=b: s.activation(WOC[b], WOC32[b], AF.Copy))
                                  k.wait("pe", wocm[b], orm[b], tp_rd[0], tp_rd[1])

                                  def f_omm(t, c=c, b=b):
                                      last = None
                                      for kc in range(NC8):
                                          last = t.matmul(ps_o[b], WOC[b][:, kc * 128:(kc + 1) * 128],
                                                          ATTNH[:, kc * 512:(kc + 1) * 512],
                                                          start=(kc == 0), stop=(kc == NC8 - 1),
                                                          skip_group_check=True)
                                      return last
                                  mo = k.run("pe", f_omm)
                                  woc_rd[b] = mo
                                  k.wait("ve", mo)
                                  orm[b] = k.run("ve", lambda v, c=c, tglob=tglob, b=b: v.tensor_tensor(
                                      X[:, c * T + tglob: c * T + tglob + 512],
                                      X[:, c * T + tglob: c * T + tglob + 512], ps_o[b], op=OP.add))
                                  last_evs.append(orm[b])
                              attnh_rd = None
                              k.wait("ve", *last_evs)
                          k.fence()

              for H in range(2):
                  _half(H)

        def ffn(app):
            with ExitStack() as ls:
                XNF = sb(f"f{app}_xn", [128, NC8 * T], BF16, ls)
                RW = sb(f"f{app}_rw", [128, 256], F32, ls)
                RWB16 = sb(f"f{app}_rwb", [128, 256], BF16, ls)
                RWT = sb(f"f{app}_rwt", [16, T], BF16, ls)
                QKFQ = sb(f"f{app}_qkfq", [64, T], BF16, ls)
                QKFK = sb(f"f{app}_qkfk", [64, T], BF16, ls)
                VAUGF = sb(f"f{app}_vaugf", [128, NT * 65], BF16, ls)

                def v3(ap, e=16):
                    return ap.rearrange("p (s e) -> p s e", e=e)

                wc_cms = [nc.sbuf_tensor(_uniq(f"f{app}_wc{i}"), [128, 1536], F32) for i in range(2)]
                WC32 = [cm.__enter__().ap() for cm in wc_cms]
                wcb_cm = nc.sbuf_tensor(_uniq(f"f{app}_wcb"), [128, NC8 * 3 * DM], BF16)
                WCB = wcb_cm.__enter__().ap()
                wcm = [None, None]
                wc_all = []
                wc_dma = []
                lnm = {}

                def issue_wdma(kidx):
                    kc, hf = kidx // 2, kidx % 2
                    k.wait("sync", wcm[hf])
                    wc_dma.append(k.dma(WC32[hf], d_wcat[kc * 128:(kc + 1) * 128,
                                                         hf * 1536:(hf + 1) * 1536]))

                def emit_wcast(kidx):
                    kc, hf = kidx // 2, kidx % 2
                    k.wait("act", wc_dma[kidx], wcm[hf])
                    wcm[hf] = k.run("act", lambda s, kc=kc, hf=hf: s.activation(
                        WCB[:, kc * 3 * DM + hf * 1536: kc * 3 * DM + (hf + 1) * 1536],
                        WC32[hf], AF.Copy))
                    wc_all.append(wcm[hf])

                issue_wdma(0)
                issue_wdma(1)

                with ExitStack() as pa:
                    lnd = make_ln_scope(f"lnf{app}", pa, gating=True, nbuf=4)
                    LG = sb(f"f{app}_lg", [128, 64], F32, pa)
                    SC1 = sb(f"f{app}_sc1", [128, 64], F32, pa)
                    SC2 = sb(f"f{app}_sc2", [128, 64], F32, pa)
                    SC3 = sb(f"f{app}_sc3", [128, 64], F32, pa)
                    M1 = sb(f"f{app}_m1", [128, 4], F32, pa)
                    M2 = sb(f"f{app}_m2", [128, 4], F32, pa)
                    A1 = sb(f"f{app}_a1", [128, 4], F32, pa)
                    NA = sb(f"f{app}_na", [128, 4], F32, pa)
                    ps_gps = [pa.enter_context(nc.psum_tensor(_uniq(f"f{app}_gps{i}"), [128, 512], F32)).ap()[:, 0:64]
                              for i in range(2)]
                    gate_first = {}
                    gps_rd = [None, None]
                    for tb in range(4):
                        gp = tb % 2
                        gate_first[tb] = [True]

                        def gating_cb(kc, xsc, mx, tb=tb, gp=gp):
                            k.wait("pe", mx, gps_rd[gp] if kc == 0 else None)

                            def f_g(t, kc=kc, xsc=xsc, tb=tb, gp=gp):
                                last = None
                                for tl in range(4):
                                    last = t.matmul(ps_gps[gp][:, tl * 16:(tl + 1) * 16],
                                                    xsc[:, tl * 128:(tl + 1) * 128],
                                                    SIMC[:, kc * 16:(kc + 1) * 16],
                                                    start=gate_first[tb][0], stop=False,
                                                    skip_group_check=True)
                                    gate_first[tb][0] = False
                                return last
                            return k.run("pe", f_g)

                        ln_tb(lnd, tb, lambda kc, tb=tb: XNF[:, kc * T + tb * 512: kc * T + (tb + 1) * 512],
                              gating_cb=gating_cb, gnrc=True)
                        m_gmm = k.mile("pe")
                        if app == 0:
                            dbg(f"fgps{tb}", ps_gps[gp])
                        k.wait("ve", m_gmm)

                        def vstep(f, sync=False):
                            m = k.run("ve", f)
                            if sync:
                                k.self_sync("ve")
                            return m

                        gv = ps_gps[gp].rearrange("p (s e) -> p s e", e=16)
                        vstep(lambda v, gv=gv: v.tensor_tensor(v3(LG), gv, SNB.unsqueeze(1).broadcast_to([128, 4, 16]), op=OP.mult))
                        if tb == 0 and app == 0:
                            dbg("g_s1o", LG)
                        vstep(lambda v, tb=tb: v.tensor_tensor(v3(LG), v3(LG), bview(GNRC[:, tb * 4:(tb + 1) * 4], 16), op=OP.mult))
                        if tb == 0 and app == 0:
                            dbg("g_s2o", LG)
                        vstep(lambda v: v.tensor_tensor(v3(LG), v3(LG), SGB.unsqueeze(1).broadcast_to([128, 4, 16]), op=OP.subtract))
                        if tb == 0 and app == 0:
                            dbg("g_logits", LG)
                        vstep(lambda v: v.reduce_max(M1, v3(LG), axis=AX.X), sync=True)
                        vstep(lambda v: v.tensor_single_scalar(SC1, LG, 0.0, op=OP.is_gt))
                        vstep(lambda v: v.tensor_tensor(v3(SC2), v3(LG), bview(M1, 16), op=OP.is_ge))
                        vstep(lambda v: v.scalar_tensor_tensor(v3(SC3), v3(SC2), -1e9, v3(LG), op0=OP.mult, op1=OP.add))
                        vstep(lambda v: v.reduce_max(M2, v3(SC3), axis=AX.X), sync=True)
                        vstep(lambda v: v.tensor_tensor(v3(SC2), v3(LG), bview(M2, 16), op=OP.is_ge))
                        vstep(lambda v: v.tensor_single_scalar(A1, M1, 0.0, op=OP.is_gt))
                        vstep(lambda v: v.tensor_scalar(NA, A1, -1.0, 1.0, op0=OP.mult, op1=OP.add))
                        vstep(lambda v: v.tensor_tensor(v3(SC1), v3(SC1), bview(A1, 16), op=OP.mult))
                        vstep(lambda v: v.tensor_tensor(v3(SC3), v3(SC2), bview(NA, 16), op=OP.mult))
                        vstep(lambda v: v.tensor_add(SC1, SC1, SC3))
                        if tb == 3 and app == 0:
                            dbg("g_sc1f", SC1)
                            dbg("g_m1f", M1)
                            dbg("g_m2f", M2)
                            dbg("g_sc2f", SC2)
                        vstep(lambda v: v.tensor_single_scalar(M1, M1, 0.0, op=OP.max))
                        vstep(lambda v: v.tensor_single_scalar(LG, LG, 0.0, op=OP.max))
                        mgc = vstep(lambda v: v.tensor_tensor(v3(LG), v3(LG), bview(M1, 16), op=OP.subtract))
                        k.wait("act", mgc)
                        mge = k.run("act", lambda s: s.activation(LG, LG, AF.Exp))
                        k.wait("ve", mge)
                        vstep(lambda v: v.tensor_mul(LG, LG, SC1))
                        vstep(lambda v: v.reduce_sum(M2, v3(LG), axis=AX.X), sync=True)
                        vstep(lambda v: v.reciprocal(M1, M2), sync=True)
                        vstep(lambda v, tb=tb: v.tensor_tensor(v3(RW[:, tb * 64:(tb + 1) * 64]), v3(LG), bview(M1, 16), op=OP.mult))
                        gps_rd[gp] = vstep(lambda v, tb=tb: v.tensor_copy(RWB16[:, tb * 64:(tb + 1) * 64], RW[:, tb * 64:(tb + 1) * 64]))
                        lnm[tb] = k.mile("ve")
                        for i_ in range(4):
                            kidx = tb * 4 + i_
                            emit_wcast(kidx)
                            if kidx + 2 < 16:
                                issue_wdma(kidx + 2)
                        if tb == 3 and app == 0:
                            dbg("g_lg", LG); dbg("g_sc1", SC1); dbg("g_m1", M1); dbg("g_m2", M2)
                    dbg(f"frw{app}", RW)
                    dbg(f"fgnrc{app}", GNRC)
                    gps_done = k.mile("ve")

                with ExitStack() as pb:
                    QCT = [sb(f"f{app}_qct{i}", [128, 1536], BF16, pb) for i in range(4)]
                    DIAG2 = [sb(f"f{app}_diag{i}", [128, 16 * 128], BF16, pb) for i in range(2)]
                    QKT = sb(f"f{app}_qkt", [128, 128], BF16, pb)
                    ps_qc = [pb.enter_context(nc.psum_tensor(_uniq(f"f{app}_psqc{i}"), [128, 1536], F32)).ap()
                             for i in range(2)]
                    ps_qkvb = pb.enter_context(nc.psum_tensor(_uniq(f"f{app}_psqkv"), [128, 512], F32)).ap()
                    ps_qkv = ps_qkvb[:, 0:192]
                    ps_mis = pb.enter_context(nc.psum_tensor(_uniq(f"f{app}_psmis"), [128, 1024], BF16)).ap()
                    ps_tp = ps_mis[:, 0:128]
                    ps_rt = ps_mis[0:16, 128:256]

                    msetf = k.run("ve", lambda v: v.memset(VAUGF, 1.0))

                    qct_rd = [None, None, None, None]
                    qc_rd = [None, None]
                    diag_rd = [None, None]
                    qkv_rd = None
                    tp_rd = None
                    evs = {}
                    mdg_l = {}

                    def emit_qc(tau):
                        for hf in range(2):
                            si = (tau * 2 + hf) % 4
                            qb = hf
                            k.wait("pe", qc_rd[qb], lnm[tau // 4] if hf == 0 else None,
                                   *( wc_all + [gps_done] if tau == 0 and hf == 0 else []))

                            def f_qc(t, tau=tau, hf=hf, qb=qb):
                                last = None
                                for kc in range(NC8):
                                    for nb in range(3):
                                        last = t.matmul(
                                            ps_qc[qb][:, nb * 512:(nb + 1) * 512],
                                            XNF[:, kc * T + tau * 128: kc * T + (tau + 1) * 128],
                                            WCB[:, kc * 3 * DM + hf * 1536 + nb * 512:
                                                kc * 3 * DM + hf * 1536 + (nb + 1) * 512],
                                            start=(kc == 0), stop=(kc == NC8 - 1),
                                            skip_group_check=True)
                                return last
                            mqc = k.run("pe", f_qc)
                            k.wait("ve", mqc, qct_rd[si])
                            qc_rd[qb] = k.run("ve", lambda v, si=si, qb=qb: v.tensor_copy(QCT[si], ps_qc[qb]))
                            qct_rd[si] = qc_rd[qb]
                            evs.setdefault(tau, []).append(qc_rd[qb])

                    def emit_diag(tau):
                        db_ = tau % 2
                        k.wait("ve", diag_rd[db_])
                        mdg_l[tau] = k.run("ve", lambda v, tau=tau, db_=db_: v.tensor_tensor(
                            DIAG2[db_].rearrange("p (e q) -> p e q", q=128),
                            IDENT.unsqueeze(1).broadcast_to([128, 16, 128]),
                            bview(RW[:, tau * 16:(tau + 1) * 16], 128), op=OP.mult))

                    emit_qc(0)
                    emit_diag(0)
                    for tau in range(NT):
                        tb, tl = tau // 4, tau % 4
                        if tau + 1 < NT:
                            emit_qc(tau + 1)
                            emit_diag(tau + 1)
                        db_ = tau % 2
                        k.wait("pe", mdg_l[tau], *evs[tau], qkv_rd)

                        def f_qkv(t, tau=tau, db_=db_):
                            last = None
                            for e in range(16):
                                si = (tau * 2 + (e // 8)) % 4
                                eo = (e % 8) * 192
                                last = t.matmul(ps_qkv, DIAG2[db_][:, e * 128:(e + 1) * 128],
                                                QCT[si][:, eo: eo + 192],
                                                start=(e == 0), stop=(e == 15), skip_group_check=True)
                            return last
                        mqkv = k.run("pe", f_qkv)
                        diag_rd[db_] = mqkv
                        qct_rd[(tau * 2) % 4] = mqkv
                        qct_rd[(tau * 2 + 1) % 4] = mqkv
                        k.wait("act", mqkv, msetf if tau == 0 else None, tp_rd)

                        def f_ev(s, tau=tau):
                            s.activation(QKT, ps_qkv[:, 0:128], AF.Copy)
                            return s.activation(VAUGF[:, tau * 65: tau * 65 + 64],
                                                ps_qkv[:, 128:192], AF.Copy)
                        qkv_rd = k.run("act", f_ev)
                        k.wait("pe", qkv_rd, tp_rd)

                        def f_tp(t, tau=tau):
                            t.transpose(ps_tp[0:64, :], QKT[:, 0:64], IDENT)
                            t.transpose(ps_tp[64:128, :], QKT[:, 64:128], IDENT)
                            return t.transpose(ps_rt, RWB16[:, tau * 16:(tau + 1) * 16], IDENT)
                        mtp = k.run("pe", f_tp)
                        k.wait("act", mtp)

                        def f_tpe(s, tau=tau):
                            s.activation(QKFQ[:, tau * 128:(tau + 1) * 128], ps_tp[0:64, :], AF.Copy)
                            s.activation(QKFK[:, tau * 128:(tau + 1) * 128], ps_tp[64:128, :], AF.Copy)
                            return s.activation(RWT[:, tau * 128:(tau + 1) * 128], ps_rt, AF.Copy)
                        tp_rd = k.run("act", f_tpe)
                    dbg(f"fqkfq{app}", QKFQ)
                    dbg(f"fqkfk{app}", QKFK)
                    dbg(f"fvaug{app}", VAUGF)
                    dbg(f"frwt{app}", RWT)
                    k.fence()

                wcb_cm.__exit__(None, None, None)
                for cm_ in reversed(wc_cms):
                    cm_.__exit__(None, None, None)
                pf_cm = nc.sbuf_tensor(_uniq(f"f{app}_pf"), [128, T], BF16)
                PF = pf_cm.__enter__().ap()
                at2_cm = nc.sbuf_tensor(_uniq(f"f{app}_attn2"), [128, T], BF16)
                ATTN2 = at2_cm.__enter__().ap()

                with ExitStack() as pc:
                    ATG = sb(f"f{app}_atg", [128, 4 * 64], BF16, pc)
                    Z4 = sb(f"f{app}_z4", [128, 4], F32, pc)
                    Z4R = sb(f"f{app}_z4r", [128, 4], F32, pc)
                    ps_sc = pc.enter_context(nc.psum_tensor(_uniq(f"f{app}_pssc"), [128, 2048], F32)).ap()
                    ps_av = pc.enter_context(nc.psum_tensor(_uniq(f"f{app}_psav"), [128, 512], F32)).ap()
                    ps_atb = pc.enter_context(nc.psum_tensor(_uniq(f"f{app}_psat"), [128, 1024], BF16)).ap()
                    ps_at = ps_atb[0:64, 0:128]
                    mz_sc = k.run("ve", lambda v: v.memset(ps_sc, 0.0))
                    pf_rd = None
                    avz_rd = None
                    at2_rd = None
                    for tb in range(4):
                        for tl in range(4):
                            tau = tb * 4 + tl
                            nk = tau + 1
                            k.wait("pe", pf_rd, mz_sc if tau == 0 else None)

                            def f_scm(t, tau=tau, nk=nk):
                                last = None
                                for kp in range(nk):
                                    last = t.matmul(ps_sc[:, kp * 128:(kp + 1) * 128],
                                                    QKFK[:, kp * 128:(kp + 1) * 128],
                                                    QKFQ[:, tau * 128:(tau + 1) * 128],
                                                    start=(kp % 4 == 0), stop=(kp == nk - 1),
                                                    skip_group_check=True)
                                return last
                            msc = k.run("pe", f_scm)
                            nch = (nk + 3) // 4
                            for c in range(nch):
                                klo, khi = c * 4, min(nk, c * 4 + 4)
                                k.wait("act", msc if c == 0 else None, pf_rd if c == 0 else None)
                                mex = k.run("act", lambda s, klo=klo, khi=khi: s.activation(
                                    PF[:, klo * 128:khi * 128], ps_sc[:, klo * 128:khi * 128],
                                    AF.Exp, scale=0.125))
                                if khi == nk:
                                    k.wait("ve", mex)
                                    mex = k.run("ve", lambda v, tau=tau: v.tensor_tensor(
                                        PF[:, tau * 128:(tau + 1) * 128],
                                        PF[:, tau * 128:(tau + 1) * 128], TRI, op=OP.mult))
                                k.wait("pe", mex, avz_rd if tl == 0 and c == 0 else None)

                                def f_av(t, klo=klo, khi=khi, tl=tl):
                                    last = None
                                    for kp in range(klo, khi):
                                        last = t.matmul(ps_av[:, tl * 128: tl * 128 + 65],
                                                        PF[:, kp * 128:(kp + 1) * 128],
                                                        VAUGF[:, kp * 65:(kp + 1) * 65],
                                                        start=(tl == 0 and kp == 0), stop=False,
                                                        skip_group_check=True)
                                    return last
                                pf_rd = k.run("pe", f_av)
                        k.wait("ve", pf_rd)

                        k.run("ve", lambda v: v.tensor_copy(Z4, pslots(ps_av, 128, 4, 1, 64)))
                        k.self_sync("ve")
                        mz4 = k.run("ve", lambda v: v.reciprocal(Z4R, Z4))
                        k.self_sync("ve")

                        def f_zsc(v):
                            return v.tensor_tensor(ATG.rearrange("p (s q) -> p s q", q=64),
                                                   pslots(ps_av, 128, 4, 64), bview(Z4R, 64), op=OP.mult)
                        avz_rd = k.run("ve", f_zsc)
                        at_rd = None
                        for tl in range(4):
                            tau = tb * 4 + tl
                            k.wait("pe", avz_rd, at_rd)
                            mat = k.run("pe", lambda t, tl=tl: t.transpose(
                                ps_at, ATG[:, tl * 64:(tl + 1) * 64], IDENT))
                            k.wait("ve", mat, at2_rd)

                            def f_a2(v, tau=tau):
                                v.tensor_copy(ATTN2[0:64, tau * 128:(tau + 1) * 128], ps_at)
                                return v.tensor_copy(ATTN2[64:128, tau * 128:(tau + 1) * 128], ps_at)
                            at_rd = k.run("ve", f_a2)
                            at2_rd = at_rd
                        avz_rd = k.mile("pe")
                    dbg(f"fattn2{app}", ATTN2)
                    k.fence()

                with ExitStack() as pd:
                    SEL = sb(f"f{app}_sel", [16, NC8 * 128], BF16, pd)
                    SELF_ = sb(f"f{app}_self", [16, NC8 * 128], F32, pd)
                    m_self = k.dma(SELF_, d_sel)
                    k.wait("ve", m_self)
                    m_sel = k.run("ve", lambda v: v.tensor_copy(SEL, SELF_))
                    if app == 0:
                        dbg("c_sel", SEL)
                    k.wait("pe", m_sel)
                    OC32 = [sb(f"f{app}_oc{i}", [128, DM], F32, pd) for i in range(2)]
                    OCB = sb(f"f{app}_ocb", [128, NC8 * DM], BF16, pd)
                    MXD2 = [sb(f"f{app}_mxd{i}", [128, NC8 * 512], BF16, pd) for i in range(2)]
                    RWBB = sb(f"f{app}_rwbb", [128, 512], BF16, pd)
                    ps_rwb = [pd.enter_context(nc.psum_tensor(_uniq(f"f{app}_psrwb{i}"), [128, 512], F32)).ap()
                              for i in range(2)]
                    ps_o = [pd.enter_context(nc.psum_tensor(_uniq(f"f{app}_pso{i}"), [128, 512], F32)).ap()
                            for i in range(2)]
                    ocm = [None, None]
                    oc_all = [m_sel]
                    for mb in range(8):
                        b = mb % 2
                        k.wait("sync", ocm[b])
                        mdo = k.dma(OC32[b], d_ocatmb[mb * 128:(mb + 1) * 128, :])
                        k.wait("act", mdo, ocm[b])
                        ocm[b] = k.run("act", lambda s, mb=mb, b=b: s.activation(
                            OCB.rearrange("p (c m) -> p c m", m=DM)[:, :, mb * 128:(mb + 1) * 128],
                            OC32[b].rearrange("p (c m) -> p c m", m=128), AF.Copy))
                        oc_all.append(ocm[b])
                    rwb_rd = [None, None]
                    mxd_rd = [None, None]
                    orm = [None, None]
                    mx_all_l = [None] * 4

                    def emit_mx(tb):
                        mp = tb % 2
                        mx_all = []
                        for kc in range(NC8):
                            rb = kc % 2
                            k.wait("pe", rwb_rd[rb], *(oc_all if tb == 0 and kc == 0 else []))
                            mrwb = k.run("pe", lambda t, kc=kc, tb=tb, rb=rb: t.matmul(
                                ps_rwb[rb], SEL[:, kc * 128:(kc + 1) * 128], RWT[:, tb * 512:(tb + 1) * 512],
                                start=True, stop=True, skip_group_check=True))
                            k.wait("ve", mrwb, mxd_rd[mp] if kc == 0 else None)

                            def f_mx(v, kc=kc, tb=tb, rb=rb, mp=mp):
                                v.tensor_copy(RWBB, ps_rwb[rb])
                                return v.tensor_tensor(MXD2[mp][:, kc * 512:(kc + 1) * 512],
                                                       ATTN2[:, tb * 512:(tb + 1) * 512], RWBB,
                                                       op=OP.mult)
                            rwb_rd[rb] = k.run("ve", f_mx)
                            if tb == 0 and app == 0 and kc == 0:
                                dbg("g_mxd0", MXD2[0][:, 0:512])
                                dbg("g_rwbb", RWBB)
                            mx_all.append(rwb_rd[rb])
                        mx_all_l[tb] = mx_all

                    emit_mx(0)
                    for tb in range(4):
                        if tb + 1 < 4:
                            emit_mx(tb + 1)
                        mp = tb % 2
                        o_mms = []
                        for c in range(NC8):
                            b = c % 2
                            k.wait("pe", orm[b], *(mx_all_l[tb] if c == 0 else []))

                            def f_oc(t, c=c, b=b, mp=mp):
                                last = None
                                for kc in range(NC8):
                                    last = t.matmul(ps_o[b], OCB[:, kc * DM + c * 128: kc * DM + (c + 1) * 128],
                                                    MXD2[mp][:, kc * 512:(kc + 1) * 512],
                                                    start=(kc == 0), stop=(kc == NC8 - 1),
                                                    skip_group_check=True)
                                return last
                            moc = k.run("pe", f_oc)
                            o_mms.append(moc)
                            k.wait("ve", moc)
                            orm[b] = k.run("ve", lambda v, c=c, tb=tb, b=b: v.tensor_tensor(
                                X[:, c * T + tb * 512: c * T + (tb + 1) * 512],
                                X[:, c * T + tb * 512: c * T + (tb + 1) * 512], ps_o[b], op=OP.add))
                        mxd_rd[mp] = o_mms[-1]
                    k.fence()
                at2_cm.__exit__(None, None, None)
                pf_cm.__exit__(None, None, None)

        def attn_b():
            with ExitStack() as ls:
                QB = sb("b_qb", [128, NC8 * T], BF16, ls)
                KBf = sb("b_kb", [128, NC8 * T], BF16, ls)
                VAUGB = sb("b_vaug", [128, NT * 16 * 65], BF16, ls)
                msetb = k.run("ve", lambda v: v.memset(VAUGB, 1.0))
                with ExitStack() as qs:
                    XNB2 = sb("b_xnb2", [128, NC8 * 1024], BF16, qs)
                    ps_qk = [qs.enter_context(nc.psum_tensor(_uniq(f"b_psqk{i}"), [128, 512], F32)).ap()
                             for i in range(2)]
                    ps_v = qs.enter_context(nc.psum_tensor(_uniq("b_psv"), [128, 512], F32)).ap()
                    vv = None
                    wvb_rd = None
                    evm = [None, None]
                    wrd_b = [None, None]
                    for pr in range(2):
                        with ExitStack() as lns:
                            lnd = make_ln_scope(f"lnb{pr}", lns, nbuf=3)
                            for j in range(2):
                                tb = 2 * pr + j
                                ln_tb(lnd, tb, lambda kc, j=j: XNB2[:, kc * 1024 + j * 512:
                                                                    kc * 1024 + (j + 1) * 512])
                            k.fence()
                        with ExitStack() as ws:
                            WST = [sb(f"b_wst{pr}_{i}", [128, DM], F32, ws) for i in range(2)]
                            WBB = [w.bitcast(BF16).rearrange("p (n two) -> p n two", two=2)[:, :, 1]
                                   for w in WST]
                            WV32 = sb(f"b_wv{pr}", [128, 512], F32, ws)
                            WVB = sb(f"b_wvb{pr}", [128, NC8 * 512], BF16, ws)
                            wvm = [None]
                            wv_pre = {0: [], 1: []}

                            def emit_wv(hf, kc):
                                k.wait("sync", wvm[0])
                                mdv = k.dma(WV32, d_wv[kc * 128:(kc + 1) * 128,
                                                       hf * 512:(hf + 1) * 512])
                                k.wait("act", mdv, wvm[0], wvb_rd)
                                wvm[0] = k.run("act", lambda s, kc=kc: s.activation(
                                    WVB[:, kc * 512:(kc + 1) * 512], WV32, AF.Copy))
                                wv_pre[hf].append(wvm[0])

                            for mb in range(16):
                                b = mb % 2
                                dst = QB if mb < 8 else KBf
                                src = d_wqmb if mb < 8 else d_wkmb
                                if mb >= 8:
                                    emit_wv(0, mb - 8)
                                k.wait("sync", wrd_b[b])
                                md = k.dma(WST[b], src[(mb % 8) * 128:((mb % 8) + 1) * 128, :])
                                mm = None
                                for j in range(2):
                                    tb = 2 * pr + j
                                    k.wait("pe", md if j == 0 else None, evm[j])

                                    def f_mm(t, b=b, j=j):
                                        last = None
                                        for kc in range(NC8):
                                            last = t.matmul(ps_qk[j], WBB[b][:, kc * 128:(kc + 1) * 128],
                                                            XNB2[:, kc * 1024 + j * 512:
                                                                 kc * 1024 + (j + 1) * 512],
                                                            start=(kc == 0), stop=(kc == NC8 - 1),
                                                            skip_group_check=True)
                                        return last
                                    mm = k.run("pe", f_mm)
                                    k.wait("ve", mm)
                                    bcol = CB[:, (1 if mb < 8 else 2) * NC8 + (mb % 8):
                                              (1 if mb < 8 else 2) * NC8 + (mb % 8) + 1]
                                    evm[j] = k.run("ve", lambda v, dst=dst, mb=mb, tb=tb, j=j, bcol=bcol:
                                                   v.tensor_scalar(
                                                       dst[:, (mb % 8) * T + tb * 512: (mb % 8) * T + (tb + 1) * 512],
                                                       ps_qk[j], bcol, None, op0=OP.add))
                                wrd_b[b] = mm
                            for hf in range(2):
                                if not wv_pre[hf]:
                                    for kc in range(NC8):
                                        emit_wv(hf, kc)
                                wvc = wv_pre[hf]
                                mv = None
                                for j in range(2):
                                    for kti in range(4):
                                        kt = pr * 8 + j * 4 + kti
                                        k.wait("pe", vv, *wvc)

                                        def f_vmm(t, j=j, kti=kti, hf=hf):
                                            last = None
                                            for kc in range(NC8):
                                                last = t.matmul(ps_v,
                                                                XNB2[:, kc * 1024 + j * 512 + kti * 128:
                                                                     kc * 1024 + j * 512 + (kti + 1) * 128],
                                                                WVB[:, kc * 512:(kc + 1) * 512],
                                                                start=(kc == 0), stop=(kc == NC8 - 1),
                                                                skip_group_check=True)
                                            return last
                                        mv = k.run("pe", f_vmm)
                                        k.wait("ve", mv, msetb)
                                        dstv = VAUGB.rearrange("p (kt h s) -> p kt h s", h=16, s=65)[:, kt, hf * 8:(hf + 1) * 8, 0:64]
                                        vv = k.run("ve", lambda v, dstv=dstv, hf=hf: v.tensor_tensor(
                                            dstv, ps_v.rearrange("p (h s) -> p h s", s=64),
                                            BVB.rearrange("p (h s) -> p h s", s=64)[:, hf * 8:(hf + 1) * 8, :],
                                            op=OP.add))
                                wvb_rd = mv
                            k.fence()

                if B_PHASE < 2:
                    return
                with ExitStack() as asx:
                    PTB = [sb(f"b_ptb{i}", [128, 2 * 512], BF16, asx) for i in range(2)]
                    ATG = [sb(f"b_atg{i}", [128, 4 * 64], BF16, asx) for i in range(2)]
                    Z4 = sb("b_z4", [128, 4], F32, asx)
                    Z4R = sb("b_z4r", [128, 4], F32, asx)
                    ATTNB = sb("b_attnb", [128, NC8 * 512], BF16, asx)
                    WOC32 = [sb(f"b_woc32{i}", [128, DM], F32, asx) for i in range(2)]
                    WOC = [sb(f"b_woc{i}", [128, DM], BF16, asx) for i in range(2)]
                    ps_sc = [asx.enter_context(nc.psum_tensor(_uniq(f"b_pssc{i}"), [128, 1024], F32)).ap()
                             for i in range(2)]
                    ps_av = [asx.enter_context(nc.psum_tensor(_uniq(f"b_psav{i}"), [128, 512], F32)).ap()
                             for i in range(2)]
                    ps_at4 = [asx.enter_context(nc.psum_tensor(_uniq(f"b_psat{i}"), [128, 1024], BF16)).ap()
                              for i in range(2)]
                    ps_o = [ps_sc[0][:, 0:512], ps_sc[0][:, 512:1024]]

                    def f_bias(v):
                        last = None
                        for kc in range(NC8):
                            last = v.tensor_scalar_add(X[:, kc * T:(kc + 1) * T],
                                                       X[:, kc * T:(kc + 1) * T],
                                                       CB[:, 3 * NC8 + kc: 3 * NC8 + kc + 1])
                        return last
                    k.run("ve", f_bias)

                    kgroups = [[2 * i, 2 * i + 1] for i in range(8)]
                    sc_rd = [None, None]
                    pt_rd = [None, None]
                    av_rd = [None, None]
                    atg_rd = [None, None]
                    at_rd = [None, None]
                    attnb_rd = None
                    wocm = [None, None]
                    wocb_rd = [None, None]
                    orm = [None, None]
                    pend = [None]

                    def flush_pend():
                        if pend[0] is None:
                            return None
                        hh, mzs = pend[0]
                        pend[0] = None
                        rlo = (hh % 2) * 64
                        hc = hh // 2
                        pb = hc % 2
                        ab = hh % 2
                        mat = None
                        for tl in range(4):
                            k.wait("pe", mzs if tl == 0 else None,
                                   at_rd[pb] if (hh % 2 == 0 and tl == 0) else None)
                            mat = k.run("pe", lambda t, tl=tl, rlo=rlo, pb=pb, ab=ab: t.transpose(
                                ps_at4[pb][rlo:rlo + 64, tl * 128:(tl + 1) * 128],
                                ATG[ab][:, tl * 64:(tl + 1) * 64], IDENT))
                        atg_rd[ab] = mat
                        if hh % 2 == 1:
                            k.wait("ve", mat, attnb_rd)
                            at_rd[pb] = k.run("ve", lambda v, hc=hc, pb=pb: v.tensor_copy(
                                ATTNB[:, hc * 512:(hc + 1) * 512], ps_at4[pb][:, 0:512]))
                            return at_rd[pb]
                        return None

                    for tq in range(4):
                        atbm = None
                        for hh in range(16):
                            rlo = (hh % 2) * 64
                            hc = hh // 2
                            apb = hh % 2
                            first_av = [True]
                            avm = None
                            NG = len(kgroups)
                            mex_l = [None] * NG

                            def emit_sc(gi):
                                grp = kgroups[gi]
                                gb = gi % 2
                                ms = []
                                for j, kp in enumerate(grp):
                                    k.wait("pe", sc_rd[gb] if j == 0 else None)
                                    mj = k.run("pe", lambda t, kp=kp, j=j, gb=gb, hc=hc,
                                               rlo=rlo, tq=tq: t.matmul(
                                        ps_sc[gb][:, j * 512:(j + 1) * 512],
                                        KBf[rlo:rlo + 64, hc * T + kp * 128: hc * T + (kp + 1) * 128],
                                        QB[rlo:rlo + 64, hc * T + tq * 512: hc * T + (tq + 1) * 512],
                                        start=True, stop=True, skip_group_check=True))
                                    k.wait("act", mj, pt_rd[gb] if j == 0 else None)
                                    ms.append(k.run("act", lambda s, j=j, gb=gb: s.activation(
                                        PTB[gb][:, j * 512:(j + 1) * 512],
                                        ps_sc[gb][:, j * 512:(j + 1) * 512], AF.Exp, scale=0.125)))
                                mex_l[gi] = ms
                                sc_rd[gb] = ms[-1]

                            emit_sc(0)
                            for gi in range(NG):
                                if gi + 1 < NG:
                                    emit_sc(gi + 1)
                                if B_PHASE < 2.5:
                                    continue
                                grp = kgroups[gi]
                                gb = gi % 2
                                avm = None
                                for j, kp in enumerate(grp):
                                    k.wait("pe", mex_l[gi][j],
                                           av_rd[apb] if first_av[0] else None)

                                    def f_avg(t, kp=kp, j=j, gb=gb, hh=hh, fa=first_av, apb=apb):
                                        last = None
                                        for tl in range(4):
                                            last = t.matmul(
                                                ps_av[apb][:, tl * 128: tl * 128 + 65],
                                                PTB[gb][:, j * 512 + tl * 128: j * 512 + (tl + 1) * 128],
                                                VAUGB[:, kp * 1040 + hh * 65: kp * 1040 + hh * 65 + 65],
                                                start=(fa[0] and tl == 0), stop=False,
                                                skip_group_check=True)
                                            fa[0] = False
                                        return last
                                    avm = k.run("pe", f_avg)
                                pt_rd[gb] = avm
                            cp = flush_pend()
                            if cp is not None:
                                atbm = cp
                            if B_PHASE < 2.7:
                                continue
                            k.wait("ve", avm, atg_rd[apb])
                            k.run("ve", lambda v, apb=apb: v.tensor_copy(
                                Z4, pslots(ps_av[apb], 128, 4, 1, 64)))
                            k.self_sync("ve")
                            mz = k.run("ve", lambda v: v.reciprocal(Z4R, Z4))
                            k.self_sync("ve")

                            def f_zsc(v, apb=apb):
                                return v.tensor_tensor(ATG[apb].rearrange("p (s q) -> p s q", q=64),
                                                       pslots(ps_av[apb], 128, 4, 64), bview(Z4R, 64),
                                                       op=OP.mult)
                            mzs = k.run("ve", f_zsc)
                            av_rd[apb] = mzs
                            if B_PHASE < 2.9:
                                continue
                            pend[0] = (hh, mzs)
                        cp = flush_pend()
                        if cp is not None:
                            atbm = cp
                        if B_PHASE < 3:
                            continue
                        o_last = []
                        for c in range(NC8):
                            b = c % 2
                            k.wait("sync", wocm[b])
                            mdw = k.dma(WOC32[b], d_womb[c * 128:(c + 1) * 128, :])
                            k.wait("act", mdw, wocm[b], wocb_rd[b])
                            wocm[b] = k.run("act", lambda s, b=b: s.activation(WOC[b], WOC32[b], AF.Copy))
                            k.wait("pe", wocm[b], orm[b], atbm)

                            def f_oc(t, c=c, b=b):
                                last = None
                                for kc in range(NC8):
                                    last = t.matmul(ps_o[b], WOC[b][:, kc * 128:(kc + 1) * 128],
                                                    ATTNB[:, kc * 512:(kc + 1) * 512],
                                                    start=(kc == 0), stop=(kc == NC8 - 1),
                                                    skip_group_check=True)
                                return last
                            moc = k.run("pe", f_oc)
                            wocb_rd[b] = moc
                            o_last.append(moc)
                            k.wait("ve", moc)
                            orm[b] = k.run("ve", lambda v, c=c, tq=tq, b=b: v.tensor_tensor(
                                X[:, c * T + tq * 512: c * T + (tq + 1) * 512],
                                X[:, c * T + tq * 512: c * T + (tq + 1) * 512], ps_o[b], op=OP.add))
                        attnb_rd = o_last[-1]
                        sc_rd[0] = later(orm[0], orm[1])
                    k.fence()

        seq = [attn_a, lambda: ffn(0), attn_b, lambda: ffn(1)][:parts]
        for f in seq:
            f()

        out_m = [k.dma(d_out[kc * 128:(kc + 1) * 128, :], X[:, kc * T:(kc + 1) * T].bitcast(F32))
                 for kc in range(NC8)]
        k.wait("sync", *out_m)

        with nc.Block() as blk:
            @blk.tensor
            def _(eng):
                for f in k.q["pe"]:
                    f(eng)

            @blk.scalar
            def _(eng):
                for f in k.q["act"]:
                    f(eng)

            @blk.vector
            def _(eng):
                for f in k.q["ve"]:
                    f(eng)

            @blk.gpsimd
            def _(eng):
                for f in k.q["pool"]:
                    f(eng)

            @blk.sync
            def _(eng):
                for f in k.q["sync"]:
                    f(eng)

    return nc


def _hi16(w):
    return np.ascontiguousarray(w.astype(ml_dtypes.bfloat16).astype(np.float32))


def _mb_layout(w, nblk):
    out = np.empty((nblk * 128, DM), np.float32)
    for mb in range(nblk):
        blk = w[:, mb * 128:(mb + 1) * 128]
        out[mb * 128:(mb + 1) * 128] = blk.reshape(NC8, 128, 128).transpose(1, 0, 2).reshape(128, DM)
    return np.ascontiguousarray(out)


def _host_consts():
    ident = np.eye(128, dtype=np.float32)

    def band(koff):
        j = np.arange(128)[:, None]
        t = np.arange(128)[None, :]
        d = j + 128 * koff - t
        return ((d >= -3) & (d <= 3)).astype(np.float32)
    m_prev, m_diag, m_next = band(-1), band(0), band(1)
    zero = np.zeros((128, 128), np.float32)
    first = np.concatenate([zero, m_diag, m_next], axis=1)
    mid = np.concatenate([m_prev, m_diag, m_next], axis=1)
    last = np.concatenate([m_prev, m_diag, zero], axis=1)
    maska = np.concatenate([first, mid, last], axis=1).astype(np.float32)
    tri = (np.arange(128)[:, None] <= np.arange(128)[None, :]).astype(np.float32)
    sel = np.zeros((16, NC8 * 128), np.float32)
    for kc in range(NC8):
        sel[2 * kc, kc * 128: kc * 128 + 64] = 1.0
        sel[2 * kc + 1, kc * 128 + 64: kc * 128 + 128] = 1.0
    return ident, maska, tri, np.ascontiguousarray(sel)


def kernel(**inp):
    x = np.asarray(inp["L0_attn_a_x"], np.float32)
    w_qkv = np.asarray(inp["L0_attn_a_w_qkv"], np.float32)
    projm = np.ascontiguousarray(np.asarray(inp["L0_attn_a_proj"], np.float32))
    w_out = np.asarray(inp["L0_attn_a_w_out"], np.float32)
    b_out = np.asarray(inp["L0_attn_a_b_out"], np.float32)
    Wq = np.asarray(inp["L0_attn_b_Wq"], np.float32)
    bq = np.asarray(inp["L0_attn_b_bq"], np.float32)
    Wk = np.asarray(inp["L0_attn_b_Wk"], np.float32)
    bk = np.asarray(inp["L0_attn_b_bk"], np.float32)
    Wv = np.ascontiguousarray(np.asarray(inp["L0_attn_b_Wv"], np.float32))
    bv = np.asarray(inp["L0_attn_b_bv"], np.float32)
    Wo = np.asarray(inp["L0_attn_b_Wo"], np.float32)
    bo = np.asarray(inp["L0_attn_b_bo"], np.float32)
    sim = np.asarray(inp["L0_ffn_sim_matrix"], np.float32)
    gates = np.asarray(inp["L0_ffn_gates"], np.float32)
    q_proj = np.asarray(inp["L0_ffn_q_proj"], np.float32)
    k_proj = np.asarray(inp["L0_ffn_k_proj"], np.float32)
    v_proj = np.asarray(inp["L0_ffn_v_proj"], np.float32)
    o_proj = np.asarray(inp["L0_ffn_o_proj"], np.float32)

    wcat = np.ascontiguousarray(np.concatenate(
        [np.concatenate([q_proj[e], k_proj[e], v_proj[e]], axis=1) for e in range(16)], axis=1))
    ocat = o_proj.reshape(16 * 64, DM)
    simc = np.ascontiguousarray(sim.reshape(NC8, 128, 16).transpose(1, 0, 2).reshape(128, NC8 * 16))
    cb = np.ascontiguousarray(np.concatenate(
        [b_out.reshape(NC8, 128).T, bq.reshape(NC8, 128).T,
         bk.reshape(NC8, 128).T, bo.reshape(NC8, 128).T], axis=1).astype(np.float32))
    ident, maska, tri, sel = _host_consts()

    nc = build_program()
    common = {
        "wqkv_qk_mb": _hi16(_mb_layout(w_qkv[:, 0:2048], 16)),
        "wqkv_v": np.ascontiguousarray(w_qkv[:, 2048:3072]),
        "projm": projm,
        "wout_mb": _mb_layout(w_out, 8),
        "wq_mb": _hi16(_mb_layout(Wq, 8)), "wk_mb": _hi16(_mb_layout(Wk, 8)),
        "Wv": Wv, "wo_mb": _mb_layout(Wo, 8),
        "Wcat": wcat, "ocat_mb": _mb_layout(ocat, 8),
        "simc": simc, "gatesr": gates.reshape(1, 16),
        "biascols": cb, "bvrow": bv.reshape(1, DM),
        "identc": ident, "maska": maska, "tric": tri, "selc": sel,
        "onesr": np.ones((128, 2), np.float32),
        "ones1": np.ones((1, 128), np.float32),
        "ones1r": np.ones((1, 128), np.float32),
        "idf1": np.ones((1, 1), np.float32),
    }
    in_maps = []
    for b in range(8):
        m = dict(common)
        m["x_fm"] = np.ascontiguousarray(x[b].T)
        in_maps.append(m)
    res = run_bass_kernel_spmd(nc, in_maps, core_ids=list(range(8)))
    if DBG_RAW:
        return res.results
    out = np.stack([np.ascontiguousarray(res.results[b]["x_out"].T) for b in range(8)])
    return out.astype(np.float32)


if __name__ == "__main__":
    build_program()
    print("build ok")
